# version 45
# baseline (speedup 1.0000x reference)
import numpy as np
import concourse.bass as bass
import concourse.mybir as mybir
from concourse.alu_op_type import AluOpType
from concourse.bass_utils import run_bass_kernel_spmd

F32 = mybir.dt.float32
BF16 = mybir.dt.bfloat16
U32 = mybir.dt.uint32
AF = mybir.ActivationFunctionType

D = 1024
T = 4096
NT = 8
NPAIR = 8
EXP = 8
C_SHIFT = -9.0

ENGINES = ("sync", "gpsimd", "tensor", "vector", "scalar")


class Sched:
    def __init__(self):
        self.plan = {e: [] for e in ENGINES}
        self.seq = {e: 0 for e in ENGINES}
        self.res = {}
        self.n_waits = 0

    def emit(self, eng, fn, reads=(), writes=(), dma=False, tag=""):
        myseq = self.seq[eng]
        token = (eng, myseq, dma)
        waits = {}

        def need(tok):
            peng, pseq, pdma = tok
            if peng == eng and not pdma:
                return
            waits[peng] = max(waits.get(peng, -1), pseq)

        for r in reads:
            st = self.res.get(r)
            if st is not None and st[0] is not None:
                need(st[0])
        for r in writes:
            st = self.res.get(r)
            if st is not None:
                if st[0] is not None:
                    need(st[0])
                for t in st[1]:
                    need(t)
        for r in reads:
            st = self.res.setdefault(r, [None, []])
            st[1].append(token)
        for r in writes:
            self.res[r] = [token, []]
        wl = sorted(waits.items())
        self.n_waits += len(wl)
        self.plan[eng].append([wl, fn, myseq, dma, tag])
        self.seq[eng] += 1

    def finalize(self):
        targets = {e: set() for e in ENGINES}
        for e in ENGINES:
            for wl, fn, myseq, dma, tag in self.plan[e]:
                if dma:
                    targets[e].add(myseq)
                for peng, pseq in wl:
                    targets[peng].add(pseq)
        self.inc_at = {e: targets[e] for e in ENGINES}
        self.amt = {}
        self.val = {}
        for e in ENGINES:
            tot = 0
            for wl, fn, myseq, dma, tag in self.plan[e]:
                if myseq in self.inc_at[e]:
                    a = 16 if dma else 1
                    self.amt[(e, myseq)] = a
                    tot += a
                self.val[(e, myseq)] = tot
        for e in ENGINES:
            seen = {}
            n_kept = 0
            for entry in self.plan[e]:
                wl = []
                for p, q in entry[0]:
                    v = self.val[(p, q)]
                    if seen.get(p, -1) < v:
                        seen[p] = v
                        wl.append((p, v))
                n_kept += len(wl)
                entry[0] = wl
        self.n_waits = sum(len(en[0]) for ee_ in ENGINES for en in self.plan[ee_])

    def check(self):
        idx = {e: 0 for e in ENGINES}
        counts = {e: 0 for e in ENGINES}
        total = sum(len(v) for v in self.plan.values())
        done = 0
        progress = True
        while progress:
            progress = False
            for e in ENGINES:
                while idx[e] < len(self.plan[e]):
                    wl, fn, myseq, dma, tag = self.plan[e][idx[e]]
                    if all(counts[p] >= v for p, v in wl):
                        if myseq in self.inc_at[e]:
                            counts[e] += self.amt[(e, myseq)]
                        idx[e] += 1
                        done += 1
                        progress = True
                    else:
                        break
        if done != total:
            msgs = []
            for e in ENGINES:
                if idx[e] < len(self.plan[e]):
                    wl, fn, myseq, dma, tag = self.plan[e][idx[e]]
                    blocked = [(p, v, counts[p]) for p, v in wl if counts[p] < v]
                    msgs.append(f"{e}@{idx[e]} tag={tag} blocked={blocked}")
            raise RuntimeError("DEADLOCK: " + " | ".join(msgs))
        return total

    def replay(self, block, sems):
        for e in ENGINES:
            entries = self.plan[e]
            inc_at = self.inc_at[e]
            amts = self.amt
            mysem = sems[e]

            def body(eng, entries=entries, e=e, inc_at=inc_at, mysem=mysem):
                for wl, fn, myseq, dma, tag in entries:
                    for peng, v in wl:
                        eng.wait_ge(sems[peng], v)
                    instr = fn(eng)
                    if myseq in inc_at:
                        if instr is None:
                            instr = eng.nop()
                        instr.then_inc(mysem, amts[(e, myseq)])
            getattr(block, e)(body)


def build_program():
    nc = bass.Bass("TRN2", target_bir_lowering=False)

    xT = nc.dram_tensor("xT", [D, T], F32, kind="ExternalInput").ap()
    wqkv_h = nc.dram_tensor("wqkv", [D, 3 * D], F32, kind="ExternalInput").ap()
    bqkv_h = nc.dram_tensor("bqkv", [1, 3 * D], F32, kind="ExternalInput").ap()
    wout_h = nc.dram_tensor("wout", [D, D], F32, kind="ExternalInput").ap()
    boutc_h = nc.dram_tensor("boutc", [128, 8], F32, kind="ExternalInput").ap()
    gwT_h = nc.dram_tensor("gwT", [D, EXP], F32, kind="ExternalInput").ap()
    wexpT_h = nc.dram_tensor("wexpT", [EXP, D, D], F32, kind="ExternalInput").ap()
    outT_h = nc.dram_tensor("outT", [D, T], F32, kind="ExternalOutput").ap()

    attn_sc = nc.dram_tensor("attn_sc", [D, T], BF16, kind="Internal").ap()
    x1bf_sc = nc.dram_tensor("x1bf_sc", [D, T], BF16, kind="Internal").ap()
    base_sc = nc.dram_tensor("base_sc", [D, T], F32, kind="Internal").ap()
    mask_sc = nc.dram_tensor("mask_sc", [2, EXP, T], BF16, kind="Internal").ap()

    L2 = {}

    def a2(name, off, n):
        L2[name] = (off, n)
        assert off + n <= 80960, (name, off + n)

    a2("xbf", 0, 32768)
    a2("qT", 32768, 4096)
    a2("kT", 36864, 4096)
    a2("vext", 40960, 4160)
    a2("astage", 45120, 2048)
    a2("wst", 47168, 3072)
    a2("rows2", 50240, 12288)
    a2("bqkvr", 62528, 3072)
    a2("rowt2", 65600, 2560)
    a2("consts2", 68160, 448)
    a2("pp", 68608, 4096)
    a2("xsq", 68608, 1024)
    a2("woutw", 32768, 8192)
    a2("attnc", 40960, 8192)
    a2("x1bfc", 50240, 4096)
    a2("x1loc", 54336, 4096)
    a2("ghl", 58432, 640)
    a2("maskst", 59072, 512)
    a2("maskr8", 59584, 512)
    a2("maskm8", 60096, 512)
    a2("maskst2", 60608, 512)
    a2("wexp", 0, 16384)
    a2("x1half", 16384, 16384)
    a2("h2m", 32768, 8192)
    a2("acc", 40960, 16384)
    a2("bsb", 57344, 1024)
    a2("wse", 58368, 2048)
    a2("mrA", 60416, 2048)
    a2("mrB", 65600, 2048)
    a2("mrA2", 62464, 2048)
    a2("mrB2", 68608, 2048)
    a2("qT2", 72768, 4096)
    a2("kT2", 76864, 4096)
    E2 = 80960

    L4 = {}

    def a4(name, off, n):
        L4[name] = (off, n)
        assert off + n <= 12624, (name, off + n)

    a4("rb1", 0, 4096)
    a4("xf32", 4096, 2048)
    a4("scr4", 6144, 1024)
    a4("r1cols", 7168, 64)
    a4("rowp", 7232, 3584)
    a4("smalls", 10816, 320)
    a4("ident", 11136, 128)
    a4("iorow", 11264, 128)
    a4("sctm", 11392, 32)
    a4("wcol", 11424, 8)
    a4("sctm2", 11432, 24)
    a4("scot", 11456, 512)
    a4("scglo", 11968, 512)
    a4("sc2", 12480, 144)
    a4("x1c", 0, 4096)
    a4("outc", 0, 4096)
    a4("based", 4096, 4096)
    E4 = 12624

    assert 2 * E2 + 4 * E4 <= 212500, (E2, E4, 2 * E2 + 4 * E4)

    AR2 = nc.sbuf_tensor("AR2", [128, E2], BF16).__enter__()
    AR4 = nc.sbuf_tensor("AR4", [128, E4], F32).__enter__()
    U32B = nc.sbuf_tensor("U32B", [128, 8], U32).__enter__()
    PS = nc.psum_tensor("PS", [128, 4096], F32).__enter__()

    def r2(name, p0, p1, lo, n):
        o, sz = L2[name]
        assert lo + n <= sz, (name, lo, n, sz)
        return AR2[p0:p1, o + lo: o + lo + n]

    def r4(name, p0, p1, lo, n):
        o, sz = L4[name]
        assert lo + n <= sz, (name, lo, n, sz)
        return AR4[p0:p1, o + lo: o + lo + n]

    def g2(name, lo, n):
        o = L2[name][0] + lo
        return [f"A2.{g}" for g in range(o // 512, (o + n - 1) // 512 + 1)]

    def g4(name, lo, n):
        o = L4[name][0] + lo
        return [f"A4.{g}" for g in range(o // 512, (o + n - 1) // 512 + 1)]

    def bank(i, p0=0, p1=128, lo=0, n=512):
        return PS[p0:p1, i * 512 + lo: i * 512 + lo + n]

    def gb(i):
        return [f"PS.{i}"]

    S = Sched()

    def mm(ps, gps, lhsT, glh, rhs, grh, start, stop, tag="", tp=None):
        if tp is None:
            S.emit("tensor", lambda e, p=ps, l=lhsT, r=rhs, st=start, sp=stop:
                   e.matmul(p, l, r, start=st, stop=sp),
                   reads=list(glh) + list(grh), writes=list(gps), tag=tag)
        else:
            S.emit("tensor", lambda e, p=ps, l=lhsT, r=rhs, st=start, sp=stop,
                   tp=tp: e.matmul(p, l, r, start=st, stop=sp, tile_position=tp),
                   reads=list(glh) + list(grh), writes=list(gps), tag=tag)

    def tr(ps, gps, in_, gin, ident, gid, tag=""):
        S.emit("tensor", lambda e, p=ps, i=in_, d=ident: e.transpose(p, i, d),
               reads=list(gin) + list(gid), writes=list(gps), tag=tag)

    def dve(fn, reads, writes, tag=""):
        S.emit("vector", fn, reads=reads, writes=writes, tag=tag)

    def pve(fn, reads, writes, tag=""):
        S.emit("gpsimd", fn, reads=reads, writes=writes, tag=tag)

    def act(fn, reads, writes, tag=""):
        S.emit("scalar", fn, reads=reads, writes=writes, tag=tag)

    def dma(q, out, in_, reads, writes, tag=""):
        S.emit(q, lambda e, o=out, i=in_: e.dma_start(out=o, in_=i),
               reads=reads, writes=writes, dma=True, tag=tag)

    onescol_bf = r2("consts2", 0, 128, 0, 1)
    onesrow_bf = r2("consts2", 0, 1, 1, 128)
    onesrow64_bf = r2("consts2", 64, 65, 129, 128)
    gc_ones = g2("consts2", 0, 512)
    dve(lambda e: e.memset(r2("consts2", 0, 128, 0, 1), 1.0), [], gc_ones, "c0")
    dve(lambda e: e.memset(r2("consts2", 0, 1, 1, 128), 1.0), [], gc_ones, "c1")
    dve(lambda e: e.memset(r2("consts2", 64, 65, 129, 128), 1.0), [], gc_ones, "c1b")
    dve(lambda e: e.memset(r4("r1cols", 0, 128, 32, 1), C_SHIFT),
        [], g4("r1cols", 32, 1), "c1c")
    ones1f = r4("smalls", 0, 1, 128, 128)
    onescol_f = r4("smalls", 0, 128, 256, 1)
    bexp = r4("smalls", 0, 128, 257, 1)
    beps = r4("smalls", 0, 1, 258, 1)
    bzero = r4("smalls", 0, 1, 259, 1)
    ident11 = r4("smalls", 0, 1, 260, 1)
    iocol = r4("smalls", 0, 128, 261, 1)
    negG = r4("smalls", 0, 8, 262, 1)
    ecol8 = r4("smalls", 0, 8, 261, 1)
    gsm = g4("smalls", 0, 512)
    dve(lambda e: e.memset(r4("smalls", 0, 1, 128, 128), 1.0), [], gsm, "c2")
    dve(lambda e: e.memset(r4("smalls", 0, 128, 256, 1), 1.0), [], gsm, "c3")
    dve(lambda e: e.memset(r4("smalls", 0, 128, 257, 1), C_SHIFT), [], gsm, "c4")
    dve(lambda e: e.memset(r4("smalls", 0, 1, 258, 1), 1e-5), [], gsm, "c5")
    dve(lambda e: e.memset(r4("smalls", 0, 1, 259, 1), 0.0), [], gsm, "c6")
    dve(lambda e: e.memset(r4("smalls", 0, 128, 264, 1), 0.0), [], gsm, "c6b")
    dve(lambda e: e.memset(r4("smalls", 0, 1, 260, 1), 1.0), [], gsm, "c7")
    iorow = r4("iorow", 0, 1, 0, 128)
    for i in range(128):
        dve(lambda e, i=i: e.memset(r4("iorow", 0, 1, i, 1), float(i)),
            [], g4("iorow", i, 1), f"io{i}")
    for ei in range(8):
        dve(lambda e, ei=ei: e.memset(r4("wcol", 0, 128, ei, 1), float(8 - ei)),
            [], g4("wcol", ei, 1), f"wc{ei}")
    tr(bank(7, 0, 128, 0, 1), gb(7), iorow, g4("iorow", 0, 128), ident11, gsm, "ioct")
    dve(lambda e: e.tensor_copy(r4("smalls", 0, 128, 261, 1), bank(7, 0, 128, 0, 1)),
        gb(7), gsm, "iocp")
    mm(bank(6, 0, 128, 0, 128), gb(6), ones1f, gsm, iorow, g4("iorow", 0, 128), True, True, "iobc")
    pve(lambda e: e.tensor_copy(r4("smalls", 0, 128, 263, 1),
                                r4("smalls", 0, 128, 261, 1)), gsm, gsm, "ioc2")
    ident128 = r4("ident", 0, 128, 0, 128)
    dve(lambda e: e.tensor_scalar(r4("ident", 0, 128, 0, 128), bank(6, 0, 128, 0, 128),
                                  r4("smalls", 0, 128, 263, 1), None, AluOpType.is_equal),
        gb(6) + gsm, g4("ident", 0, 128), "ident")

    dma("sync", r4("smalls", 0, 128, 0, 64).rearrange("p (dt e) -> p dt e", dt=8),
        gwT_h.rearrange("(dt p) e -> p dt e", p=128), ["H.gw"], gsm, "gw_in")
    dma("sync", r4("smalls", 0, 128, 64, 8), boutc_h, ["H.boutc"], gsm, "bout_in")
    dma("gpsimd", r2("bqkvr", 0, 1, 0, 3072), bqkv_h, ["H.bqkv"],
        g2("bqkvr", 0, 3072), "bqkv_in")
    for dt in range(8):
        mm(bank(5, 0, 1, 0, 8), gb(5), onescol_f, gsm,
           r4("smalls", 0, 128, dt * 8, 8), gsm, dt == 0, dt == 7, "Gsum")
    dve(lambda e: e.tensor_scalar(r4("rowp", 0, 1, 0, 8), bank(5, 0, 1, 0, 8), -1.0,
                                  None, AluOpType.mult), gb(5), g4("rowp", 0, 8), "negGr")
    tr(bank(5, 0, 8, 512, 1), gb(5), r4("rowp", 0, 1, 0, 8), g4("rowp", 0, 8),
       ident11, gsm, "negGt")
    dve(lambda e: e.tensor_copy(r4("smalls", 0, 8, 262, 1), bank(5, 0, 8, 512, 1)),
        gb(5), gsm, "negGc")

    def rowslot(i, n=512, p0=0, p1=1):
        return r4("rowp", p0, p1, i * 512, n)

    def rowslot_g(i, n=512):
        return g4("rowp", i * 512, n)

    R_M, R_E2, R_VAR, R_LN, R_R, R_A, R_B = 0, 1, 2, 3, 4, 5, 6

    def emit_phaseA_tc(tc):
        for dt in range(8):
            if dt % 4 == 0:
                half = dt // 4
                dma("sync", r4("xf32", 0, 128, 0, 2048)
                    .rearrange("p (dt t) -> p dt t", dt=4),
                    xT[half * 512:(half + 1) * 512, tc * 512:(tc + 1) * 512]
                    .rearrange("(dt p) t -> p dt t", p=128),
                    ["H.xT"], g4("xf32", 0, 2048), f"xin{tc}_{half}")
            xf = r4("xf32", 0, 128, (dt % 4) * 512, 512)
            gxf = g4("xf32", (dt % 4) * 512, 512)
            xb = r2("xbf", 0, 128, dt * 4096 + tc * 512, 512)
            gxb = g2("xbf", dt * 4096 + tc * 512, 512)
            dve(lambda e, o=xb, i=xf: e.tensor_copy(o, i), gxf, gxb, f"cast{tc}_{dt}")
            sq = r2("xsq", 0, 128, (dt % 2) * 512, 512)
            gsq = g2("xsq", (dt % 2) * 512, 512)
            dve(lambda e, o=sq, i=xb: e.tensor_tensor(o, i, i, AluOpType.mult),
                gxb, gsq, f"sqA{tc}_{dt}")
            mm(bank(0, 0, 1), gb(0), onescol_bf, gc_ones, xb, gxb,
               dt == 0, dt == 7, f"s1_{tc}_{dt}")
            mm(bank(1, 0, 1), gb(1), onescol_bf, gc_ones, sq, gsq,
               dt == 0, dt == 7, f"s2_{tc}_{dt}")
        dve(lambda e: e.tensor_scalar(rowslot(R_M), bank(0, 0, 1), 1.0 / D, None,
                                      AluOpType.mult), gb(0), rowslot_g(R_M), f"mA{tc}")
        dve(lambda e: e.tensor_scalar(rowslot(R_E2), bank(1, 0, 1), 1.0 / D, None,
                                      AluOpType.mult), gb(1), rowslot_g(R_E2), f"eA{tc}")
        dve(lambda e: e.tensor_tensor(rowslot(R_VAR), rowslot(R_M), rowslot(R_M),
                                      AluOpType.mult),
            rowslot_g(R_M), rowslot_g(R_VAR), f"m2A{tc}")
        dve(lambda e: e.tensor_tensor(rowslot(R_VAR), rowslot(R_E2), rowslot(R_VAR),
                                      AluOpType.subtract),
            rowslot_g(R_E2) + rowslot_g(R_VAR), rowslot_g(R_VAR), f"vA{tc}")
        act(lambda e: e.activation(rowslot(R_LN), rowslot(R_VAR), AF.Ln,
                                   bias=beps, scale=1.0),
            rowslot_g(R_VAR) + gsm, rowslot_g(R_LN), f"lnA{tc}")
        act(lambda e: e.activation(rowslot(R_R), rowslot(R_LN), AF.Exp,
                                   bias=bzero, scale=-0.5),
            rowslot_g(R_LN) + gsm, rowslot_g(R_R), f"rA{tc}")
        act(lambda e, tc=tc: e.activation(r2("rows2", 0, 1, 4096 + tc * 512, 512),
                                          rowslot(R_LN), AF.Exp, bias=bzero, scale=0.5),
            rowslot_g(R_LN) + gsm, g2("rows2", 4096 + tc * 512, 512), f"ivA{tc}")
        dve(lambda e, tc=tc: e.tensor_scalar(r2("rows2", 0, 1, 8192 + tc * 512, 512),
                                             rowslot(R_M), -1.0, None, AluOpType.mult),
            rowslot_g(R_M), g2("rows2", 8192 + tc * 512, 512), f"nmA{tc}")
        dve(lambda e, tc=tc: e.scalar_tensor_tensor(r2("rows2", 0, 1, tc * 512, 512),
                                                    rowslot(R_R), -1.0, rowslot(R_M),
                                                    AluOpType.mult, AluOpType.mult),
            rowslot_g(R_R) + rowslot_g(R_M), g2("rows2", tc * 512, 512), f"nrmA{tc}")
        rhi = r2("rowt2", 0, 1, 0, 512)
        rlo = r2("rowt2", 0, 1, 512, 512)
        dve(lambda e: e.tensor_copy(rhi, rowslot(R_R)), rowslot_g(R_R),
            g2("rowt2", 0, 512), f"rhiA{tc}")
        dve(lambda e: e.tensor_tensor(rlo, rowslot(R_R), rhi, AluOpType.subtract),
            rowslot_g(R_R) + g2("rowt2", 0, 512), g2("rowt2", 512, 512), f"rloA{tc}")
        mm(bank(2), gb(2), onesrow_bf, gc_ones, rhi, g2("rowt2", 0, 512),
           True, False, f"rbA{tc}")
        mm(bank(2), gb(2), onesrow_bf, gc_ones, rlo, g2("rowt2", 512, 512),
           False, True, f"rbB{tc}")
        dve(lambda e, tc=tc: e.tensor_copy(r4("rb1", 0, 128, tc * 512, 512), bank(2)),
            gb(2), g4("rb1", tc * 512, 512), f"rbC{tc}")
        for sub in range(4):
            tr(bank(3, 0, 128, sub, 1), gb(3),
               r4("rowp", 0, 1, R_R * 512 + sub * 128, 128), rowslot_g(R_R),
               ident11, gsm, f"rcT{tc}_{sub}")
        dve(lambda e, tc=tc: e.tensor_copy(r4("r1cols", 0, 128, tc * 4, 4),
                                           bank(3, 0, 128, 0, 4)),
            gb(3), g4("r1cols", tc * 4, 4), f"rcC{tc}")

    wq = r2("wst", 0, 128, 0, 1024)
    wk = r2("wst", 0, 128, 1024, 1024)
    wv = r2("wst", 0, 128, 2048, 1024)
    gwq = g2("wst", 0, 1024)
    gwk = g2("wst", 1024, 1024)
    gwv = g2("wst", 2048, 1024)
    wsq_r = r2("rowt2", 0, 1, 1024, 128)
    wsk_r = r2("rowt2", 0, 1, 1152, 128)
    wsv_r = r2("rowt2", 0, 1, 1280, 128)
    gws0 = g2("rowt2", 1024, 128)
    gws1 = g2("rowt2", 1152, 128)
    gws2 = g2("rowt2", 1280, 128)

    def proj_thunks(pn):
        th = []
        qdst = "qT" if pn % 2 == 0 else "qT2"
        kdst = "kT" if pn % 2 == 0 else "kT2"
        for (w, gw_, col0, nm) in ((wq, gwq, 128 * pn, "q"),
                                   (wk, gwk, D + 128 * pn, "k"),
                                   (wv, gwv, 2 * D + 128 * pn, "v")):
            th.append(lambda w=w, gw_=gw_, col0=col0, nm=nm: dma(
                "gpsimd", w.rearrange("p (dt o) -> p dt o", dt=8),
                wqkv_h[:, col0:col0 + 128].rearrange("(dt p2) o -> p2 dt o", p2=128),
                ["H.wqkv"], gw_, f"w{nm}i{pn}"))
        for si, (w, gw_) in enumerate(((wq, gwq), (wk, gwk), (wv, gwv))):
            for dt in range(8):
                th.append(lambda w=w, gw_=gw_, si=si, dt=dt: mm(
                    bank(7, 0, 1, si * 128, 128), gb(7), onescol_bf, gc_ones,
                    w[:, dt * 128:(dt + 1) * 128], gw_, dt == 0, dt == 7,
                    f"ws{pn}{si}"))
            th.append(lambda si=si: dve(
                lambda e, si=si: e.tensor_copy(
                    r2("rowt2", 0, 1, 1024 + si * 128, 128),
                    bank(7, 0, 1, si * 128, 128)),
                gb(7), g2("rowt2", 1024 + si * 128, 128), f"wsc{pn}{si}"))
        bq_r = r2("bqkvr", 0, 1, 128 * pn, 128)
        bk_r = r2("bqkvr", 0, 1, D + 128 * pn, 128)
        gbq = g2("bqkvr", 128 * pn, 128)
        gbk = g2("bqkvr", D + 128 * pn, 128)
        for tc in range(NT):
            for (dst, w, gw_, wsr, gwsr, br, gbr) in (
                    (qdst, wq, gwq, wsq_r, gws0, bq_r, gbq),
                    (kdst, wk, gwk, wsk_r, gws1, bk_r, gbk)):
                for dt in range(8):
                    th.append(lambda dst=dst, w=w, gw_=gw_, tc=tc, dt=dt: mm(
                        bank(7), gb(7), w[:, dt * 128:(dt + 1) * 128], gw_,
                        r2("xbf", 0, 128, dt * 4096 + tc * 512, 512),
                        g2("xbf", dt * 4096 + tc * 512, 512),
                        dt == 0, False, f"{dst}m{pn}_{tc}_{dt}"))
                th.append(lambda dst=dst, wsr=wsr, gwsr=gwsr, tc=tc: mm(
                    bank(7), gb(7), wsr, gwsr,
                    r2("rows2", 0, 1, tc * 512, 512), g2("rows2", tc * 512, 512),
                    False, False, f"{dst}c1{pn}_{tc}"))
                th.append(lambda dst=dst, br=br, gbr=gbr, tc=tc: mm(
                    bank(7), gb(7), br, gbr,
                    r2("rows2", 0, 1, 4096 + tc * 512, 512),
                    g2("rows2", 4096 + tc * 512, 512), False, True,
                    f"{dst}c2{pn}_{tc}"))
                th.append(lambda dst=dst, tc=tc: dve(
                    lambda e, dst=dst, tc=tc: e.tensor_tensor(
                        r2(dst, 0, 128, tc * 512, 512), bank(7),
                        r4("rb1", 0, 128, tc * 512, 512), AluOpType.mult),
                    gb(7) + g4("rb1", tc * 512, 512), g2(dst, tc * 512, 512),
                    f"{dst}e{pn}_{tc}"))
        return th

    th0 = proj_thunks(0)
    n0 = len(th0)
    for tc in range(NT):
        emit_phaseA_tc(tc)
        want0 = min(n0, 30 + 22 * (tc + 1))
        while n0 - len(th0) < want0:
            th0.pop(0)()
    while th0:
        th0.pop(0)()
    norm_bg = []
    for p in range(NPAIR):
        qsrc = "qT" if p % 2 == 0 else "qT2"
        ksrc = "kT" if p % 2 == 0 else "kT2"
        if p == 0:
            dve(lambda e: e.memset(
                r2("vext", 0, 128, 0, 4160)
                .rearrange("p (kt c) -> p kt c", kt=32)[:, :, 64:130:65], 1.0),
                [], g2("vext", 0, 4160), "vones")
        bv_r = r2("bqkvr", 0, 1, 2 * D + 128 * p, 128)
        gbv = g2("bqkvr", 2 * D + 128 * p, 128)

        def v_group(kt, vb, p=p, bv_r=bv_r, gbv=gbv):
            for dt in range(8):
                mm(bank(vb, 0, 128, 0, 128), gb(vb),
                   r2("xbf", 0, 128, dt * 4096 + kt * 128, 128),
                   g2("xbf", dt * 4096 + kt * 128, 128),
                   wv[:, dt * 128:(dt + 1) * 128], gwv, dt == 0, False, f"vm{p}_{kt}")
            mm(bank(vb, 0, 128, 0, 128), gb(vb),
               r2("rows2", 0, 1, 8192 + kt * 128, 128),
               g2("rows2", 8192 + kt * 128, 128), wsv_r, gws2, False, False,
               f"vc1{p}_{kt}")
            mm(bank(vb, 0, 128, 0, 128), gb(vb),
               r2("rows2", 0, 1, 4096 + kt * 128, 128),
               g2("rows2", 4096 + kt * 128, 128), bv_r, gbv, False, True, f"vc2{p}_{kt}")
            for hh in range(2):
                dve(lambda e, kt=kt, hh=hh, vb=vb: e.tensor_scalar(
                        r2("vext", 0, 128, kt * 130 + 65 * hh, 64),
                        bank(vb, 0, 128, hh * 64, 64),
                        r4("r1cols", 0, 128, kt, 1), None, AluOpType.mult),
                    gb(vb) + g4("r1cols", kt, 1), g2("vext", kt * 130 + 65 * hh, 64),
                    f"ve{p}_{kt}_{hh}")

        if p == 0:
            for kt in range(32):
                v_group(kt, 5 + kt % 2)

        bg = proj_thunks(p + 1) if p < NPAIR - 1 else []
        nbg = len(bg)
        bgi = 0
        for qc in range(NT):
            par = qc % 2

            def emit_S(kt, qc=qc, qsrc=qsrc, ksrc=ksrc):
                for hh in range(2):
                    s_b = 2 + 2 * (kt % 2) + hh
                    mm(bank(s_b), gb(s_b),
                       r2(ksrc, 64 * hh, 64 * hh + 64, kt * 128, 128),
                       g2(ksrc, kt * 128, 128),
                       r2(qsrc, 64 * hh, 64 * hh + 64, qc * 512, 512),
                       g2(qsrc, qc * 512, 512), True, True, f"S{p}{qc}_{kt}_{hh}")

            def emit_E(kt, qc=qc):
                pb = 2 + 2 * (kt % 2)
                pp = r2("pp", 0, 128, (kt % 4) * 1024, 1024)
                gpp = g2("pp", (kt % 4) * 1024, 1024)
                act(lambda e, o=pp, b=pb: e.activation(
                        o, bank(b, 0, 128, 0, 1024), AF.Exp,
                        bias=r4("r1cols", 0, 128, 32, 1), scale=0.125),
                    gb(pb) + gb(pb + 1) + g4("r1cols", 32, 1), gpp, f"E{p}{qc}_{kt}")

            def emit_A(kt, qc=qc):
                for hh in range(2):
                    pp = r2("pp", 0, 128, (kt % 4) * 1024 + 512 * hh, 512)
                    gpp = g2("pp", (kt % 4) * 1024 + 512 * hh, 512)
                    mm(bank(0, 64 * hh, 64 * hh + 64), gb(0),
                       r2("vext", 0, 128, kt * 130 + 65 * hh, 64),
                       g2("vext", kt * 130 + 65 * hh, 64),
                       pp, gpp, kt == 0, kt == 31, f"A{p}{qc}_{kt}_{hh}",
                       tp=(0, 64 * hh))
                for hh in range(2):
                    pp = r2("pp", 0, 128, (kt % 4) * 1024 + 512 * hh, 512)
                    gpp = g2("pp", (kt % 4) * 1024 + 512 * hh, 512)
                    mm(bank(1, 64 * hh, 64 * hh + 1), gb(1),
                       onescol_bf, gc_ones,
                       pp, gpp, kt == 0, kt == 31, f"D{p}{qc}_{kt}_{hh}",
                       tp=(0, 64 * hh))

            for kt in range(32):
                if qc == 0 and p > 0:
                    v_group(kt, 6 + kt % 2)
                emit_S(kt)
                if kt >= 2:
                    emit_A(kt - 2)
                emit_E(kt)
                if norm_bg and kt >= 10:
                    norm_bg.pop(0)()
                kv = min(max(kt - 4, 0), 22)
                if p > 0:
                    virt = -1 if qc == 0 else (qc - 1) * 23 + kv
                    want = 0 if virt < 0 else (nbg * (virt + 1)) // (7 * 23)
                else:
                    virt = qc * 23 + kv
                    want = (nbg * (virt + 1)) // (8 * 23)
                if 4 <= kt <= 26:
                    while bgi < want:
                        bg[bgi]()
                        bgi += 1
            emit_A(30)
            emit_A(31)
            dve(lambda e: e.tensor_copy(r4("scr4", 0, 128, 0, 512),
                                        bank(0, 0, 128)),
                gb(0), g4("scr4", 0, 512), f"avc{p}{qc}")
            dve(lambda e: e.tensor_copy(r4("scr4", 0, 65, 512, 512),
                                        bank(1, 0, 65)),
                gb(1), g4("scr4", 512, 512), f"avd{p}{qc}")
            dve(lambda e: e.reciprocal(r4("scglo", 0, 65, 0, 512),
                                       r4("scr4", 0, 65, 512, 512)),
                g4("scr4", 512, 512), g4("scglo", 0, 512), f"rc{p}{qc}")
            for hh in range(2):
                pb = 64 * hh
                rec = r4("scglo", pb, pb + 1, 0, 512)
                rchi = r2("rowt2", pb, pb + 1, 1472, 512)
                rclo = r2("rowt2", pb, pb + 1, 1984, 512)
                dve(lambda e, rec=rec, rchi=rchi: e.tensor_copy(rchi, rec),
                    g4("scglo", 0, 512), g2("rowt2", 1472, 512), f"rh{p}{qc}_{hh}")
                dve(lambda e, rec=rec, rchi=rchi, rclo=rclo: e.tensor_tensor(
                        rclo, rec, rchi, AluOpType.subtract),
                    g4("scglo", 0, 512) + g2("rowt2", 1472, 512),
                    g2("rowt2", 1984, 512), f"rl{p}{qc}_{hh}")

            def mk1(p=p, qc=qc, par=par):
                def t():
                    for hh in range(2):
                        pb = 64 * hh
                        ones = (onesrow_bf if hh == 0
                                else onesrow64_bf[0:1, 0:128])
                        onesap = (r2("consts2", 0, 1, 1, 64) if hh == 0
                                  else r2("consts2", 64, 65, 129, 64))
                        rchi = r2("rowt2", pb, pb + 1, 1472, 512)
                        rclo = r2("rowt2", pb, pb + 1, 1984, 512)
                        mm(bank(6, pb, pb + 64), gb(6), onesap, gc_ones,
                           rchi, g2("rowt2", 1472, 512), True, False,
                           f"bA{p}{qc}_{hh}")
                        mm(bank(6, pb, pb + 64), gb(6), onesap, gc_ones,
                           rclo, g2("rowt2", 1984, 512), False, True,
                           f"bB{p}{qc}_{hh}")
                    dve(lambda e: e.tensor_copy(r4("scot", 0, 128, 0, 512),
                                                bank(6, 0, 128)),
                        gb(6), g4("scot", 0, 512), f"bC{p}{qc}")
                    for hh in range(2):
                        pb = 64 * hh
                        stg = r2("astage", pb, pb + 64, par * 512, 512)
                        gstg = g2("astage", par * 512, 512)
                        dve(lambda e, stg=stg, pb=pb: e.tensor_tensor(
                                stg, r4("scr4", pb, pb + 64, 0, 512),
                                r4("scot", pb, pb + 64, 0, 512),
                                AluOpType.mult),
                            g4("scr4", 0, 512) + g4("scot", 0, 512), gstg,
                            f"ae{p}{qc}_{hh}")
                        dma("sync",
                            attn_sc[128 * p + 64 * hh:128 * p + 64 * hh + 64,
                                    qc * 512:(qc + 1) * 512],
                            stg, gstg, [f"attn.{p}_{hh}"], f"ao{p}{qc}_{hh}")
                return t

            norm_bg.append(mk1())
        while bgi < nbg:
            bg[bgi]()
            bgi += 1
    while norm_bg:
        norm_bg.pop(0)()

    dma("gpsimd", r2("woutw", 0, 128, 0, 8192).rearrange("p (dt o) -> p dt o", dt=8),
        wout_h.rearrange("(dt p) o -> p dt o", p=128), ["H.wout"],
        g2("woutw", 0, 8192), "wouti")
    dve(lambda e: e.memset(r2("ghl", 0, 128, 0, 640), 0.0), [], g2("ghl", 0, 640), "gh0")
    for dt in range(8):
        dve(lambda e, dt=dt: e.tensor_copy(r2("ghl", 0, 128, dt * 40, 8),
                                           r4("smalls", 0, 128, dt * 8, 8)),
            gsm + g2("ghl", 0, 640), g2("ghl", 0, 640), f"gh{dt}")
        pve(lambda e, dt=dt: e.tensor_tensor(r2("ghl", 0, 128, dt * 40 + 32, 8),
                                             r4("smalls", 0, 128, dt * 8, 8),
                                             r2("ghl", 0, 128, dt * 40, 8),
                                             AluOpType.subtract),
            gsm + g2("ghl", 0, 640), g2("ghl", 0, 640), f"gl{dt}")

    for tc in range(NT):
        cb = tc % 2
        dma("scalar",
            r2("attnc", 0, 128, cb * 4096, 4096).rearrange("p (dt t) -> p dt t", dt=8),
            attn_sc[:, tc * 512:(tc + 1) * 512].rearrange("(dt p) t -> p dt t", p=128),
            [f"attn.{pp_}_{hh_}" for pp_ in range(8) for hh_ in range(2)],
            g2("attnc", cb * 4096, 4096), f"ai{tc}")
        for oc in range(8):
            wb = oc % 2
            if oc % 4 == 0:
                half = oc // 4
                dma("sync", r4("xf32", 0, 128, 0, 2048)
                    .rearrange("p (dt t) -> p dt t", dt=4),
                    xT[half * 512:(half + 1) * 512, tc * 512:(tc + 1) * 512]
                    .rearrange("(dt p) t -> p dt t", p=128),
                    ["H.xT"], g4("xf32", 0, 2048), f"x2i{tc}_{half}")
            for dt in range(8):
                mm(bank(wb), gb(wb),
                   r2("woutw", 0, 128, dt * 1024 + oc * 128, 128),
                   g2("woutw", dt * 1024 + oc * 128, 128),
                   r2("attnc", 0, 128, cb * 4096 + dt * 512, 512),
                   g2("attnc", cb * 4096 + dt * 512, 512),
                   dt == 0, dt == 7, f"wo{tc}_{oc}_{dt}")
            dve(lambda e, oc=oc, wb=wb: e.scalar_tensor_tensor(
                    r4("x1c", 0, 128, oc * 512, 512), bank(wb),
                    r4("smalls", 0, 128, 64 + oc, 1),
                    r4("xf32", 0, 128, (oc % 4) * 512, 512),
                    AluOpType.add, AluOpType.add),
                gb(wb) + gsm + g4("xf32", (oc % 4) * 512, 512),
                g4("x1c", oc * 512, 512), f"x1_{tc}_{oc}")
        for oc in range(8):
            dve(lambda e, oc=oc: e.tensor_copy(r2("x1bfc", 0, 128, oc * 512, 512),
                                               r4("x1c", 0, 128, oc * 512, 512)),
                g4("x1c", oc * 512, 512), g2("x1bfc", oc * 512, 512), f"xb_{tc}_{oc}")
            dve(lambda e, oc=oc: e.tensor_tensor(r2("x1loc", 0, 128, oc * 512, 512),
                                                 r4("x1c", 0, 128, oc * 512, 512),
                                                 r2("x1bfc", 0, 128, oc * 512, 512),
                                                 AluOpType.subtract),
                g4("x1c", oc * 512, 512) + g2("x1bfc", oc * 512, 512),
                g2("x1loc", oc * 512, 512), f"xl_{tc}_{oc}")
            sq = r2("xsq", 0, 128, (oc % 2) * 512, 512)
            gsq = g2("xsq", (oc % 2) * 512, 512)
            dve(lambda e, oc=oc, o=sq: e.tensor_tensor(
                    o, r2("x1bfc", 0, 128, oc * 512, 512),
                    r2("x1bfc", 0, 128, oc * 512, 512), AluOpType.mult),
                g2("x1bfc", oc * 512, 512), gsq, f"sqC{tc}_{oc}")
            mm(bank(2, 0, 1), gb(2), onescol_bf, gc_ones,
               r2("x1bfc", 0, 128, oc * 512, 512), g2("x1bfc", oc * 512, 512),
               oc == 0, oc == 7, f"t1_{tc}_{oc}")
            mm(bank(3, 0, 1), gb(3), onescol_bf, gc_ones, sq, gsq,
               oc == 0, oc == 7, f"t2_{tc}_{oc}")
        dma("scalar", x1bf_sc[:, tc * 512:(tc + 1) * 512]
            .rearrange("(oc p) t -> p oc t", p=128),
            r2("x1bfc", 0, 128, 0, 4096).rearrange("p (oc t) -> p oc t", oc=8),
            g2("x1bfc", 0, 4096), [f"x1bf.{tc}"], f"x1o{tc}")
        dve(lambda e: e.tensor_scalar(rowslot(R_M), bank(2, 0, 1), 1.0 / D, None,
                                      AluOpType.mult), gb(2), rowslot_g(R_M), f"mC{tc}")
        dve(lambda e: e.tensor_scalar(rowslot(R_E2), bank(3, 0, 1), 1.0 / D, None,
                                      AluOpType.mult), gb(3), rowslot_g(R_E2), f"eC{tc}")
        dve(lambda e: e.tensor_tensor(rowslot(R_VAR), rowslot(R_M), rowslot(R_M),
                                      AluOpType.mult),
            rowslot_g(R_M), rowslot_g(R_VAR), f"m2C{tc}")
        dve(lambda e: e.tensor_tensor(rowslot(R_VAR), rowslot(R_E2), rowslot(R_VAR),
                                      AluOpType.subtract),
            rowslot_g(R_E2) + rowslot_g(R_VAR), rowslot_g(R_VAR), f"vC{tc}")
        act(lambda e: e.activation(rowslot(R_LN), rowslot(R_VAR), AF.Ln,
                                   bias=beps, scale=1.0),
            rowslot_g(R_VAR) + gsm, rowslot_g(R_LN), f"lnC{tc}")
        act(lambda e: e.activation(rowslot(R_R), rowslot(R_LN), AF.Exp,
                                   bias=bzero, scale=-0.5),
            rowslot_g(R_LN) + gsm, rowslot_g(R_R), f"r2C{tc}")
        dve(lambda e: e.tensor_scalar(rowslot(R_A), rowslot(R_M), -1.0, None,
                                      AluOpType.mult),
            rowslot_g(R_M), rowslot_g(R_A), f"nmC{tc}")
        dve(lambda e: e.tensor_tensor(rowslot(R_B), rowslot(R_A), rowslot(R_R),
                                      AluOpType.mult),
            rowslot_g(R_A) + rowslot_g(R_R), rowslot_g(R_B), f"nmrC{tc}")
        dve(lambda e: e.tensor_scalar(rowslot(R_VAR), rowslot(R_R), 1.0, None,
                                      AluOpType.add),
            rowslot_g(R_R), rowslot_g(R_VAR), f"rpC{tc}")
        dve(lambda e: e.tensor_tensor(rowslot(R_E2), rowslot(R_M), rowslot(R_R),
                                      AluOpType.mult),
            rowslot_g(R_M) + rowslot_g(R_R), rowslot_g(R_E2), f"mrC{tc}")
        for (slot, bnk, dsto) in ((R_VAR, 4, 0), (R_E2, 5, 512)):
            hi = r2("rowt2", 0, 1, 1536, 512)
            lo = r2("rowt2", 0, 1, 2048, 512)
            dve(lambda e, slot=slot, hi=hi: e.tensor_copy(hi, rowslot(slot)),
                rowslot_g(slot), g2("rowt2", 1536, 512), f"hi{tc}_{bnk}")
            dve(lambda e, slot=slot, hi=hi, lo=lo: e.tensor_tensor(
                    lo, rowslot(slot), hi, AluOpType.subtract),
                rowslot_g(slot) + g2("rowt2", 1536, 512), g2("rowt2", 2048, 512),
                f"lo{tc}_{bnk}")
            mm(bank(bnk), gb(bnk), onesrow_bf, gc_ones, hi, g2("rowt2", 1536, 512),
               True, False, f"bb{tc}_{bnk}a")
            mm(bank(bnk), gb(bnk), onesrow_bf, gc_ones, lo, g2("rowt2", 2048, 512),
               False, True, f"bb{tc}_{bnk}b")
            dve(lambda e, bnk=bnk, dsto=dsto: e.tensor_copy(
                    r4("scr4", 0, 128, dsto, 512), bank(bnk)),
                gb(bnk), g4("scr4", dsto, 512), f"bc{tc}_{bnk}")
        for oc in range(8):
            dve(lambda e, oc=oc: e.tensor_tensor(r4("scot", 0, 128, 0, 512),
                                                 r4("x1c", 0, 128, oc * 512, 512),
                                                 r4("scr4", 0, 128, 0, 512),
                                                 AluOpType.mult),
                g4("x1c", oc * 512, 512) + g4("scr4", 0, 512), g4("scot", 0, 512),
                f"u{tc}_{oc}")
            dve(lambda e, oc=oc: e.tensor_tensor(r4("x1c", 0, 128, oc * 512, 512),
                                                 r4("scot", 0, 128, 0, 512),
                                                 r4("scr4", 0, 128, 512, 512),
                                                 AluOpType.subtract),
                g4("scot", 0, 512) + g4("scr4", 512, 512),
                g4("x1c", oc * 512, 512), f"bs{tc}_{oc}")
        dma("scalar", base_sc[:, tc * 512:(tc + 1) * 512]
            .rearrange("(oc p) t -> p oc t", p=128),
            r4("x1c", 0, 128, 0, 4096).rearrange("p (oc t) -> p oc t", oc=8),
            g4("x1c", 0, 4096), [f"base.{tc}"], f"bo{tc}")
        for dt in range(8):
            mm(bank(6, 0, 8), gb(6), r2("ghl", 0, 128, dt * 40, 8), g2("ghl", 0, 640),
               r2("x1bfc", 0, 128, dt * 512, 512), g2("x1bfc", dt * 512, 512),
               dt == 0, dt == 7, f"gA{tc}_{dt}")
        for dt in range(8):
            mm(bank(5, 0, 8), gb(5), r2("ghl", 0, 128, dt * 40 + 32, 8),
               g2("ghl", 0, 640), r2("x1bfc", 0, 128, dt * 512, 512),
               g2("x1bfc", dt * 512, 512), dt == 0, False, f"gB{tc}_{dt}")
        for dt in range(8):
            mm(bank(5, 0, 8), gb(5), r2("ghl", 0, 128, dt * 40, 8),
               g2("ghl", 0, 640), r2("x1loc", 0, 128, dt * 512, 512),
               g2("x1loc", dt * 512, 512), False, dt == 7, f"gC{tc}_{dt}")
        dve(lambda e: e.tensor_copy(r4("scglo", 0, 8, 0, 512), bank(5, 0, 8)),
            gb(5), g4("scglo", 0, 512), f"ge{tc}")
        dve(lambda e: e.tensor_tensor(r4("scot", 0, 8, 0, 512), bank(6, 0, 8),
                                      r4("scglo", 0, 8, 0, 512), AluOpType.add),
            gb(6) + g4("scglo", 0, 512), g4("scot", 0, 512), f"gs{tc}")
        mm(bank(6, 0, 8), gb(6), ones1f[0:1, 0:8], gsm, rowslot(R_M),
           rowslot_g(R_M), True, True, f"mb{tc}")
        dve(lambda e: e.scalar_tensor_tensor(r4("scglo", 0, 8, 0, 512), bank(6, 0, 8),
                                             negG, r4("scot", 0, 8, 0, 512),
                                             AluOpType.mult, AluOpType.add),
            gb(6) + gsm + g4("scot", 0, 512), g4("scglo", 0, 512), f"sc{tc}")
        for sub in range(4):
            sb_ = 7 if sub % 2 == 0 else 5
            o = (sub % 2) * 72

            def s4(lo, n):
                return r4("sc2", 0, 128, o + lo, n)

            def gs4(lo, n):
                return g4("sc2", o + lo, n)

            tr(bank(sb_, 0, 128, 0, 8), gb(sb_),
               r4("scglo", 0, 8, sub * 128, 128), g4("scglo", 0, 512),
               ident128[0:8, 0:8], g4("ident", 0, 128), f"st{tc}_{sub}")
            dve(lambda e, a=s4(0, 8), sb_=sb_: e.tensor_copy(
                    a, bank(sb_, 0, 128, 0, 8)),
                gb(sb_), gs4(0, 8), f"sp{tc}_{sub}")
            pve(lambda e, a=s4(8, 8), b=s4(0, 8): e.tensor_copy(a, b),
                gs4(0, 8), gs4(8, 8), f"hA{tc}_{sub}")
            dve(lambda e, a=s4(16, 1), b=s4(8, 8): e.tensor_reduce(
                    a, b, mybir.AxisListType.X, AluOpType.max),
                gs4(8, 8), gs4(16, 1), f"mx{tc}_{sub}")
            pve(lambda e, a=s4(24, 8), b=s4(8, 8), c=s4(16, 1): e.tensor_scalar(
                    a, b, c, None, AluOpType.is_equal),
                gs4(8, 9), gs4(24, 8), f"eq{tc}_{sub}")
            dve(lambda e, a=s4(32, 8), b=s4(24, 8): e.tensor_tensor(
                    a, b, r4("wcol", 0, 128, 0, 8), AluOpType.mult),
                gs4(24, 8) + g4("wcol", 0, 8), gs4(32, 8), f"wm{tc}_{sub}")
            pve(lambda e, a=s4(40, 8), b=s4(32, 8): e.tensor_copy(a, b),
                gs4(32, 8), gs4(40, 8), f"hB{tc}_{sub}")
            dve(lambda e, a=s4(17, 1), b=s4(40, 8): e.tensor_reduce(
                    a, b, mybir.AxisListType.X, AluOpType.max),
                gs4(40, 8), gs4(17, 1), f"m2{tc}_{sub}")
            pve(lambda e, a=s4(48, 8), b=s4(40, 8), c=s4(17, 1): e.tensor_scalar(
                    a, b, c, None, AluOpType.is_equal),
                gs4(40, 9), gs4(48, 8), f"oh{tc}_{sub}")
            tr(bank(sb_, 0, 8, 8, 128), gb(sb_), s4(48, 8),
               gs4(48, 8), ident128, g4("ident", 0, 128), f"ot{tc}_{sub}")
            dve(lambda e, sub=sub, sb_=sb_: e.tensor_copy(
                    r2("maskst", 0, 8, sub * 128, 128), bank(sb_, 0, 8, 8, 128)),
                gb(sb_), g2("maskst", sub * 128, 128), f"oc{tc}_{sub}")
        pve(lambda e: e.tensor_copy(r2("maskst2", 0, 8, 0, 512),
                                    r2("maskst", 0, 8, 0, 512)),
            g2("maskst", 0, 512), g2("maskst2", 0, 512), f"mh{tc}")
        mm(bank(6, 0, 8), gb(6), ones1f[0:1, 0:8], gsm, rowslot(R_R),
           rowslot_g(R_R), True, True, f"rb8{tc}")
        dve(lambda e: e.tensor_tensor(r2("maskr8", 0, 8, 0, 512),
                                      r2("maskst2", 0, 8, 0, 512), bank(6, 0, 8),
                                      AluOpType.mult),
            g2("maskst2", 0, 512) + gb(6), g2("maskr8", 0, 512), f"mr{tc}")
        mm(bank(6, 0, 8), gb(6), ones1f[0:1, 0:8], gsm, rowslot(R_B),
           rowslot_g(R_B), True, True, f"nb8{tc}")
        dve(lambda e: e.tensor_tensor(r2("maskm8", 0, 8, 0, 512),
                                      r2("maskst2", 0, 8, 0, 512), bank(6, 0, 8),
                                      AluOpType.mult),
            g2("maskst2", 0, 512) + gb(6), g2("maskm8", 0, 512), f"mm{tc}")
        dma("scalar", mask_sc[0, :, tc * 512:(tc + 1) * 512],
            r2("maskr8", 0, 8, 0, 512), g2("maskr8", 0, 512), [f"mskA.{tc}"], f"mo{tc}")
        dma("scalar", mask_sc[1, :, tc * 512:(tc + 1) * 512],
            r2("maskm8", 0, 8, 0, 512), g2("maskm8", 0, 512), [f"mskB.{tc}"], f"m2o{tc}")

    for sweep in range(2):
        th0 = sweep * 2048
        dma("scalar", r2("x1half", 0, 128, 0, 16384)
            .rearrange("p (ht t) -> p ht t", ht=8),
            x1bf_sc[:, th0:th0 + 2048].rearrange("(ht p) t -> p ht t", p=128),
            [f"x1bf.{tc}" for tc in range(sweep * 4, sweep * 4 + 4)],
            g2("x1half", 0, 16384), f"xhi{sweep}")
        for ee in range(EXP):
            eb = ee % 2
            dma("gpsimd", r2("wexp", 0, 128, eb * 8192, 8192)
                .rearrange("p (ht f) -> p ht f", ht=8),
                wexpT_h[ee].rearrange("(ht p) f -> p ht f", p=128),
                ["H.wexpT"], g2("wexp", eb * 8192, 8192), f"we{sweep}_{ee}")
            for half in range(2):
                for ht in range(8):
                    mm(bank(6, 0, 1), gb(6), onescol_bf, gc_ones,
                       r2("wexp", 0, 128, eb * 8192 + ht * 1024 + half * 512, 512),
                       g2("wexp", eb * 8192 + ht * 1024 + half * 512, 512),
                       ht == 0, ht == 7, f"wsd{sweep}_{ee}_{half}")
                dve(lambda e, eb=eb, half=half: e.tensor_copy(
                        r2("wse", 0, 1, eb * 1024 + half * 512, 512), bank(6, 0, 1)),
                    gb(6), g2("wse", eb * 1024 + half * 512, 512),
                    f"wsv{sweep}_{ee}_{half}")
            mA = "mrA" if eb == 0 else "mrA2"
            mB = "mrB" if eb == 0 else "mrB2"
            dma("sync", r2(mA, 0, 1, 0, 2048),
                mask_sc[0, ee, th0:th0 + 2048][None, :],
                [f"mskA.{tc}" for tc in range(sweep * 4, sweep * 4 + 4)],
                g2(mA, 0, 2048), f"mai{sweep}_{ee}")
            dma("sync", r2(mB, 0, 1, 0, 2048),
                mask_sc[1, ee, th0:th0 + 2048][None, :],
                [f"mskB.{tc}" for tc in range(sweep * 4, sweep * 4 + 4)],
                g2(mB, 0, 2048), f"mbi{sweep}_{ee}")
            for tci in range(4):
                hb = tci % 2
                mm(bank(7), gb(7), onesrow_bf, gc_ones,
                   r2(mA, 0, 1, tci * 512, 512), g2(mA, tci * 512, 512),
                   True, True, f"Bb{sweep}_{ee}_{tci}")
                dve(lambda e, hb=hb: e.tensor_copy(r2("bsb", 0, 128, hb * 512, 512),
                                                   bank(7)),
                    gb(7), g2("bsb", hb * 512, 512), f"Bc{sweep}_{ee}_{tci}")
                for ht in range(8):
                    dve(lambda e, hb=hb, ht=ht, tci=tci: e.tensor_tensor(
                            r2("h2m", 0, 128, hb * 4096 + ht * 512, 512),
                            r2("x1half", 0, 128, ht * 2048 + tci * 512, 512),
                            r2("bsb", 0, 128, hb * 512, 512), AluOpType.mult),
                        g2("x1half", ht * 2048 + tci * 512, 512)
                        + g2("bsb", hb * 512, 512),
                        g2("h2m", hb * 4096 + ht * 512, 512), f"hm{sweep}_{ee}_{tci}_{ht}")
                for fc in range(8):
                    bnk = fc % 6
                    for ht in range(8):
                        mm(bank(bnk), gb(bnk),
                           r2("wexp", 0, 128, eb * 8192 + ht * 1024 + fc * 128, 128),
                           g2("wexp", eb * 8192 + ht * 1024 + fc * 128, 128),
                           r2("h2m", 0, 128, hb * 4096 + ht * 512, 512),
                           g2("h2m", hb * 4096 + ht * 512, 512),
                           ht == 0, False, f"ym{sweep}_{ee}_{tci}_{fc}_{ht}")
                    mm(bank(bnk), gb(bnk),
                       r2("wse", 0, 1, eb * 1024 + fc * 128, 128),
                       g2("wse", eb * 1024 + fc * 128, 128),
                       r2(mB, 0, 1, tci * 512, 512), g2(mB, tci * 512, 512),
                       False, True, f"yc{sweep}_{ee}_{tci}_{fc}")
                    if ee == 0:
                        dve(lambda e, bnk=bnk, fc=fc, tci=tci: e.tensor_copy(
                                r2("acc", 0, 128, fc * 2048 + tci * 512, 512),
                                bank(bnk)),
                            gb(bnk), g2("acc", fc * 2048 + tci * 512, 512),
                            f"ac{sweep}_{ee}_{tci}_{fc}")
                    else:
                        dve(lambda e, bnk=bnk, fc=fc, tci=tci: e.tensor_tensor(
                                r2("acc", 0, 128, fc * 2048 + tci * 512, 512),
                                bank(bnk),
                                r2("acc", 0, 128, fc * 2048 + tci * 512, 512),
                                AluOpType.add),
                            gb(bnk) + g2("acc", fc * 2048 + tci * 512, 512),
                            g2("acc", fc * 2048 + tci * 512, 512),
                            f"ac{sweep}_{ee}_{tci}_{fc}")
        for tci in range(4):
            tc = sweep * 4 + tci
            dma("sync", r4("based", 0, 128, 0, 4096)
                .rearrange("p (oc t) -> p oc t", oc=8),
                base_sc[:, tc * 512:(tc + 1) * 512]
                .rearrange("(oc p) t -> p oc t", p=128),
                [f"base.{tc}"], g4("based", 0, 4096), f"bi{tc}")
            for fc in range(8):
                dve(lambda e, fc=fc, tci=tci: e.tensor_tensor(
                        r4("outc", 0, 128, fc * 512, 512),
                        r2("acc", 0, 128, fc * 2048 + tci * 512, 512),
                        r4("based", 0, 128, fc * 512, 512), AluOpType.add),
                    g2("acc", fc * 2048 + tci * 512, 512) + g4("based", fc * 512, 512),
                    g4("outc", fc * 512, 512), f"oe{tc}_{fc}")
            dma("scalar", outT_h[:, tc * 512:(tc + 1) * 512]
                .rearrange("(oc p) t -> p oc t", p=128),
                r4("outc", 0, 128, 0, 4096).rearrange("p (oc t) -> p oc t", oc=8),
                g4("outc", 0, 4096), [f"out.{tc}"], f"ot{tc}")

    S.emit("scalar", lambda e: e.nop(),
           reads=[f"out.{tc}" for tc in range(NT)], tag="final")
    S.finalize()
    return nc, S


_CACHE = {}


def _get_nc():
    if "nc" not in _CACHE:
        nc, S = build_program()
        S.check()
        import contextlib
        stack = contextlib.ExitStack()
        sems = {e: stack.enter_context(nc.semaphore(f"sem_{e}")) for e in ENGINES}
        with nc.Block() as block:
            S.replay(block, sems)
        stack.close()
        _CACHE["nc"] = nc
    return _CACHE["nc"]


def kernel(L0_attn_data, L0_attn_W_qkv, L0_attn_b_qkv, L0_attn_W_out, L0_attn_b_out,
           L0_ffn_gate_w, L0_ffn_expert_w):
    x = np.asarray(L0_attn_data, dtype=np.float32)
    wqkv = np.ascontiguousarray(np.asarray(L0_attn_W_qkv, dtype=np.float32))
    bqkv = np.ascontiguousarray(np.asarray(L0_attn_b_qkv, dtype=np.float32)
                                .reshape(1, 3 * D))
    wout = np.ascontiguousarray(np.asarray(L0_attn_W_out, dtype=np.float32))
    boutc = np.ascontiguousarray(np.asarray(L0_attn_b_out, dtype=np.float32)
                                 .reshape(8, 128).T)
    gwT = np.ascontiguousarray(np.asarray(L0_ffn_gate_w, dtype=np.float32).T)
    wexpT = np.ascontiguousarray(np.asarray(L0_ffn_expert_w, dtype=np.float32)
                                 .transpose(0, 2, 1))
    nc = _get_nc()
    in_maps = []
    for b in range(8):
        in_maps.append({
            "xT": np.ascontiguousarray(x[b].T),
            "wqkv": wqkv, "bqkv": bqkv, "wout": wout, "boutc": boutc,
            "gwT": gwT, "wexpT": wexpT,
        })
    res = run_bass_kernel_spmd(nc, in_maps, core_ids=list(range(8)))
    return np.stack([np.ascontiguousarray(res.results[b]["outT"].T) for b in range(8)])


if __name__ == "__main__":
    nc, S = build_program()
    n = S.check()
    print("plan ok:", n, "instrs;", {e: len(S.plan[e]) for e in ENGINES},
          "waits:", S.n_waits, "incs:", sum(len(S.inc_at[e]) for e in ENGINES))


# version 46
# speedup vs baseline: 1.0360x; 1.0360x over previous
import numpy as np
import concourse.bass as bass
import concourse.mybir as mybir
from concourse.alu_op_type import AluOpType
from concourse.bass_utils import run_bass_kernel_spmd

F32 = mybir.dt.float32
BF16 = mybir.dt.bfloat16
U32 = mybir.dt.uint32
AF = mybir.ActivationFunctionType

D = 1024
T = 4096
NT = 8
NPAIR = 8
EXP = 8
C_SHIFT = -9.0

ENGINES = ("sync", "gpsimd", "tensor", "vector", "scalar")


class Sched:
    def __init__(self):
        self.plan = {e: [] for e in ENGINES}
        self.seq = {e: 0 for e in ENGINES}
        self.res = {}
        self.n_waits = 0

    def emit(self, eng, fn, reads=(), writes=(), dma=False, tag=""):
        myseq = self.seq[eng]
        token = (eng, myseq, dma)
        waits = {}

        def need(tok):
            peng, pseq, pdma = tok
            if peng == eng and not pdma:
                return
            waits[peng] = max(waits.get(peng, -1), pseq)

        for r in reads:
            st = self.res.get(r)
            if st is not None and st[0] is not None:
                need(st[0])
        for r in writes:
            st = self.res.get(r)
            if st is not None:
                if st[0] is not None:
                    need(st[0])
                for t in st[1]:
                    need(t)
        for r in reads:
            st = self.res.setdefault(r, [None, []])
            st[1].append(token)
        for r in writes:
            self.res[r] = [token, []]
        wl = sorted(waits.items())
        self.n_waits += len(wl)
        self.plan[eng].append([wl, fn, myseq, dma, tag])
        self.seq[eng] += 1

    def finalize(self):
        targets = {e: set() for e in ENGINES}
        for e in ENGINES:
            for wl, fn, myseq, dma, tag in self.plan[e]:
                if dma:
                    targets[e].add(myseq)
                for peng, pseq in wl:
                    targets[peng].add(pseq)
        self.inc_at = {e: targets[e] for e in ENGINES}
        self.amt = {}
        self.val = {}
        for e in ENGINES:
            tot = 0
            for wl, fn, myseq, dma, tag in self.plan[e]:
                if myseq in self.inc_at[e]:
                    a = 16 if dma else 1
                    self.amt[(e, myseq)] = a
                    tot += a
                self.val[(e, myseq)] = tot
        for e in ENGINES:
            seen = {}
            n_kept = 0
            for entry in self.plan[e]:
                wl = []
                for p, q in entry[0]:
                    v = self.val[(p, q)]
                    if seen.get(p, -1) < v:
                        seen[p] = v
                        wl.append((p, v))
                n_kept += len(wl)
                entry[0] = wl
        self.n_waits = sum(len(en[0]) for ee_ in ENGINES for en in self.plan[ee_])

    def check(self):
        idx = {e: 0 for e in ENGINES}
        counts = {e: 0 for e in ENGINES}
        total = sum(len(v) for v in self.plan.values())
        done = 0
        progress = True
        while progress:
            progress = False
            for e in ENGINES:
                while idx[e] < len(self.plan[e]):
                    wl, fn, myseq, dma, tag = self.plan[e][idx[e]]
                    if all(counts[p] >= v for p, v in wl):
                        if myseq in self.inc_at[e]:
                            counts[e] += self.amt[(e, myseq)]
                        idx[e] += 1
                        done += 1
                        progress = True
                    else:
                        break
        if done != total:
            msgs = []
            for e in ENGINES:
                if idx[e] < len(self.plan[e]):
                    wl, fn, myseq, dma, tag = self.plan[e][idx[e]]
                    blocked = [(p, v, counts[p]) for p, v in wl if counts[p] < v]
                    msgs.append(f"{e}@{idx[e]} tag={tag} blocked={blocked}")
            raise RuntimeError("DEADLOCK: " + " | ".join(msgs))
        return total

    def replay(self, block, sems):
        for e in ENGINES:
            entries = self.plan[e]
            inc_at = self.inc_at[e]
            amts = self.amt
            mysem = sems[e]

            def body(eng, entries=entries, e=e, inc_at=inc_at, mysem=mysem):
                for wl, fn, myseq, dma, tag in entries:
                    for peng, v in wl:
                        eng.wait_ge(sems[peng], v)
                    instr = fn(eng)
                    if myseq in inc_at:
                        if instr is None:
                            instr = eng.nop()
                        instr.then_inc(mysem, amts[(e, myseq)])
            getattr(block, e)(body)


def build_program():
    nc = bass.Bass("TRN2", target_bir_lowering=False)

    xT = nc.dram_tensor("xT", [D, T], F32, kind="ExternalInput").ap()
    wqkv_h = nc.dram_tensor("wqkv", [D, 3 * D], F32, kind="ExternalInput").ap()
    bqkv_h = nc.dram_tensor("bqkv", [1, 3 * D], F32, kind="ExternalInput").ap()
    wout_h = nc.dram_tensor("wout", [D, D], F32, kind="ExternalInput").ap()
    boutc_h = nc.dram_tensor("boutc", [128, 8], F32, kind="ExternalInput").ap()
    gwT_h = nc.dram_tensor("gwT", [D, EXP], F32, kind="ExternalInput").ap()
    wexpT_h = nc.dram_tensor("wexpT", [EXP, D, D], F32, kind="ExternalInput").ap()
    outT_h = nc.dram_tensor("outT", [D, T], F32, kind="ExternalOutput").ap()

    attn_sc = nc.dram_tensor("attn_sc", [D, T], BF16, kind="Internal").ap()
    x1bf_sc = nc.dram_tensor("x1bf_sc", [D, T], BF16, kind="Internal").ap()
    base_sc = nc.dram_tensor("base_sc", [D, T], F32, kind="Internal").ap()
    mask_sc = nc.dram_tensor("mask_sc", [2, EXP, T], BF16, kind="Internal").ap()

    L2 = {}

    def a2(name, off, n):
        L2[name] = (off, n)
        assert off + n <= 80960, (name, off + n)

    a2("xbf", 0, 32768)
    a2("qT", 32768, 4096)
    a2("kT", 36864, 4096)
    a2("vext", 40960, 4160)
    a2("astage", 45120, 2048)
    a2("wst", 47168, 3072)
    a2("rows2", 50240, 12288)
    a2("bqkvr", 62528, 3072)
    a2("rowt2", 65600, 2560)
    a2("consts2", 68160, 448)
    a2("pp", 68608, 4096)
    a2("xsq", 68608, 1024)
    a2("woutw", 32768, 8192)
    a2("attnc", 40960, 8192)
    a2("x1bfc", 50240, 4096)
    a2("x1loc", 54336, 4096)
    a2("ghl", 58432, 640)
    a2("maskst", 59072, 512)
    a2("maskr8", 59584, 512)
    a2("maskm8", 60096, 512)
    a2("maskst2", 60608, 512)
    a2("wexp", 0, 16384)
    a2("x1half", 16384, 16384)
    a2("h2m", 32768, 8192)
    a2("acc", 40960, 16384)
    a2("bsb", 57344, 1024)
    a2("wse", 58368, 2048)
    a2("mrA", 60416, 2048)
    a2("mrB", 65600, 2048)
    a2("mrA2", 62464, 2048)
    a2("mrB2", 68608, 2048)
    a2("qT2", 72768, 4096)
    a2("kT2", 76864, 4096)
    E2 = 80960

    L4 = {}

    def a4(name, off, n):
        L4[name] = (off, n)
        assert off + n <= 12624, (name, off + n)

    a4("rb1", 0, 4096)
    a4("xf32", 4096, 2048)
    a4("scr4", 6144, 1024)
    a4("r1cols", 7168, 64)
    a4("rowp", 7232, 3584)
    a4("smalls", 10816, 320)
    a4("ident", 11136, 128)
    a4("iorow", 11264, 128)
    a4("sctm", 11392, 32)
    a4("wcol", 11424, 8)
    a4("sctm2", 11432, 24)
    a4("scot", 11456, 512)
    a4("scglo", 11968, 512)
    a4("sc2", 12480, 144)
    a4("x1c", 0, 4096)
    a4("outc", 0, 4096)
    a4("based", 4096, 4096)
    E4 = 12624

    assert 2 * E2 + 4 * E4 <= 212500, (E2, E4, 2 * E2 + 4 * E4)

    AR2 = nc.sbuf_tensor("AR2", [128, E2], BF16).__enter__()
    AR4 = nc.sbuf_tensor("AR4", [128, E4], F32).__enter__()
    U32B = nc.sbuf_tensor("U32B", [128, 8], U32).__enter__()
    PS = nc.psum_tensor("PS", [128, 4096], F32).__enter__()

    def r2(name, p0, p1, lo, n):
        o, sz = L2[name]
        assert lo + n <= sz, (name, lo, n, sz)
        return AR2[p0:p1, o + lo: o + lo + n]

    def r4(name, p0, p1, lo, n):
        o, sz = L4[name]
        assert lo + n <= sz, (name, lo, n, sz)
        return AR4[p0:p1, o + lo: o + lo + n]

    def g2(name, lo, n):
        o = L2[name][0] + lo
        return [f"A2.{g}" for g in range(o // 512, (o + n - 1) // 512 + 1)]

    def g4(name, lo, n):
        o = L4[name][0] + lo
        return [f"A4.{g}" for g in range(o // 512, (o + n - 1) // 512 + 1)]

    def bank(i, p0=0, p1=128, lo=0, n=512):
        return PS[p0:p1, i * 512 + lo: i * 512 + lo + n]

    def gb(i):
        return [f"PS.{i}"]

    S = Sched()

    def mm(ps, gps, lhsT, glh, rhs, grh, start, stop, tag=""):
        S.emit("tensor", lambda e, p=ps, l=lhsT, r=rhs, st=start, sp=stop:
               e.matmul(p, l, r, start=st, stop=sp),
               reads=list(glh) + list(grh), writes=list(gps), tag=tag)

    def tr(ps, gps, in_, gin, ident, gid, tag=""):
        S.emit("tensor", lambda e, p=ps, i=in_, d=ident: e.transpose(p, i, d),
               reads=list(gin) + list(gid), writes=list(gps), tag=tag)

    def dve(fn, reads, writes, tag=""):
        S.emit("vector", fn, reads=reads, writes=writes, tag=tag)

    def pve(fn, reads, writes, tag=""):
        S.emit("gpsimd", fn, reads=reads, writes=writes, tag=tag)

    def act(fn, reads, writes, tag=""):
        S.emit("scalar", fn, reads=reads, writes=writes, tag=tag)

    def dma(q, out, in_, reads, writes, tag=""):
        S.emit(q, lambda e, o=out, i=in_: e.dma_start(out=o, in_=i),
               reads=reads, writes=writes, dma=True, tag=tag)

    onescol_bf = r2("consts2", 0, 128, 0, 1)
    onesrow_bf = r2("consts2", 0, 1, 1, 128)
    onesrow64_bf = r2("consts2", 64, 65, 129, 128)
    gc_ones = g2("consts2", 0, 512)
    dve(lambda e: e.memset(r2("consts2", 0, 128, 0, 1), 1.0), [], gc_ones, "c0")
    dve(lambda e: e.memset(r2("consts2", 0, 1, 1, 128), 1.0), [], gc_ones, "c1")
    dve(lambda e: e.memset(r2("consts2", 64, 65, 129, 128), 1.0), [], gc_ones, "c1b")
    dve(lambda e: e.memset(r4("r1cols", 0, 128, 32, 1), C_SHIFT),
        [], g4("r1cols", 32, 1), "c1c")
    ones1f = r4("smalls", 0, 1, 128, 128)
    onescol_f = r4("smalls", 0, 128, 256, 1)
    bexp = r4("smalls", 0, 128, 257, 1)
    beps = r4("smalls", 0, 1, 258, 1)
    bzero = r4("smalls", 0, 1, 259, 1)
    ident11 = r4("smalls", 0, 1, 260, 1)
    iocol = r4("smalls", 0, 128, 261, 1)
    negG = r4("smalls", 0, 8, 262, 1)
    ecol8 = r4("smalls", 0, 8, 261, 1)
    gsm = g4("smalls", 0, 512)
    dve(lambda e: e.memset(r4("smalls", 0, 1, 128, 128), 1.0), [], gsm, "c2")
    dve(lambda e: e.memset(r4("smalls", 0, 128, 256, 1), 1.0), [], gsm, "c3")
    dve(lambda e: e.memset(r4("smalls", 0, 128, 257, 1), C_SHIFT), [], gsm, "c4")
    dve(lambda e: e.memset(r4("smalls", 0, 1, 258, 1), 1e-5), [], gsm, "c5")
    dve(lambda e: e.memset(r4("smalls", 0, 1, 259, 1), 0.0), [], gsm, "c6")
    dve(lambda e: e.memset(r4("smalls", 0, 128, 264, 1), 0.0), [], gsm, "c6b")
    dve(lambda e: e.memset(r4("smalls", 0, 1, 260, 1), 1.0), [], gsm, "c7")
    iorow = r4("iorow", 0, 1, 0, 128)
    for i in range(128):
        dve(lambda e, i=i: e.memset(r4("iorow", 0, 1, i, 1), float(i)),
            [], g4("iorow", i, 1), f"io{i}")
    for ei in range(8):
        dve(lambda e, ei=ei: e.memset(r4("wcol", 0, 128, ei, 1), float(8 - ei)),
            [], g4("wcol", ei, 1), f"wc{ei}")
    tr(bank(7, 0, 128, 0, 1), gb(7), iorow, g4("iorow", 0, 128), ident11, gsm, "ioct")
    dve(lambda e: e.tensor_copy(r4("smalls", 0, 128, 261, 1), bank(7, 0, 128, 0, 1)),
        gb(7), gsm, "iocp")
    mm(bank(6, 0, 128, 0, 128), gb(6), ones1f, gsm, iorow, g4("iorow", 0, 128), True, True, "iobc")
    pve(lambda e: e.tensor_copy(r4("smalls", 0, 128, 263, 1),
                                r4("smalls", 0, 128, 261, 1)), gsm, gsm, "ioc2")
    ident128 = r4("ident", 0, 128, 0, 128)
    dve(lambda e: e.tensor_scalar(r4("ident", 0, 128, 0, 128), bank(6, 0, 128, 0, 128),
                                  r4("smalls", 0, 128, 263, 1), None, AluOpType.is_equal),
        gb(6) + gsm, g4("ident", 0, 128), "ident")

    dma("sync", r4("smalls", 0, 128, 0, 64).rearrange("p (dt e) -> p dt e", dt=8),
        gwT_h.rearrange("(dt p) e -> p dt e", p=128), ["H.gw"], gsm, "gw_in")
    dma("sync", r4("smalls", 0, 128, 64, 8), boutc_h, ["H.boutc"], gsm, "bout_in")
    dma("gpsimd", r2("bqkvr", 0, 1, 0, 3072), bqkv_h, ["H.bqkv"],
        g2("bqkvr", 0, 3072), "bqkv_in")
    for dt in range(8):
        mm(bank(5, 0, 1, 0, 8), gb(5), onescol_f, gsm,
           r4("smalls", 0, 128, dt * 8, 8), gsm, dt == 0, dt == 7, "Gsum")
    dve(lambda e: e.tensor_scalar(r4("rowp", 0, 1, 0, 8), bank(5, 0, 1, 0, 8), -1.0,
                                  None, AluOpType.mult), gb(5), g4("rowp", 0, 8), "negGr")
    tr(bank(5, 0, 8, 512, 1), gb(5), r4("rowp", 0, 1, 0, 8), g4("rowp", 0, 8),
       ident11, gsm, "negGt")
    dve(lambda e: e.tensor_copy(r4("smalls", 0, 8, 262, 1), bank(5, 0, 8, 512, 1)),
        gb(5), gsm, "negGc")

    def rowslot(i, n=512, p0=0, p1=1):
        return r4("rowp", p0, p1, i * 512, n)

    def rowslot_g(i, n=512):
        return g4("rowp", i * 512, n)

    R_M, R_E2, R_VAR, R_LN, R_R, R_A, R_B = 0, 1, 2, 3, 4, 5, 6

    def emit_phaseA_tc(tc):
        for dt in range(8):
            if dt % 4 == 0:
                half = dt // 4
                dma("sync", r4("xf32", 0, 128, 0, 2048)
                    .rearrange("p (dt t) -> p dt t", dt=4),
                    xT[half * 512:(half + 1) * 512, tc * 512:(tc + 1) * 512]
                    .rearrange("(dt p) t -> p dt t", p=128),
                    ["H.xT"], g4("xf32", 0, 2048), f"xin{tc}_{half}")
            xf = r4("xf32", 0, 128, (dt % 4) * 512, 512)
            gxf = g4("xf32", (dt % 4) * 512, 512)
            xb = r2("xbf", 0, 128, dt * 4096 + tc * 512, 512)
            gxb = g2("xbf", dt * 4096 + tc * 512, 512)
            dve(lambda e, o=xb, i=xf: e.tensor_copy(o, i), gxf, gxb, f"cast{tc}_{dt}")
            sq = r2("xsq", 0, 128, (dt % 2) * 512, 512)
            gsq = g2("xsq", (dt % 2) * 512, 512)
            dve(lambda e, o=sq, i=xb: e.tensor_tensor(o, i, i, AluOpType.mult),
                gxb, gsq, f"sqA{tc}_{dt}")
            mm(bank(0, 0, 1), gb(0), onescol_bf, gc_ones, xb, gxb,
               dt == 0, dt == 7, f"s1_{tc}_{dt}")
            mm(bank(1, 0, 1), gb(1), onescol_bf, gc_ones, sq, gsq,
               dt == 0, dt == 7, f"s2_{tc}_{dt}")
        dve(lambda e: e.tensor_scalar(rowslot(R_M), bank(0, 0, 1), 1.0 / D, None,
                                      AluOpType.mult), gb(0), rowslot_g(R_M), f"mA{tc}")
        dve(lambda e: e.tensor_scalar(rowslot(R_E2), bank(1, 0, 1), 1.0 / D, None,
                                      AluOpType.mult), gb(1), rowslot_g(R_E2), f"eA{tc}")
        dve(lambda e: e.tensor_tensor(rowslot(R_VAR), rowslot(R_M), rowslot(R_M),
                                      AluOpType.mult),
            rowslot_g(R_M), rowslot_g(R_VAR), f"m2A{tc}")
        dve(lambda e: e.tensor_tensor(rowslot(R_VAR), rowslot(R_E2), rowslot(R_VAR),
                                      AluOpType.subtract),
            rowslot_g(R_E2) + rowslot_g(R_VAR), rowslot_g(R_VAR), f"vA{tc}")
        act(lambda e: e.activation(rowslot(R_LN), rowslot(R_VAR), AF.Ln,
                                   bias=beps, scale=1.0),
            rowslot_g(R_VAR) + gsm, rowslot_g(R_LN), f"lnA{tc}")
        act(lambda e: e.activation(rowslot(R_R), rowslot(R_LN), AF.Exp,
                                   bias=bzero, scale=-0.5),
            rowslot_g(R_LN) + gsm, rowslot_g(R_R), f"rA{tc}")
        act(lambda e, tc=tc: e.activation(r2("rows2", 0, 1, 4096 + tc * 512, 512),
                                          rowslot(R_LN), AF.Exp, bias=bzero, scale=0.5),
            rowslot_g(R_LN) + gsm, g2("rows2", 4096 + tc * 512, 512), f"ivA{tc}")
        dve(lambda e, tc=tc: e.tensor_scalar(r2("rows2", 0, 1, 8192 + tc * 512, 512),
                                             rowslot(R_M), -1.0, None, AluOpType.mult),
            rowslot_g(R_M), g2("rows2", 8192 + tc * 512, 512), f"nmA{tc}")
        dve(lambda e, tc=tc: e.scalar_tensor_tensor(r2("rows2", 0, 1, tc * 512, 512),
                                                    rowslot(R_R), -1.0, rowslot(R_M),
                                                    AluOpType.mult, AluOpType.mult),
            rowslot_g(R_R) + rowslot_g(R_M), g2("rows2", tc * 512, 512), f"nrmA{tc}")
        rhi = r2("rowt2", 0, 1, 0, 512)
        rlo = r2("rowt2", 0, 1, 512, 512)
        dve(lambda e: e.tensor_copy(rhi, rowslot(R_R)), rowslot_g(R_R),
            g2("rowt2", 0, 512), f"rhiA{tc}")
        dve(lambda e: e.tensor_tensor(rlo, rowslot(R_R), rhi, AluOpType.subtract),
            rowslot_g(R_R) + g2("rowt2", 0, 512), g2("rowt2", 512, 512), f"rloA{tc}")
        mm(bank(2), gb(2), onesrow_bf, gc_ones, rhi, g2("rowt2", 0, 512),
           True, False, f"rbA{tc}")
        mm(bank(2), gb(2), onesrow_bf, gc_ones, rlo, g2("rowt2", 512, 512),
           False, True, f"rbB{tc}")
        dve(lambda e, tc=tc: e.tensor_copy(r4("rb1", 0, 128, tc * 512, 512), bank(2)),
            gb(2), g4("rb1", tc * 512, 512), f"rbC{tc}")
        for sub in range(4):
            tr(bank(3, 0, 128, sub, 1), gb(3),
               r4("rowp", 0, 1, R_R * 512 + sub * 128, 128), rowslot_g(R_R),
               ident11, gsm, f"rcT{tc}_{sub}")
        dve(lambda e, tc=tc: e.tensor_copy(r4("r1cols", 0, 128, tc * 4, 4),
                                           bank(3, 0, 128, 0, 4)),
            gb(3), g4("r1cols", tc * 4, 4), f"rcC{tc}")

    wq = r2("wst", 0, 128, 0, 1024)
    wk = r2("wst", 0, 128, 1024, 1024)
    wv = r2("wst", 0, 128, 2048, 1024)
    gwq = g2("wst", 0, 1024)
    gwk = g2("wst", 1024, 1024)
    gwv = g2("wst", 2048, 1024)
    wsq_r = r2("rowt2", 0, 1, 1024, 128)
    wsk_r = r2("rowt2", 0, 1, 1152, 128)
    wsv_r = r2("rowt2", 0, 1, 1280, 128)
    gws0 = g2("rowt2", 1024, 128)
    gws1 = g2("rowt2", 1152, 128)
    gws2 = g2("rowt2", 1280, 128)

    def proj_thunks(pn):
        th = []
        qdst = "qT" if pn % 2 == 0 else "qT2"
        kdst = "kT" if pn % 2 == 0 else "kT2"
        for (w, gw_, col0, nm) in ((wq, gwq, 128 * pn, "q"),
                                   (wk, gwk, D + 128 * pn, "k"),
                                   (wv, gwv, 2 * D + 128 * pn, "v")):
            th.append(lambda w=w, gw_=gw_, col0=col0, nm=nm: dma(
                "gpsimd", w.rearrange("p (dt o) -> p dt o", dt=8),
                wqkv_h[:, col0:col0 + 128].rearrange("(dt p2) o -> p2 dt o", p2=128),
                ["H.wqkv"], gw_, f"w{nm}i{pn}"))
        for si, (w, gw_) in enumerate(((wq, gwq), (wk, gwk), (wv, gwv))):
            for dt in range(8):
                th.append(lambda w=w, gw_=gw_, si=si, dt=dt: mm(
                    bank(7, 0, 1, si * 128, 128), gb(7), onescol_bf, gc_ones,
                    w[:, dt * 128:(dt + 1) * 128], gw_, dt == 0, dt == 7,
                    f"ws{pn}{si}"))
            th.append(lambda si=si: dve(
                lambda e, si=si: e.tensor_copy(
                    r2("rowt2", 0, 1, 1024 + si * 128, 128),
                    bank(7, 0, 1, si * 128, 128)),
                gb(7), g2("rowt2", 1024 + si * 128, 128), f"wsc{pn}{si}"))
        bq_r = r2("bqkvr", 0, 1, 128 * pn, 128)
        bk_r = r2("bqkvr", 0, 1, D + 128 * pn, 128)
        gbq = g2("bqkvr", 128 * pn, 128)
        gbk = g2("bqkvr", D + 128 * pn, 128)
        for tc in range(NT):
            for (dst, w, gw_, wsr, gwsr, br, gbr) in (
                    (qdst, wq, gwq, wsq_r, gws0, bq_r, gbq),
                    (kdst, wk, gwk, wsk_r, gws1, bk_r, gbk)):
                for dt in range(8):
                    th.append(lambda dst=dst, w=w, gw_=gw_, tc=tc, dt=dt: mm(
                        bank(7), gb(7), w[:, dt * 128:(dt + 1) * 128], gw_,
                        r2("xbf", 0, 128, dt * 4096 + tc * 512, 512),
                        g2("xbf", dt * 4096 + tc * 512, 512),
                        dt == 0, False, f"{dst}m{pn}_{tc}_{dt}"))
                th.append(lambda dst=dst, wsr=wsr, gwsr=gwsr, tc=tc: mm(
                    bank(7), gb(7), wsr, gwsr,
                    r2("rows2", 0, 1, tc * 512, 512), g2("rows2", tc * 512, 512),
                    False, False, f"{dst}c1{pn}_{tc}"))
                th.append(lambda dst=dst, br=br, gbr=gbr, tc=tc: mm(
                    bank(7), gb(7), br, gbr,
                    r2("rows2", 0, 1, 4096 + tc * 512, 512),
                    g2("rows2", 4096 + tc * 512, 512), False, True,
                    f"{dst}c2{pn}_{tc}"))
                th.append(lambda dst=dst, tc=tc: dve(
                    lambda e, dst=dst, tc=tc: e.tensor_tensor(
                        r2(dst, 0, 128, tc * 512, 512), bank(7),
                        r4("rb1", 0, 128, tc * 512, 512), AluOpType.mult),
                    gb(7) + g4("rb1", tc * 512, 512), g2(dst, tc * 512, 512),
                    f"{dst}e{pn}_{tc}"))
        return th

    th0 = proj_thunks(0)
    n0 = len(th0)
    for tc in range(NT):
        emit_phaseA_tc(tc)
        want0 = min(n0, 30 + 22 * (tc + 1))
        while n0 - len(th0) < want0:
            th0.pop(0)()
    while th0:
        th0.pop(0)()
    norm_bg = []
    for p in range(NPAIR):
        qsrc = "qT" if p % 2 == 0 else "qT2"
        ksrc = "kT" if p % 2 == 0 else "kT2"
        if p == 0:
            dve(lambda e: e.memset(
                r2("vext", 0, 128, 0, 4160)
                .rearrange("p (kt c) -> p kt c", kt=32)[:, :, 64:130:65], 1.0),
                [], g2("vext", 0, 4160), "vones")
        bv_r = r2("bqkvr", 0, 1, 2 * D + 128 * p, 128)
        gbv = g2("bqkvr", 2 * D + 128 * p, 128)

        def v_group(kt, vb, p=p, bv_r=bv_r, gbv=gbv):
            for dt in range(8):
                mm(bank(vb, 0, 128, 0, 128), gb(vb),
                   r2("xbf", 0, 128, dt * 4096 + kt * 128, 128),
                   g2("xbf", dt * 4096 + kt * 128, 128),
                   wv[:, dt * 128:(dt + 1) * 128], gwv, dt == 0, False, f"vm{p}_{kt}")
            mm(bank(vb, 0, 128, 0, 128), gb(vb),
               r2("rows2", 0, 1, 8192 + kt * 128, 128),
               g2("rows2", 8192 + kt * 128, 128), wsv_r, gws2, False, False,
               f"vc1{p}_{kt}")
            mm(bank(vb, 0, 128, 0, 128), gb(vb),
               r2("rows2", 0, 1, 4096 + kt * 128, 128),
               g2("rows2", 4096 + kt * 128, 128), bv_r, gbv, False, True, f"vc2{p}_{kt}")
            for hh in range(2):
                dve(lambda e, kt=kt, hh=hh, vb=vb: e.tensor_scalar(
                        r2("vext", 0, 128, kt * 130 + 65 * hh, 64),
                        bank(vb, 0, 128, hh * 64, 64),
                        r4("r1cols", 0, 128, kt, 1), None, AluOpType.mult),
                    gb(vb) + g4("r1cols", kt, 1), g2("vext", kt * 130 + 65 * hh, 64),
                    f"ve{p}_{kt}_{hh}")

        if p == 0:
            for kt in range(32):
                v_group(kt, 5 + kt % 2)

        bg = proj_thunks(p + 1) if p < NPAIR - 1 else []
        nbg = len(bg)
        bgi = 0
        for qc in range(NT):
            par = qc % 2

            def emit_S(kt, qc=qc, qsrc=qsrc, ksrc=ksrc):
                for hh in range(2):
                    s_b = 2 + 2 * (kt % 2) + hh
                    mm(bank(s_b), gb(s_b),
                       r2(ksrc, 64 * hh, 64 * hh + 64, kt * 128, 128),
                       g2(ksrc, kt * 128, 128),
                       r2(qsrc, 64 * hh, 64 * hh + 64, qc * 512, 512),
                       g2(qsrc, qc * 512, 512), True, True, f"S{p}{qc}_{kt}_{hh}")

            def emit_E(kt, qc=qc):
                pb = 2 + 2 * (kt % 2)
                pp = r2("pp", 0, 128, (kt % 4) * 1024, 1024)
                gpp = g2("pp", (kt % 4) * 1024, 1024)
                act(lambda e, o=pp, b=pb: e.activation(
                        o, bank(b, 0, 128, 0, 1024), AF.Exp,
                        bias=r4("r1cols", 0, 128, 32, 1), scale=0.125),
                    gb(pb) + gb(pb + 1) + g4("r1cols", 32, 1), gpp, f"E{p}{qc}_{kt}")

            def emit_A(kt, qc=qc):
                for hh in range(2):
                    pp = r2("pp", 0, 128, (kt % 4) * 1024 + 512 * hh, 512)
                    gpp = g2("pp", (kt % 4) * 1024 + 512 * hh, 512)
                    mm(bank(hh, 0, 65), gb(hh),
                       r2("vext", 0, 128, kt * 130 + 65 * hh, 65),
                       g2("vext", kt * 130 + 65 * hh, 65),
                       pp, gpp, kt == 0, kt == 31, f"A{p}{qc}_{kt}_{hh}")

            for kt in range(32):
                if qc == 0 and p > 0:
                    v_group(kt, 6 + kt % 2)
                emit_S(kt)
                if kt >= 2:
                    emit_A(kt - 2)
                emit_E(kt)
                if norm_bg and kt >= 10:
                    norm_bg.pop(0)()
                kv = min(max(kt - 4, 0), 22)
                if p > 0:
                    virt = -1 if qc == 0 else (qc - 1) * 23 + kv
                    want = 0 if virt < 0 else (nbg * (virt + 1)) // (7 * 23)
                else:
                    virt = qc * 23 + kv
                    want = (nbg * (virt + 1)) // (8 * 23)
                if 4 <= kt <= 26:
                    while bgi < want:
                        bg[bgi]()
                        bgi += 1
            emit_A(30)
            emit_A(31)
            rowsets = (
                (("rowt2", 1536), ("rowt2", 2048)),
                (("astage", 0), ("astage", 512)),
                (("astage", 1024), ("astage", 1536)),
                (("rowt2", 1536), ("rowt2", 2048)),
            )
            for hh in range(2):
                cidx = hh + 2 * par
                dve(lambda e, hh=hh: e.tensor_copy(
                        r4("scr4", 0, 65, hh * 512, 512), bank(hh, 0, 65)),
                    gb(hh), g4("scr4", hh * 512, 512), f"avc{p}{qc}_{hh}")
                cT = "scot" if hh == 0 else "scglo"
                rslot = R_A if hh == 0 else R_B
                rec = rowslot(rslot, p0=64, p1=65)
                grec = rowslot_g(rslot)
                dve(lambda e, rec=rec, hh=hh: e.reciprocal(
                        rec, r4("scr4", 64, 65, hh * 512, 512)),
                    g4("scr4", hh * 512, 512), grec, f"rc{p}{qc}_{hh}")
                (nhi, ohi), (nlo, olo) = rowsets[cidx]
                rchi = r2(nhi, 64, 65, ohi, 512)
                ghi = g2(nhi, ohi, 512)
                rclo = r2(nlo, 64, 65, olo, 512)
                glo = g2(nlo, olo, 512)
                dve(lambda e, rec=rec, rchi=rchi: e.tensor_copy(rchi, rec), grec,
                    ghi, f"rh{p}{qc}_{hh}")

                def mk1(rchi=rchi, ghi=ghi, cT=cT, p=p, qc=qc, hh=hh, par=par):
                    def t():
                        mm(bank(6, 0, 64), gb(6), onesrow64_bf[0:1, 0:64], gc_ones,
                           rchi, ghi, True, True, f"bA{p}{qc}_{hh}")
                        dve(lambda e: e.tensor_copy(r4(cT, 0, 64, 0, 512),
                                                    bank(6, 0, 64)),
                            gb(6), g4(cT, 0, 512), f"bC{p}{qc}_{hh}")
                        stg = r2("astage", 0, 64, hh * 1024 + par * 512, 512)
                        gstg = g2("astage", hh * 1024 + par * 512, 512)
                        dve(lambda e, stg=stg: e.tensor_tensor(
                                stg, r4("scr4", 0, 64, hh * 512, 512),
                                r4(cT, 0, 64, 0, 512), AluOpType.mult),
                            g4("scr4", hh * 512, 512) + g4(cT, 0, 512), gstg,
                            f"ae{p}{qc}_{hh}")
                        dma("sync",
                            attn_sc[128 * p + 64 * hh:128 * p + 64 * hh + 64,
                                    qc * 512:(qc + 1) * 512],
                            stg, gstg, [f"attn.{p}_{hh}"], f"ao{p}{qc}_{hh}")
                    return t

                norm_bg.append(mk1())
        while bgi < nbg:
            bg[bgi]()
            bgi += 1
    while norm_bg:
        norm_bg.pop(0)()

    dma("gpsimd", r2("woutw", 0, 128, 0, 8192).rearrange("p (dt o) -> p dt o", dt=8),
        wout_h.rearrange("(dt p) o -> p dt o", p=128), ["H.wout"],
        g2("woutw", 0, 8192), "wouti")
    dve(lambda e: e.memset(r2("ghl", 0, 128, 0, 640), 0.0), [], g2("ghl", 0, 640), "gh0")
    for dt in range(8):
        dve(lambda e, dt=dt: e.tensor_copy(r2("ghl", 0, 128, dt * 40, 8),
                                           r4("smalls", 0, 128, dt * 8, 8)),
            gsm + g2("ghl", 0, 640), g2("ghl", 0, 640), f"gh{dt}")
        pve(lambda e, dt=dt: e.tensor_tensor(r2("ghl", 0, 128, dt * 40 + 32, 8),
                                             r4("smalls", 0, 128, dt * 8, 8),
                                             r2("ghl", 0, 128, dt * 40, 8),
                                             AluOpType.subtract),
            gsm + g2("ghl", 0, 640), g2("ghl", 0, 640), f"gl{dt}")

    for tc in range(NT):
        cb = tc % 2
        dma("scalar",
            r2("attnc", 0, 128, cb * 4096, 4096).rearrange("p (dt t) -> p dt t", dt=8),
            attn_sc[:, tc * 512:(tc + 1) * 512].rearrange("(dt p) t -> p dt t", p=128),
            [f"attn.{pp_}_{hh_}" for pp_ in range(8) for hh_ in range(2)],
            g2("attnc", cb * 4096, 4096), f"ai{tc}")
        for oc in range(8):
            wb = oc % 2
            if oc % 4 == 0:
                half = oc // 4
                dma("sync", r4("xf32", 0, 128, 0, 2048)
                    .rearrange("p (dt t) -> p dt t", dt=4),
                    xT[half * 512:(half + 1) * 512, tc * 512:(tc + 1) * 512]
                    .rearrange("(dt p) t -> p dt t", p=128),
                    ["H.xT"], g4("xf32", 0, 2048), f"x2i{tc}_{half}")
            for dt in range(8):
                mm(bank(wb), gb(wb),
                   r2("woutw", 0, 128, dt * 1024 + oc * 128, 128),
                   g2("woutw", dt * 1024 + oc * 128, 128),
                   r2("attnc", 0, 128, cb * 4096 + dt * 512, 512),
                   g2("attnc", cb * 4096 + dt * 512, 512),
                   dt == 0, dt == 7, f"wo{tc}_{oc}_{dt}")
            dve(lambda e, oc=oc, wb=wb: e.scalar_tensor_tensor(
                    r4("x1c", 0, 128, oc * 512, 512), bank(wb),
                    r4("smalls", 0, 128, 64 + oc, 1),
                    r4("xf32", 0, 128, (oc % 4) * 512, 512),
                    AluOpType.add, AluOpType.add),
                gb(wb) + gsm + g4("xf32", (oc % 4) * 512, 512),
                g4("x1c", oc * 512, 512), f"x1_{tc}_{oc}")
        for oc in range(8):
            dve(lambda e, oc=oc: e.tensor_copy(r2("x1bfc", 0, 128, oc * 512, 512),
                                               r4("x1c", 0, 128, oc * 512, 512)),
                g4("x1c", oc * 512, 512), g2("x1bfc", oc * 512, 512), f"xb_{tc}_{oc}")
            dve(lambda e, oc=oc: e.tensor_tensor(r2("x1loc", 0, 128, oc * 512, 512),
                                                 r4("x1c", 0, 128, oc * 512, 512),
                                                 r2("x1bfc", 0, 128, oc * 512, 512),
                                                 AluOpType.subtract),
                g4("x1c", oc * 512, 512) + g2("x1bfc", oc * 512, 512),
                g2("x1loc", oc * 512, 512), f"xl_{tc}_{oc}")
            sq = r2("xsq", 0, 128, (oc % 2) * 512, 512)
            gsq = g2("xsq", (oc % 2) * 512, 512)
            dve(lambda e, oc=oc, o=sq: e.tensor_tensor(
                    o, r2("x1bfc", 0, 128, oc * 512, 512),
                    r2("x1bfc", 0, 128, oc * 512, 512), AluOpType.mult),
                g2("x1bfc", oc * 512, 512), gsq, f"sqC{tc}_{oc}")
            mm(bank(2, 0, 1), gb(2), onescol_bf, gc_ones,
               r2("x1bfc", 0, 128, oc * 512, 512), g2("x1bfc", oc * 512, 512),
               oc == 0, oc == 7, f"t1_{tc}_{oc}")
            mm(bank(3, 0, 1), gb(3), onescol_bf, gc_ones, sq, gsq,
               oc == 0, oc == 7, f"t2_{tc}_{oc}")
        dma("scalar", x1bf_sc[:, tc * 512:(tc + 1) * 512]
            .rearrange("(oc p) t -> p oc t", p=128),
            r2("x1bfc", 0, 128, 0, 4096).rearrange("p (oc t) -> p oc t", oc=8),
            g2("x1bfc", 0, 4096), [f"x1bf.{tc}"], f"x1o{tc}")
        dve(lambda e: e.tensor_scalar(rowslot(R_M), bank(2, 0, 1), 1.0 / D, None,
                                      AluOpType.mult), gb(2), rowslot_g(R_M), f"mC{tc}")
        dve(lambda e: e.tensor_scalar(rowslot(R_E2), bank(3, 0, 1), 1.0 / D, None,
                                      AluOpType.mult), gb(3), rowslot_g(R_E2), f"eC{tc}")
        dve(lambda e: e.tensor_tensor(rowslot(R_VAR), rowslot(R_M), rowslot(R_M),
                                      AluOpType.mult),
            rowslot_g(R_M), rowslot_g(R_VAR), f"m2C{tc}")
        dve(lambda e: e.tensor_tensor(rowslot(R_VAR), rowslot(R_E2), rowslot(R_VAR),
                                      AluOpType.subtract),
            rowslot_g(R_E2) + rowslot_g(R_VAR), rowslot_g(R_VAR), f"vC{tc}")
        act(lambda e: e.activation(rowslot(R_LN), rowslot(R_VAR), AF.Ln,
                                   bias=beps, scale=1.0),
            rowslot_g(R_VAR) + gsm, rowslot_g(R_LN), f"lnC{tc}")
        act(lambda e: e.activation(rowslot(R_R), rowslot(R_LN), AF.Exp,
                                   bias=bzero, scale=-0.5),
            rowslot_g(R_LN) + gsm, rowslot_g(R_R), f"r2C{tc}")
        dve(lambda e: e.tensor_scalar(rowslot(R_A), rowslot(R_M), -1.0, None,
                                      AluOpType.mult),
            rowslot_g(R_M), rowslot_g(R_A), f"nmC{tc}")
        dve(lambda e: e.tensor_tensor(rowslot(R_B), rowslot(R_A), rowslot(R_R),
                                      AluOpType.mult),
            rowslot_g(R_A) + rowslot_g(R_R), rowslot_g(R_B), f"nmrC{tc}")
        dve(lambda e: e.tensor_scalar(rowslot(R_VAR), rowslot(R_R), 1.0, None,
                                      AluOpType.add),
            rowslot_g(R_R), rowslot_g(R_VAR), f"rpC{tc}")
        dve(lambda e: e.tensor_tensor(rowslot(R_E2), rowslot(R_M), rowslot(R_R),
                                      AluOpType.mult),
            rowslot_g(R_M) + rowslot_g(R_R), rowslot_g(R_E2), f"mrC{tc}")
        for (slot, bnk, dsto) in ((R_VAR, 4, 0), (R_E2, 5, 512)):
            hi = r2("rowt2", 0, 1, 1536, 512)
            lo = r2("rowt2", 0, 1, 2048, 512)
            dve(lambda e, slot=slot, hi=hi: e.tensor_copy(hi, rowslot(slot)),
                rowslot_g(slot), g2("rowt2", 1536, 512), f"hi{tc}_{bnk}")
            dve(lambda e, slot=slot, hi=hi, lo=lo: e.tensor_tensor(
                    lo, rowslot(slot), hi, AluOpType.subtract),
                rowslot_g(slot) + g2("rowt2", 1536, 512), g2("rowt2", 2048, 512),
                f"lo{tc}_{bnk}")
            mm(bank(bnk), gb(bnk), onesrow_bf, gc_ones, hi, g2("rowt2", 1536, 512),
               True, False, f"bb{tc}_{bnk}a")
            mm(bank(bnk), gb(bnk), onesrow_bf, gc_ones, lo, g2("rowt2", 2048, 512),
               False, True, f"bb{tc}_{bnk}b")
            dve(lambda e, bnk=bnk, dsto=dsto: e.tensor_copy(
                    r4("scr4", 0, 128, dsto, 512), bank(bnk)),
                gb(bnk), g4("scr4", dsto, 512), f"bc{tc}_{bnk}")
        for oc in range(8):
            dve(lambda e, oc=oc: e.tensor_tensor(r4("scot", 0, 128, 0, 512),
                                                 r4("x1c", 0, 128, oc * 512, 512),
                                                 r4("scr4", 0, 128, 0, 512),
                                                 AluOpType.mult),
                g4("x1c", oc * 512, 512) + g4("scr4", 0, 512), g4("scot", 0, 512),
                f"u{tc}_{oc}")
            dve(lambda e, oc=oc: e.tensor_tensor(r4("x1c", 0, 128, oc * 512, 512),
                                                 r4("scot", 0, 128, 0, 512),
                                                 r4("scr4", 0, 128, 512, 512),
                                                 AluOpType.subtract),
                g4("scot", 0, 512) + g4("scr4", 512, 512),
                g4("x1c", oc * 512, 512), f"bs{tc}_{oc}")
        dma("scalar", base_sc[:, tc * 512:(tc + 1) * 512]
            .rearrange("(oc p) t -> p oc t", p=128),
            r4("x1c", 0, 128, 0, 4096).rearrange("p (oc t) -> p oc t", oc=8),
            g4("x1c", 0, 4096), [f"base.{tc}"], f"bo{tc}")
        for dt in range(8):
            mm(bank(6, 0, 8), gb(6), r2("ghl", 0, 128, dt * 40, 8), g2("ghl", 0, 640),
               r2("x1bfc", 0, 128, dt * 512, 512), g2("x1bfc", dt * 512, 512),
               dt == 0, dt == 7, f"gA{tc}_{dt}")
        for dt in range(8):
            mm(bank(5, 0, 8), gb(5), r2("ghl", 0, 128, dt * 40 + 32, 8),
               g2("ghl", 0, 640), r2("x1bfc", 0, 128, dt * 512, 512),
               g2("x1bfc", dt * 512, 512), dt == 0, False, f"gB{tc}_{dt}")
        for dt in range(8):
            mm(bank(5, 0, 8), gb(5), r2("ghl", 0, 128, dt * 40, 8),
               g2("ghl", 0, 640), r2("x1loc", 0, 128, dt * 512, 512),
               g2("x1loc", dt * 512, 512), False, dt == 7, f"gC{tc}_{dt}")
        dve(lambda e: e.tensor_copy(r4("scglo", 0, 8, 0, 512), bank(5, 0, 8)),
            gb(5), g4("scglo", 0, 512), f"ge{tc}")
        dve(lambda e: e.tensor_tensor(r4("scot", 0, 8, 0, 512), bank(6, 0, 8),
                                      r4("scglo", 0, 8, 0, 512), AluOpType.add),
            gb(6) + g4("scglo", 0, 512), g4("scot", 0, 512), f"gs{tc}")
        mm(bank(6, 0, 8), gb(6), ones1f[0:1, 0:8], gsm, rowslot(R_M),
           rowslot_g(R_M), True, True, f"mb{tc}")
        dve(lambda e: e.scalar_tensor_tensor(r4("scglo", 0, 8, 0, 512), bank(6, 0, 8),
                                             negG, r4("scot", 0, 8, 0, 512),
                                             AluOpType.mult, AluOpType.add),
            gb(6) + gsm + g4("scot", 0, 512), g4("scglo", 0, 512), f"sc{tc}")
        for sub in range(4):
            sb_ = 7 if sub % 2 == 0 else 5
            o = (sub % 2) * 72

            def s4(lo, n):
                return r4("sc2", 0, 128, o + lo, n)

            def gs4(lo, n):
                return g4("sc2", o + lo, n)

            tr(bank(sb_, 0, 128, 0, 8), gb(sb_),
               r4("scglo", 0, 8, sub * 128, 128), g4("scglo", 0, 512),
               ident128[0:8, 0:8], g4("ident", 0, 128), f"st{tc}_{sub}")
            dve(lambda e, a=s4(0, 8), sb_=sb_: e.tensor_copy(
                    a, bank(sb_, 0, 128, 0, 8)),
                gb(sb_), gs4(0, 8), f"sp{tc}_{sub}")
            pve(lambda e, a=s4(8, 8), b=s4(0, 8): e.tensor_copy(a, b),
                gs4(0, 8), gs4(8, 8), f"hA{tc}_{sub}")
            dve(lambda e, a=s4(16, 1), b=s4(8, 8): e.tensor_reduce(
                    a, b, mybir.AxisListType.X, AluOpType.max),
                gs4(8, 8), gs4(16, 1), f"mx{tc}_{sub}")
            pve(lambda e, a=s4(24, 8), b=s4(8, 8), c=s4(16, 1): e.tensor_scalar(
                    a, b, c, None, AluOpType.is_equal),
                gs4(8, 9), gs4(24, 8), f"eq{tc}_{sub}")
            dve(lambda e, a=s4(32, 8), b=s4(24, 8): e.tensor_tensor(
                    a, b, r4("wcol", 0, 128, 0, 8), AluOpType.mult),
                gs4(24, 8) + g4("wcol", 0, 8), gs4(32, 8), f"wm{tc}_{sub}")
            pve(lambda e, a=s4(40, 8), b=s4(32, 8): e.tensor_copy(a, b),
                gs4(32, 8), gs4(40, 8), f"hB{tc}_{sub}")
            dve(lambda e, a=s4(17, 1), b=s4(40, 8): e.tensor_reduce(
                    a, b, mybir.AxisListType.X, AluOpType.max),
                gs4(40, 8), gs4(17, 1), f"m2{tc}_{sub}")
            pve(lambda e, a=s4(48, 8), b=s4(40, 8), c=s4(17, 1): e.tensor_scalar(
                    a, b, c, None, AluOpType.is_equal),
                gs4(40, 9), gs4(48, 8), f"oh{tc}_{sub}")
            tr(bank(sb_, 0, 8, 8, 128), gb(sb_), s4(48, 8),
               gs4(48, 8), ident128, g4("ident", 0, 128), f"ot{tc}_{sub}")
            dve(lambda e, sub=sub, sb_=sb_: e.tensor_copy(
                    r2("maskst", 0, 8, sub * 128, 128), bank(sb_, 0, 8, 8, 128)),
                gb(sb_), g2("maskst", sub * 128, 128), f"oc{tc}_{sub}")
        pve(lambda e: e.tensor_copy(r2("maskst2", 0, 8, 0, 512),
                                    r2("maskst", 0, 8, 0, 512)),
            g2("maskst", 0, 512), g2("maskst2", 0, 512), f"mh{tc}")
        mm(bank(6, 0, 8), gb(6), ones1f[0:1, 0:8], gsm, rowslot(R_R),
           rowslot_g(R_R), True, True, f"rb8{tc}")
        dve(lambda e: e.tensor_tensor(r2("maskr8", 0, 8, 0, 512),
                                      r2("maskst2", 0, 8, 0, 512), bank(6, 0, 8),
                                      AluOpType.mult),
            g2("maskst2", 0, 512) + gb(6), g2("maskr8", 0, 512), f"mr{tc}")
        mm(bank(6, 0, 8), gb(6), ones1f[0:1, 0:8], gsm, rowslot(R_B),
           rowslot_g(R_B), True, True, f"nb8{tc}")
        dve(lambda e: e.tensor_tensor(r2("maskm8", 0, 8, 0, 512),
                                      r2("maskst2", 0, 8, 0, 512), bank(6, 0, 8),
                                      AluOpType.mult),
            g2("maskst2", 0, 512) + gb(6), g2("maskm8", 0, 512), f"mm{tc}")
        dma("scalar", mask_sc[0, :, tc * 512:(tc + 1) * 512],
            r2("maskr8", 0, 8, 0, 512), g2("maskr8", 0, 512), [f"mskA.{tc}"], f"mo{tc}")
        dma("scalar", mask_sc[1, :, tc * 512:(tc + 1) * 512],
            r2("maskm8", 0, 8, 0, 512), g2("maskm8", 0, 512), [f"mskB.{tc}"], f"m2o{tc}")

    for sweep in range(2):
        th0 = sweep * 2048
        dma("scalar", r2("x1half", 0, 128, 0, 16384)
            .rearrange("p (ht t) -> p ht t", ht=8),
            x1bf_sc[:, th0:th0 + 2048].rearrange("(ht p) t -> p ht t", p=128),
            [f"x1bf.{tc}" for tc in range(sweep * 4, sweep * 4 + 4)],
            g2("x1half", 0, 16384), f"xhi{sweep}")
        for ee in range(EXP):
            eb = ee % 2
            dma("gpsimd", r2("wexp", 0, 128, eb * 8192, 8192)
                .rearrange("p (ht f) -> p ht f", ht=8),
                wexpT_h[ee].rearrange("(ht p) f -> p ht f", p=128),
                ["H.wexpT"], g2("wexp", eb * 8192, 8192), f"we{sweep}_{ee}")
            for half in range(2):
                for ht in range(8):
                    mm(bank(6, 0, 1), gb(6), onescol_bf, gc_ones,
                       r2("wexp", 0, 128, eb * 8192 + ht * 1024 + half * 512, 512),
                       g2("wexp", eb * 8192 + ht * 1024 + half * 512, 512),
                       ht == 0, ht == 7, f"wsd{sweep}_{ee}_{half}")
                dve(lambda e, eb=eb, half=half: e.tensor_copy(
                        r2("wse", 0, 1, eb * 1024 + half * 512, 512), bank(6, 0, 1)),
                    gb(6), g2("wse", eb * 1024 + half * 512, 512),
                    f"wsv{sweep}_{ee}_{half}")
            mA = "mrA" if eb == 0 else "mrA2"
            mB = "mrB" if eb == 0 else "mrB2"
            dma("sync", r2(mA, 0, 1, 0, 2048),
                mask_sc[0, ee, th0:th0 + 2048][None, :],
                [f"mskA.{tc}" for tc in range(sweep * 4, sweep * 4 + 4)],
                g2(mA, 0, 2048), f"mai{sweep}_{ee}")
            dma("sync", r2(mB, 0, 1, 0, 2048),
                mask_sc[1, ee, th0:th0 + 2048][None, :],
                [f"mskB.{tc}" for tc in range(sweep * 4, sweep * 4 + 4)],
                g2(mB, 0, 2048), f"mbi{sweep}_{ee}")
            for tci in range(4):
                hb = tci % 2
                mm(bank(7), gb(7), onesrow_bf, gc_ones,
                   r2(mA, 0, 1, tci * 512, 512), g2(mA, tci * 512, 512),
                   True, True, f"Bb{sweep}_{ee}_{tci}")
                dve(lambda e, hb=hb: e.tensor_copy(r2("bsb", 0, 128, hb * 512, 512),
                                                   bank(7)),
                    gb(7), g2("bsb", hb * 512, 512), f"Bc{sweep}_{ee}_{tci}")
                for ht in range(8):
                    dve(lambda e, hb=hb, ht=ht, tci=tci: e.tensor_tensor(
                            r2("h2m", 0, 128, hb * 4096 + ht * 512, 512),
                            r2("x1half", 0, 128, ht * 2048 + tci * 512, 512),
                            r2("bsb", 0, 128, hb * 512, 512), AluOpType.mult),
                        g2("x1half", ht * 2048 + tci * 512, 512)
                        + g2("bsb", hb * 512, 512),
                        g2("h2m", hb * 4096 + ht * 512, 512), f"hm{sweep}_{ee}_{tci}_{ht}")
                for fc in range(8):
                    bnk = fc % 6
                    for ht in range(8):
                        mm(bank(bnk), gb(bnk),
                           r2("wexp", 0, 128, eb * 8192 + ht * 1024 + fc * 128, 128),
                           g2("wexp", eb * 8192 + ht * 1024 + fc * 128, 128),
                           r2("h2m", 0, 128, hb * 4096 + ht * 512, 512),
                           g2("h2m", hb * 4096 + ht * 512, 512),
                           ht == 0, False, f"ym{sweep}_{ee}_{tci}_{fc}_{ht}")
                    mm(bank(bnk), gb(bnk),
                       r2("wse", 0, 1, eb * 1024 + fc * 128, 128),
                       g2("wse", eb * 1024 + fc * 128, 128),
                       r2(mB, 0, 1, tci * 512, 512), g2(mB, tci * 512, 512),
                       False, True, f"yc{sweep}_{ee}_{tci}_{fc}")
                    if ee == 0:
                        dve(lambda e, bnk=bnk, fc=fc, tci=tci: e.tensor_copy(
                                r2("acc", 0, 128, fc * 2048 + tci * 512, 512),
                                bank(bnk)),
                            gb(bnk), g2("acc", fc * 2048 + tci * 512, 512),
                            f"ac{sweep}_{ee}_{tci}_{fc}")
                    else:
                        dve(lambda e, bnk=bnk, fc=fc, tci=tci: e.tensor_tensor(
                                r2("acc", 0, 128, fc * 2048 + tci * 512, 512),
                                bank(bnk),
                                r2("acc", 0, 128, fc * 2048 + tci * 512, 512),
                                AluOpType.add),
                            gb(bnk) + g2("acc", fc * 2048 + tci * 512, 512),
                            g2("acc", fc * 2048 + tci * 512, 512),
                            f"ac{sweep}_{ee}_{tci}_{fc}")
        for tci in range(4):
            tc = sweep * 4 + tci
            dma("sync", r4("based", 0, 128, 0, 4096)
                .rearrange("p (oc t) -> p oc t", oc=8),
                base_sc[:, tc * 512:(tc + 1) * 512]
                .rearrange("(oc p) t -> p oc t", p=128),
                [f"base.{tc}"], g4("based", 0, 4096), f"bi{tc}")
            for fc in range(8):
                dve(lambda e, fc=fc, tci=tci: e.tensor_tensor(
                        r4("outc", 0, 128, fc * 512, 512),
                        r2("acc", 0, 128, fc * 2048 + tci * 512, 512),
                        r4("based", 0, 128, fc * 512, 512), AluOpType.add),
                    g2("acc", fc * 2048 + tci * 512, 512) + g4("based", fc * 512, 512),
                    g4("outc", fc * 512, 512), f"oe{tc}_{fc}")
            dma("scalar", outT_h[:, tc * 512:(tc + 1) * 512]
                .rearrange("(oc p) t -> p oc t", p=128),
                r4("outc", 0, 128, 0, 4096).rearrange("p (oc t) -> p oc t", oc=8),
                g4("outc", 0, 4096), [f"out.{tc}"], f"ot{tc}")

    S.emit("scalar", lambda e: e.nop(),
           reads=[f"out.{tc}" for tc in range(NT)], tag="final")
    S.finalize()
    return nc, S


_CACHE = {}


def _get_nc():
    if "nc" not in _CACHE:
        nc, S = build_program()
        S.check()
        import contextlib
        stack = contextlib.ExitStack()
        sems = {e: stack.enter_context(nc.semaphore(f"sem_{e}")) for e in ENGINES}
        with nc.Block() as block:
            S.replay(block, sems)
        stack.close()
        _CACHE["nc"] = nc
    return _CACHE["nc"]


def kernel(L0_attn_data, L0_attn_W_qkv, L0_attn_b_qkv, L0_attn_W_out, L0_attn_b_out,
           L0_ffn_gate_w, L0_ffn_expert_w):
    x = np.asarray(L0_attn_data, dtype=np.float32)
    wqkv = np.ascontiguousarray(np.asarray(L0_attn_W_qkv, dtype=np.float32))
    bqkv = np.ascontiguousarray(np.asarray(L0_attn_b_qkv, dtype=np.float32)
                                .reshape(1, 3 * D))
    wout = np.ascontiguousarray(np.asarray(L0_attn_W_out, dtype=np.float32))
    boutc = np.ascontiguousarray(np.asarray(L0_attn_b_out, dtype=np.float32)
                                 .reshape(8, 128).T)
    gwT = np.ascontiguousarray(np.asarray(L0_ffn_gate_w, dtype=np.float32).T)
    wexpT = np.ascontiguousarray(np.asarray(L0_ffn_expert_w, dtype=np.float32)
                                 .transpose(0, 2, 1))
    nc = _get_nc()
    in_maps = []
    for b in range(8):
        in_maps.append({
            "xT": np.ascontiguousarray(x[b].T),
            "wqkv": wqkv, "bqkv": bqkv, "wout": wout, "boutc": boutc,
            "gwT": gwT, "wexpT": wexpT,
        })
    res = run_bass_kernel_spmd(nc, in_maps, core_ids=list(range(8)))
    return np.stack([np.ascontiguousarray(res.results[b]["outT"].T) for b in range(8)])


if __name__ == "__main__":
    nc, S = build_program()
    n = S.check()
    print("plan ok:", n, "instrs;", {e: len(S.plan[e]) for e in ENGINES},
          "waits:", S.n_waits, "incs:", sum(len(S.inc_at[e]) for e in ENGINES))


# version 47 (final)
# speedup vs baseline: 1.0425x; 1.0062x over previous
import numpy as np
import concourse.bass as bass
import concourse.mybir as mybir
from concourse.alu_op_type import AluOpType
from concourse.bass_utils import run_bass_kernel_spmd

F32 = mybir.dt.float32
BF16 = mybir.dt.bfloat16
U32 = mybir.dt.uint32
AF = mybir.ActivationFunctionType

D = 1024
T = 4096
NT = 8
NPAIR = 8
EXP = 8
C_SHIFT = -9.0

ENGINES = ("sync", "gpsimd", "tensor", "vector", "scalar")


class Sched:
    def __init__(self):
        self.plan = {e: [] for e in ENGINES}
        self.seq = {e: 0 for e in ENGINES}
        self.res = {}
        self.n_waits = 0

    def emit(self, eng, fn, reads=(), writes=(), dma=False, tag=""):
        myseq = self.seq[eng]
        token = (eng, myseq, dma)
        waits = {}

        def need(tok):
            peng, pseq, pdma = tok
            if peng == eng and not pdma:
                return
            waits[peng] = max(waits.get(peng, -1), pseq)

        for r in reads:
            st = self.res.get(r)
            if st is not None and st[0] is not None:
                need(st[0])
        for r in writes:
            st = self.res.get(r)
            if st is not None:
                if st[0] is not None:
                    need(st[0])
                for t in st[1]:
                    need(t)
        for r in reads:
            st = self.res.setdefault(r, [None, []])
            st[1].append(token)
        for r in writes:
            self.res[r] = [token, []]
        wl = sorted(waits.items())
        self.n_waits += len(wl)
        self.plan[eng].append([wl, fn, myseq, dma, tag])
        self.seq[eng] += 1

    def finalize(self):
        targets = {e: set() for e in ENGINES}
        for e in ENGINES:
            for wl, fn, myseq, dma, tag in self.plan[e]:
                if dma:
                    targets[e].add(myseq)
                for peng, pseq in wl:
                    targets[peng].add(pseq)
        self.inc_at = {e: targets[e] for e in ENGINES}
        self.amt = {}
        self.val = {}
        for e in ENGINES:
            tot = 0
            for wl, fn, myseq, dma, tag in self.plan[e]:
                if myseq in self.inc_at[e]:
                    a = 16 if dma else 1
                    self.amt[(e, myseq)] = a
                    tot += a
                self.val[(e, myseq)] = tot
        for e in ENGINES:
            seen = {}
            n_kept = 0
            for entry in self.plan[e]:
                wl = []
                for p, q in entry[0]:
                    v = self.val[(p, q)]
                    if seen.get(p, -1) < v:
                        seen[p] = v
                        wl.append((p, v))
                n_kept += len(wl)
                entry[0] = wl
        self.n_waits = sum(len(en[0]) for ee_ in ENGINES for en in self.plan[ee_])

    def check(self):
        idx = {e: 0 for e in ENGINES}
        counts = {e: 0 for e in ENGINES}
        total = sum(len(v) for v in self.plan.values())
        done = 0
        progress = True
        while progress:
            progress = False
            for e in ENGINES:
                while idx[e] < len(self.plan[e]):
                    wl, fn, myseq, dma, tag = self.plan[e][idx[e]]
                    if all(counts[p] >= v for p, v in wl):
                        if myseq in self.inc_at[e]:
                            counts[e] += self.amt[(e, myseq)]
                        idx[e] += 1
                        done += 1
                        progress = True
                    else:
                        break
        if done != total:
            msgs = []
            for e in ENGINES:
                if idx[e] < len(self.plan[e]):
                    wl, fn, myseq, dma, tag = self.plan[e][idx[e]]
                    blocked = [(p, v, counts[p]) for p, v in wl if counts[p] < v]
                    msgs.append(f"{e}@{idx[e]} tag={tag} blocked={blocked}")
            raise RuntimeError("DEADLOCK: " + " | ".join(msgs))
        return total

    def replay(self, block, sems):
        for e in ENGINES:
            entries = self.plan[e]
            inc_at = self.inc_at[e]
            amts = self.amt
            mysem = sems[e]

            def body(eng, entries=entries, e=e, inc_at=inc_at, mysem=mysem):
                for wl, fn, myseq, dma, tag in entries:
                    for peng, v in wl:
                        eng.wait_ge(sems[peng], v)
                    instr = fn(eng)
                    if myseq in inc_at:
                        if instr is None:
                            instr = eng.nop()
                        instr.then_inc(mysem, amts[(e, myseq)])
            getattr(block, e)(body)


def build_program():
    nc = bass.Bass("TRN2", target_bir_lowering=False)

    xT = nc.dram_tensor("xT", [D, T], F32, kind="ExternalInput").ap()
    wqkv_h = nc.dram_tensor("wqkv", [D, 3 * D], F32, kind="ExternalInput").ap()
    bqkv_h = nc.dram_tensor("bqkv", [1, 3 * D], F32, kind="ExternalInput").ap()
    wout_h = nc.dram_tensor("wout", [D, D], F32, kind="ExternalInput").ap()
    boutc_h = nc.dram_tensor("boutc", [128, 8], F32, kind="ExternalInput").ap()
    gwT_h = nc.dram_tensor("gwT", [D, EXP], F32, kind="ExternalInput").ap()
    wexpT_h = nc.dram_tensor("wexpT", [EXP, D, D], F32, kind="ExternalInput").ap()
    outT_h = nc.dram_tensor("outT", [D, T], F32, kind="ExternalOutput").ap()

    attn_sc = nc.dram_tensor("attn_sc", [D, T], BF16, kind="Internal").ap()
    x1bf_sc = nc.dram_tensor("x1bf_sc", [D, T], BF16, kind="Internal").ap()
    base_sc = nc.dram_tensor("base_sc", [D, T], F32, kind="Internal").ap()
    mask_sc = nc.dram_tensor("mask_sc", [2, EXP, T], BF16, kind="Internal").ap()

    L2 = {}

    def a2(name, off, n):
        L2[name] = (off, n)
        assert off + n <= 80960, (name, off + n)

    a2("xbf", 0, 32768)
    a2("qT", 32768, 4096)
    a2("kT", 36864, 4096)
    a2("vext", 40960, 4160)
    a2("astage", 45120, 2048)
    a2("wst", 47168, 3072)
    a2("rows2", 50240, 12288)
    a2("bqkvr", 62528, 3072)
    a2("rowt2", 65600, 2560)
    a2("consts2", 68160, 448)
    a2("pp", 68608, 4096)
    a2("xsq", 68608, 1024)
    a2("woutw", 32768, 8192)
    a2("attnc", 40960, 8192)
    a2("x1bfc", 50240, 4096)
    a2("x1loc", 54336, 4096)
    a2("ghl", 58432, 640)
    a2("maskst", 59072, 512)
    a2("maskr8", 59584, 512)
    a2("maskm8", 60096, 512)
    a2("maskst2", 60608, 512)
    a2("wexp", 0, 16384)
    a2("x1half", 16384, 16384)
    a2("h2m", 32768, 8192)
    a2("acc", 40960, 16384)
    a2("bsb", 57344, 1024)
    a2("wse", 58368, 2048)
    a2("mrA", 60416, 2048)
    a2("mrB", 65600, 2048)
    a2("mrA2", 62464, 2048)
    a2("mrB2", 68608, 2048)
    a2("qT2", 72768, 4096)
    a2("kT2", 76864, 4096)
    E2 = 80960

    L4 = {}

    def a4(name, off, n):
        L4[name] = (off, n)
        assert off + n <= 12624, (name, off + n)

    a4("rb1", 0, 4096)
    a4("xf32", 4096, 2048)
    a4("scr4", 6144, 1024)
    a4("r1cols", 7168, 64)
    a4("rowp", 7232, 3584)
    a4("smalls", 10816, 320)
    a4("ident", 11136, 128)
    a4("iorow", 11264, 128)
    a4("sctm", 11392, 32)
    a4("wcol", 11424, 8)
    a4("sctm2", 11432, 24)
    a4("scot", 11456, 512)
    a4("scglo", 11968, 512)
    a4("sc2", 12480, 144)
    a4("x1c", 0, 4096)
    a4("outc", 0, 4096)
    a4("based", 4096, 4096)
    E4 = 12624

    assert 2 * E2 + 4 * E4 <= 212500, (E2, E4, 2 * E2 + 4 * E4)

    AR2 = nc.sbuf_tensor("AR2", [128, E2], BF16).__enter__()
    AR4 = nc.sbuf_tensor("AR4", [128, E4], F32).__enter__()
    U32B = nc.sbuf_tensor("U32B", [128, 8], U32).__enter__()
    PS = nc.psum_tensor("PS", [128, 4096], F32).__enter__()

    def r2(name, p0, p1, lo, n):
        o, sz = L2[name]
        assert lo + n <= sz, (name, lo, n, sz)
        return AR2[p0:p1, o + lo: o + lo + n]

    def r4(name, p0, p1, lo, n):
        o, sz = L4[name]
        assert lo + n <= sz, (name, lo, n, sz)
        return AR4[p0:p1, o + lo: o + lo + n]

    def g2(name, lo, n):
        o = L2[name][0] + lo
        return [f"A2.{g}" for g in range(o // 512, (o + n - 1) // 512 + 1)]

    def g4(name, lo, n):
        o = L4[name][0] + lo
        return [f"A4.{g}" for g in range(o // 512, (o + n - 1) // 512 + 1)]

    def bank(i, p0=0, p1=128, lo=0, n=512):
        return PS[p0:p1, i * 512 + lo: i * 512 + lo + n]

    def gb(i):
        return [f"PS.{i}"]

    S = Sched()

    def mm(ps, gps, lhsT, glh, rhs, grh, start, stop, tag=""):
        S.emit("tensor", lambda e, p=ps, l=lhsT, r=rhs, st=start, sp=stop:
               e.matmul(p, l, r, start=st, stop=sp),
               reads=list(glh) + list(grh), writes=list(gps), tag=tag)

    def tr(ps, gps, in_, gin, ident, gid, tag=""):
        S.emit("tensor", lambda e, p=ps, i=in_, d=ident: e.transpose(p, i, d),
               reads=list(gin) + list(gid), writes=list(gps), tag=tag)

    def dve(fn, reads, writes, tag=""):
        S.emit("vector", fn, reads=reads, writes=writes, tag=tag)

    def pve(fn, reads, writes, tag=""):
        S.emit("gpsimd", fn, reads=reads, writes=writes, tag=tag)

    def act(fn, reads, writes, tag=""):
        S.emit("scalar", fn, reads=reads, writes=writes, tag=tag)

    def dma(q, out, in_, reads, writes, tag=""):
        S.emit(q, lambda e, o=out, i=in_: e.dma_start(out=o, in_=i),
               reads=reads, writes=writes, dma=True, tag=tag)

    onescol_bf = r2("consts2", 0, 128, 0, 1)
    onesrow_bf = r2("consts2", 0, 1, 1, 128)
    onesrow64_bf = r2("consts2", 64, 65, 129, 128)
    gc_ones = g2("consts2", 0, 512)
    dve(lambda e: e.memset(r2("consts2", 0, 128, 0, 1), 1.0), [], gc_ones, "c0")
    dve(lambda e: e.memset(r2("consts2", 0, 1, 1, 128), 1.0), [], gc_ones, "c1")
    dve(lambda e: e.memset(r2("consts2", 64, 65, 129, 128), 1.0), [], gc_ones, "c1b")
    dve(lambda e: e.memset(r4("r1cols", 0, 128, 32, 1), C_SHIFT),
        [], g4("r1cols", 32, 1), "c1c")
    ones1f = r4("smalls", 0, 1, 128, 128)
    onescol_f = r4("smalls", 0, 128, 256, 1)
    bexp = r4("smalls", 0, 128, 257, 1)
    beps = r4("smalls", 0, 1, 258, 1)
    bzero = r4("smalls", 0, 1, 259, 1)
    ident11 = r4("smalls", 0, 1, 260, 1)
    iocol = r4("smalls", 0, 128, 261, 1)
    negG = r4("smalls", 0, 8, 262, 1)
    ecol8 = r4("smalls", 0, 8, 261, 1)
    gsm = g4("smalls", 0, 512)
    dve(lambda e: e.memset(r4("smalls", 0, 1, 128, 128), 1.0), [], gsm, "c2")
    dve(lambda e: e.memset(r4("smalls", 0, 128, 256, 1), 1.0), [], gsm, "c3")
    dve(lambda e: e.memset(r4("smalls", 0, 128, 257, 1), C_SHIFT), [], gsm, "c4")
    dve(lambda e: e.memset(r4("smalls", 0, 1, 258, 1), 1e-5), [], gsm, "c5")
    dve(lambda e: e.memset(r4("smalls", 0, 1, 259, 1), 0.0), [], gsm, "c6")
    dve(lambda e: e.memset(r4("smalls", 0, 128, 264, 1), 0.0), [], gsm, "c6b")
    dve(lambda e: e.memset(r4("smalls", 0, 1, 260, 1), 1.0), [], gsm, "c7")
    iorow = r4("iorow", 0, 1, 0, 128)
    for i in range(128):
        dve(lambda e, i=i: e.memset(r4("iorow", 0, 1, i, 1), float(i)),
            [], g4("iorow", i, 1), f"io{i}")
    for ei in range(8):
        dve(lambda e, ei=ei: e.memset(r4("wcol", 0, 128, ei, 1), float(8 - ei)),
            [], g4("wcol", ei, 1), f"wc{ei}")
    tr(bank(7, 0, 128, 0, 1), gb(7), iorow, g4("iorow", 0, 128), ident11, gsm, "ioct")
    dve(lambda e: e.tensor_copy(r4("smalls", 0, 128, 261, 1), bank(7, 0, 128, 0, 1)),
        gb(7), gsm, "iocp")
    mm(bank(6, 0, 128, 0, 128), gb(6), ones1f, gsm, iorow, g4("iorow", 0, 128), True, True, "iobc")
    pve(lambda e: e.tensor_copy(r4("smalls", 0, 128, 263, 1),
                                r4("smalls", 0, 128, 261, 1)), gsm, gsm, "ioc2")
    ident128 = r4("ident", 0, 128, 0, 128)
    dve(lambda e: e.tensor_scalar(r4("ident", 0, 128, 0, 128), bank(6, 0, 128, 0, 128),
                                  r4("smalls", 0, 128, 263, 1), None, AluOpType.is_equal),
        gb(6) + gsm, g4("ident", 0, 128), "ident")

    dma("sync", r4("smalls", 0, 128, 0, 64).rearrange("p (dt e) -> p dt e", dt=8),
        gwT_h.rearrange("(dt p) e -> p dt e", p=128), ["H.gw"], gsm, "gw_in")
    dma("sync", r4("smalls", 0, 128, 64, 8), boutc_h, ["H.boutc"], gsm, "bout_in")
    dma("gpsimd", r2("bqkvr", 0, 1, 0, 3072), bqkv_h, ["H.bqkv"],
        g2("bqkvr", 0, 3072), "bqkv_in")
    for dt in range(8):
        mm(bank(5, 0, 1, 0, 8), gb(5), onescol_f, gsm,
           r4("smalls", 0, 128, dt * 8, 8), gsm, dt == 0, dt == 7, "Gsum")
    dve(lambda e: e.tensor_scalar(r4("rowp", 0, 1, 0, 8), bank(5, 0, 1, 0, 8), -1.0,
                                  None, AluOpType.mult), gb(5), g4("rowp", 0, 8), "negGr")
    tr(bank(5, 0, 8, 512, 1), gb(5), r4("rowp", 0, 1, 0, 8), g4("rowp", 0, 8),
       ident11, gsm, "negGt")
    dve(lambda e: e.tensor_copy(r4("smalls", 0, 8, 262, 1), bank(5, 0, 8, 512, 1)),
        gb(5), gsm, "negGc")

    def rowslot(i, n=512, p0=0, p1=1):
        return r4("rowp", p0, p1, i * 512, n)

    def rowslot_g(i, n=512):
        return g4("rowp", i * 512, n)

    R_M, R_E2, R_VAR, R_LN, R_R, R_A, R_B = 0, 1, 2, 3, 4, 5, 6

    def emit_phaseA_tc(tc):
        for dt in range(8):
            if dt % 4 == 0:
                half = dt // 4
                dma("sync", r4("xf32", 0, 128, 0, 2048)
                    .rearrange("p (dt t) -> p dt t", dt=4),
                    xT[half * 512:(half + 1) * 512, tc * 512:(tc + 1) * 512]
                    .rearrange("(dt p) t -> p dt t", p=128),
                    ["H.xT"], g4("xf32", 0, 2048), f"xin{tc}_{half}")
            xf = r4("xf32", 0, 128, (dt % 4) * 512, 512)
            gxf = g4("xf32", (dt % 4) * 512, 512)
            xb = r2("xbf", 0, 128, dt * 4096 + tc * 512, 512)
            gxb = g2("xbf", dt * 4096 + tc * 512, 512)
            dve(lambda e, o=xb, i=xf: e.tensor_copy(o, i), gxf, gxb, f"cast{tc}_{dt}")
            sq = r2("xsq", 0, 128, (dt % 2) * 512, 512)
            gsq = g2("xsq", (dt % 2) * 512, 512)
            dve(lambda e, o=sq, i=xb: e.tensor_tensor(o, i, i, AluOpType.mult),
                gxb, gsq, f"sqA{tc}_{dt}")
            mm(bank(0, 0, 1), gb(0), onescol_bf, gc_ones, xb, gxb,
               dt == 0, dt == 7, f"s1_{tc}_{dt}")
            mm(bank(1, 0, 1), gb(1), onescol_bf, gc_ones, sq, gsq,
               dt == 0, dt == 7, f"s2_{tc}_{dt}")
        dve(lambda e: e.tensor_scalar(rowslot(R_M), bank(0, 0, 1), 1.0 / D, None,
                                      AluOpType.mult), gb(0), rowslot_g(R_M), f"mA{tc}")
        dve(lambda e: e.tensor_scalar(rowslot(R_E2), bank(1, 0, 1), 1.0 / D, None,
                                      AluOpType.mult), gb(1), rowslot_g(R_E2), f"eA{tc}")
        dve(lambda e: e.tensor_tensor(rowslot(R_VAR), rowslot(R_M), rowslot(R_M),
                                      AluOpType.mult),
            rowslot_g(R_M), rowslot_g(R_VAR), f"m2A{tc}")
        dve(lambda e: e.tensor_tensor(rowslot(R_VAR), rowslot(R_E2), rowslot(R_VAR),
                                      AluOpType.subtract),
            rowslot_g(R_E2) + rowslot_g(R_VAR), rowslot_g(R_VAR), f"vA{tc}")
        act(lambda e: e.activation(rowslot(R_LN), rowslot(R_VAR), AF.Ln,
                                   bias=beps, scale=1.0),
            rowslot_g(R_VAR) + gsm, rowslot_g(R_LN), f"lnA{tc}")
        act(lambda e: e.activation(rowslot(R_R), rowslot(R_LN), AF.Exp,
                                   bias=bzero, scale=-0.5),
            rowslot_g(R_LN) + gsm, rowslot_g(R_R), f"rA{tc}")
        act(lambda e, tc=tc: e.activation(r2("rows2", 0, 1, 4096 + tc * 512, 512),
                                          rowslot(R_LN), AF.Exp, bias=bzero, scale=0.5),
            rowslot_g(R_LN) + gsm, g2("rows2", 4096 + tc * 512, 512), f"ivA{tc}")
        dve(lambda e, tc=tc: e.tensor_scalar(r2("rows2", 0, 1, 8192 + tc * 512, 512),
                                             rowslot(R_M), -1.0, None, AluOpType.mult),
            rowslot_g(R_M), g2("rows2", 8192 + tc * 512, 512), f"nmA{tc}")
        dve(lambda e, tc=tc: e.scalar_tensor_tensor(r2("rows2", 0, 1, tc * 512, 512),
                                                    rowslot(R_R), -1.0, rowslot(R_M),
                                                    AluOpType.mult, AluOpType.mult),
            rowslot_g(R_R) + rowslot_g(R_M), g2("rows2", tc * 512, 512), f"nrmA{tc}")
        rhi = r2("rowt2", 0, 1, 0, 512)
        rlo = r2("rowt2", 0, 1, 512, 512)
        dve(lambda e: e.tensor_copy(rhi, rowslot(R_R)), rowslot_g(R_R),
            g2("rowt2", 0, 512), f"rhiA{tc}")
        dve(lambda e: e.tensor_tensor(rlo, rowslot(R_R), rhi, AluOpType.subtract),
            rowslot_g(R_R) + g2("rowt2", 0, 512), g2("rowt2", 512, 512), f"rloA{tc}")
        mm(bank(2), gb(2), onesrow_bf, gc_ones, rhi, g2("rowt2", 0, 512),
           True, False, f"rbA{tc}")
        mm(bank(2), gb(2), onesrow_bf, gc_ones, rlo, g2("rowt2", 512, 512),
           False, True, f"rbB{tc}")
        dve(lambda e, tc=tc: e.tensor_copy(r4("rb1", 0, 128, tc * 512, 512), bank(2)),
            gb(2), g4("rb1", tc * 512, 512), f"rbC{tc}")
        for sub in range(4):
            tr(bank(3, 0, 128, sub, 1), gb(3),
               r4("rowp", 0, 1, R_R * 512 + sub * 128, 128), rowslot_g(R_R),
               ident11, gsm, f"rcT{tc}_{sub}")
        dve(lambda e, tc=tc: e.tensor_copy(r4("r1cols", 0, 128, tc * 4, 4),
                                           bank(3, 0, 128, 0, 4)),
            gb(3), g4("r1cols", tc * 4, 4), f"rcC{tc}")

    wq = r2("wst", 0, 128, 0, 1024)
    wk = r2("wst", 0, 128, 1024, 1024)
    wv = r2("wst", 0, 128, 2048, 1024)
    gwq = g2("wst", 0, 1024)
    gwk = g2("wst", 1024, 1024)
    gwv = g2("wst", 2048, 1024)
    wsq_r = r2("rowt2", 0, 1, 1024, 128)
    wsk_r = r2("rowt2", 0, 1, 1152, 128)
    wsv_r = r2("rowt2", 0, 1, 1280, 128)
    gws0 = g2("rowt2", 1024, 128)
    gws1 = g2("rowt2", 1152, 128)
    gws2 = g2("rowt2", 1280, 128)

    def proj_thunks(pn):
        th = []
        qdst = "qT" if pn % 2 == 0 else "qT2"
        kdst = "kT" if pn % 2 == 0 else "kT2"
        for (w, gw_, col0, nm) in ((wq, gwq, 128 * pn, "q"),
                                   (wk, gwk, D + 128 * pn, "k"),
                                   (wv, gwv, 2 * D + 128 * pn, "v")):
            th.append(lambda w=w, gw_=gw_, col0=col0, nm=nm: dma(
                "gpsimd", w.rearrange("p (dt o) -> p dt o", dt=8),
                wqkv_h[:, col0:col0 + 128].rearrange("(dt p2) o -> p2 dt o", p2=128),
                ["H.wqkv"], gw_, f"w{nm}i{pn}"))
        for si, (w, gw_) in enumerate(((wq, gwq), (wk, gwk), (wv, gwv))):
            for dt in range(8):
                th.append(lambda w=w, gw_=gw_, si=si, dt=dt: mm(
                    bank(7, 0, 1, si * 128, 128), gb(7), onescol_bf, gc_ones,
                    w[:, dt * 128:(dt + 1) * 128], gw_, dt == 0, dt == 7,
                    f"ws{pn}{si}"))
            th.append(lambda si=si: dve(
                lambda e, si=si: e.tensor_copy(
                    r2("rowt2", 0, 1, 1024 + si * 128, 128),
                    bank(7, 0, 1, si * 128, 128)),
                gb(7), g2("rowt2", 1024 + si * 128, 128), f"wsc{pn}{si}"))
        bq_r = r2("bqkvr", 0, 1, 128 * pn, 128)
        bk_r = r2("bqkvr", 0, 1, D + 128 * pn, 128)
        gbq = g2("bqkvr", 128 * pn, 128)
        gbk = g2("bqkvr", D + 128 * pn, 128)
        for tc in range(NT):
            for (dst, w, gw_, wsr, gwsr, br, gbr) in (
                    (qdst, wq, gwq, wsq_r, gws0, bq_r, gbq),
                    (kdst, wk, gwk, wsk_r, gws1, bk_r, gbk)):
                for dt in range(8):
                    th.append(lambda dst=dst, w=w, gw_=gw_, tc=tc, dt=dt: mm(
                        bank(7), gb(7), w[:, dt * 128:(dt + 1) * 128], gw_,
                        r2("xbf", 0, 128, dt * 4096 + tc * 512, 512),
                        g2("xbf", dt * 4096 + tc * 512, 512),
                        dt == 0, False, f"{dst}m{pn}_{tc}_{dt}"))
                th.append(lambda dst=dst, wsr=wsr, gwsr=gwsr, tc=tc: mm(
                    bank(7), gb(7), wsr, gwsr,
                    r2("rows2", 0, 1, tc * 512, 512), g2("rows2", tc * 512, 512),
                    False, False, f"{dst}c1{pn}_{tc}"))
                th.append(lambda dst=dst, br=br, gbr=gbr, tc=tc: mm(
                    bank(7), gb(7), br, gbr,
                    r2("rows2", 0, 1, 4096 + tc * 512, 512),
                    g2("rows2", 4096 + tc * 512, 512), False, True,
                    f"{dst}c2{pn}_{tc}"))
                th.append(lambda dst=dst, tc=tc: dve(
                    lambda e, dst=dst, tc=tc: e.tensor_tensor(
                        r2(dst, 0, 128, tc * 512, 512), bank(7),
                        r4("rb1", 0, 128, tc * 512, 512), AluOpType.mult),
                    gb(7) + g4("rb1", tc * 512, 512), g2(dst, tc * 512, 512),
                    f"{dst}e{pn}_{tc}"))
        return th

    th0 = proj_thunks(0)
    n0 = len(th0)
    for tc in range(NT):
        emit_phaseA_tc(tc)
        want0 = min(n0, 30 + 22 * (tc + 1))
        while n0 - len(th0) < want0:
            th0.pop(0)()
    while th0:
        th0.pop(0)()
    norm_bg = []
    for p in range(NPAIR):
        qsrc = "qT" if p % 2 == 0 else "qT2"
        ksrc = "kT" if p % 2 == 0 else "kT2"
        if p == 0:
            dve(lambda e: e.memset(
                r2("vext", 0, 128, 0, 4160)
                .rearrange("p (kt c) -> p kt c", kt=32)[:, :, 64:130:65], 1.0),
                [], g2("vext", 0, 4160), "vones")
        bv_r = r2("bqkvr", 0, 1, 2 * D + 128 * p, 128)
        gbv = g2("bqkvr", 2 * D + 128 * p, 128)

        def v_group(kt, vb, p=p, bv_r=bv_r, gbv=gbv):
            for dt in range(8):
                mm(bank(vb, 0, 128, 0, 128), gb(vb),
                   r2("xbf", 0, 128, dt * 4096 + kt * 128, 128),
                   g2("xbf", dt * 4096 + kt * 128, 128),
                   wv[:, dt * 128:(dt + 1) * 128], gwv, dt == 0, False, f"vm{p}_{kt}")
            mm(bank(vb, 0, 128, 0, 128), gb(vb),
               r2("rows2", 0, 1, 8192 + kt * 128, 128),
               g2("rows2", 8192 + kt * 128, 128), wsv_r, gws2, False, False,
               f"vc1{p}_{kt}")
            mm(bank(vb, 0, 128, 0, 128), gb(vb),
               r2("rows2", 0, 1, 4096 + kt * 128, 128),
               g2("rows2", 4096 + kt * 128, 128), bv_r, gbv, False, True, f"vc2{p}_{kt}")
            for hh in range(2):
                dve(lambda e, kt=kt, hh=hh, vb=vb: e.tensor_scalar(
                        r2("vext", 0, 128, kt * 130 + 65 * hh, 64),
                        bank(vb, 0, 128, hh * 64, 64),
                        r4("r1cols", 0, 128, kt, 1), None, AluOpType.mult),
                    gb(vb) + g4("r1cols", kt, 1), g2("vext", kt * 130 + 65 * hh, 64),
                    f"ve{p}_{kt}_{hh}")

        if p == 0:
            for kt in range(32):
                v_group(kt, 5 + kt % 2)

        bg = proj_thunks(p + 1) if p < NPAIR - 1 else []
        nbg = len(bg)
        bgi = 0
        for qc in range(NT):
            par = qc % 2

            def emit_S(kt, qc=qc, qsrc=qsrc, ksrc=ksrc):
                for hh in range(2):
                    s_b = 2 + 2 * (kt % 2) + hh
                    mm(bank(s_b), gb(s_b),
                       r2(ksrc, 64 * hh, 64 * hh + 64, kt * 128, 128),
                       g2(ksrc, kt * 128, 128),
                       r2(qsrc, 64 * hh, 64 * hh + 64, qc * 512, 512),
                       g2(qsrc, qc * 512, 512), True, True, f"S{p}{qc}_{kt}_{hh}")

            def emit_E(kt, qc=qc):
                pb = 2 + 2 * (kt % 2)
                pp = r2("pp", 0, 128, (kt % 4) * 1024, 1024)
                gpp = g2("pp", (kt % 4) * 1024, 1024)
                act(lambda e, o=pp, b=pb: e.activation(
                        o, bank(b, 0, 128, 0, 1024), AF.Exp,
                        bias=r4("r1cols", 0, 128, 32, 1), scale=0.125),
                    gb(pb) + gb(pb + 1) + g4("r1cols", 32, 1), gpp, f"E{p}{qc}_{kt}")

            def emit_A(kt, qc=qc):
                for hh in range(2):
                    pp = r2("pp", 0, 128, (kt % 4) * 1024 + 512 * hh, 512)
                    gpp = g2("pp", (kt % 4) * 1024 + 512 * hh, 512)
                    mm(bank(hh, 0, 65), gb(hh),
                       r2("vext", 0, 128, kt * 130 + 65 * hh, 65),
                       g2("vext", kt * 130 + 65 * hh, 65),
                       pp, gpp, kt == 0, kt == 31, f"A{p}{qc}_{kt}_{hh}")

            for kt in range(32):
                if qc == 0 and p > 0:
                    v_group(kt, 6 + kt % 2)
                emit_S(kt)
                if kt >= 2:
                    emit_A(kt - 2)
                emit_E(kt)
                if norm_bg and kt >= 10:
                    norm_bg.pop(0)()
                kv = min(max(kt - 4, 0), 22)
                if p > 0:
                    virt = -1 if qc == 0 else (qc - 1) * 23 + kv
                    want = 0 if virt < 0 else (nbg * (virt + 1)) // (7 * 23)
                else:
                    virt = qc * 23 + kv
                    want = (nbg * (virt + 1)) // (8 * 23)
                if 4 <= kt <= 26:
                    while bgi < want:
                        bg[bgi]()
                        bgi += 1
            emit_A(30)
            emit_A(31)
            rowsets = (
                (("rowt2", 1536), ("rowt2", 2048)),
                (("astage", 0), ("astage", 512)),
                (("astage", 1024), ("astage", 1536)),
                (("rowt2", 1536), ("rowt2", 2048)),
            )
            for hh in range(2):
                cidx = hh + 2 * par
                dve(lambda e, hh=hh: e.tensor_copy(
                        r4("scr4", 0, 65, hh * 512, 512), bank(hh, 0, 65)),
                    gb(hh), g4("scr4", hh * 512, 512), f"avc{p}{qc}_{hh}")
                cT = "scot" if hh == 0 else "scglo"
                rslot = R_A if hh == 0 else R_B
                rec = rowslot(rslot, p0=64, p1=65)
                grec = rowslot_g(rslot)
                dve(lambda e, rec=rec, hh=hh: e.reciprocal(
                        rec, r4("scr4", 64, 65, hh * 512, 512)),
                    g4("scr4", hh * 512, 512), grec, f"rc{p}{qc}_{hh}")
                (nhi, ohi), (nlo, olo) = rowsets[cidx]
                rchi = r2(nhi, 64, 65, ohi, 512)
                ghi = g2(nhi, ohi, 512)
                rclo = r2(nlo, 64, 65, olo, 512)
                glo = g2(nlo, olo, 512)
                dve(lambda e, rec=rec, rchi=rchi: e.tensor_copy(rchi, rec), grec,
                    ghi, f"rh{p}{qc}_{hh}")

                def mk1(rchi=rchi, ghi=ghi, cT=cT, p=p, qc=qc, hh=hh, par=par):
                    def t():
                        mm(bank(6, 0, 64), gb(6), onesrow64_bf[0:1, 0:64], gc_ones,
                           rchi, ghi, True, True, f"bA{p}{qc}_{hh}")
                        dve(lambda e: e.tensor_copy(r4(cT, 0, 64, 0, 512),
                                                    bank(6, 0, 64)),
                            gb(6), g4(cT, 0, 512), f"bC{p}{qc}_{hh}")
                        stg = r2("astage", 0, 64, hh * 1024 + par * 512, 512)
                        gstg = g2("astage", hh * 1024 + par * 512, 512)
                        dve(lambda e, stg=stg: e.tensor_tensor(
                                stg, r4("scr4", 0, 64, hh * 512, 512),
                                r4(cT, 0, 64, 0, 512), AluOpType.mult),
                            g4("scr4", hh * 512, 512) + g4(cT, 0, 512), gstg,
                            f"ae{p}{qc}_{hh}")
                        dma("sync",
                            attn_sc[128 * p + 64 * hh:128 * p + 64 * hh + 64,
                                    qc * 512:(qc + 1) * 512],
                            stg, gstg, [f"attn.{p}_{hh}"], f"ao{p}{qc}_{hh}")
                    return t

                norm_bg.append(mk1())
        while bgi < nbg:
            bg[bgi]()
            bgi += 1
    while norm_bg:
        norm_bg.pop(0)()

    dma("gpsimd", r2("woutw", 0, 128, 0, 8192).rearrange("p (dt o) -> p dt o", dt=8),
        wout_h.rearrange("(dt p) o -> p dt o", p=128), ["H.wout"],
        g2("woutw", 0, 8192), "wouti")
    dve(lambda e: e.memset(r2("ghl", 0, 128, 0, 640), 0.0), [], g2("ghl", 0, 640), "gh0")
    for dt in range(8):
        dve(lambda e, dt=dt: e.tensor_copy(r2("ghl", 0, 128, dt * 40, 8),
                                           r4("smalls", 0, 128, dt * 8, 8)),
            gsm + g2("ghl", 0, 640), g2("ghl", 0, 640), f"gh{dt}")
        pve(lambda e, dt=dt: e.tensor_tensor(r2("ghl", 0, 128, dt * 40 + 32, 8),
                                             r4("smalls", 0, 128, dt * 8, 8),
                                             r2("ghl", 0, 128, dt * 40, 8),
                                             AluOpType.subtract),
            gsm + g2("ghl", 0, 640), g2("ghl", 0, 640), f"gl{dt}")

    for tc in range(NT):
        cb = tc % 2
        dma("gpsimd",
            r2("attnc", 0, 128, cb * 4096, 4096).rearrange("p (dt t) -> p dt t", dt=8),
            attn_sc[:, tc * 512:(tc + 1) * 512].rearrange("(dt p) t -> p dt t", p=128),
            [f"attn.{pp_}_{hh_}" for pp_ in range(8) for hh_ in range(2)],
            g2("attnc", cb * 4096, 4096), f"ai{tc}")
        for oc in range(8):
            wb = oc % 2
            if oc % 4 == 0:
                half = oc // 4
                dma("sync", r4("xf32", 0, 128, 0, 2048)
                    .rearrange("p (dt t) -> p dt t", dt=4),
                    xT[half * 512:(half + 1) * 512, tc * 512:(tc + 1) * 512]
                    .rearrange("(dt p) t -> p dt t", p=128),
                    ["H.xT"], g4("xf32", 0, 2048), f"x2i{tc}_{half}")
            for dt in range(8):
                mm(bank(wb), gb(wb),
                   r2("woutw", 0, 128, dt * 1024 + oc * 128, 128),
                   g2("woutw", dt * 1024 + oc * 128, 128),
                   r2("attnc", 0, 128, cb * 4096 + dt * 512, 512),
                   g2("attnc", cb * 4096 + dt * 512, 512),
                   dt == 0, dt == 7, f"wo{tc}_{oc}_{dt}")
            dve(lambda e, oc=oc, wb=wb: e.scalar_tensor_tensor(
                    r4("x1c", 0, 128, oc * 512, 512), bank(wb),
                    r4("smalls", 0, 128, 64 + oc, 1),
                    r4("xf32", 0, 128, (oc % 4) * 512, 512),
                    AluOpType.add, AluOpType.add),
                gb(wb) + gsm + g4("xf32", (oc % 4) * 512, 512),
                g4("x1c", oc * 512, 512), f"x1_{tc}_{oc}")
        for oc in range(8):
            dve(lambda e, oc=oc: e.tensor_copy(r2("x1bfc", 0, 128, oc * 512, 512),
                                               r4("x1c", 0, 128, oc * 512, 512)),
                g4("x1c", oc * 512, 512), g2("x1bfc", oc * 512, 512), f"xb_{tc}_{oc}")
            dve(lambda e, oc=oc: e.tensor_tensor(r2("x1loc", 0, 128, oc * 512, 512),
                                                 r4("x1c", 0, 128, oc * 512, 512),
                                                 r2("x1bfc", 0, 128, oc * 512, 512),
                                                 AluOpType.subtract),
                g4("x1c", oc * 512, 512) + g2("x1bfc", oc * 512, 512),
                g2("x1loc", oc * 512, 512), f"xl_{tc}_{oc}")
            sq = r2("xsq", 0, 128, (oc % 2) * 512, 512)
            gsq = g2("xsq", (oc % 2) * 512, 512)
            dve(lambda e, oc=oc, o=sq: e.tensor_tensor(
                    o, r2("x1bfc", 0, 128, oc * 512, 512),
                    r2("x1bfc", 0, 128, oc * 512, 512), AluOpType.mult),
                g2("x1bfc", oc * 512, 512), gsq, f"sqC{tc}_{oc}")
            mm(bank(2, 0, 1), gb(2), onescol_bf, gc_ones,
               r2("x1bfc", 0, 128, oc * 512, 512), g2("x1bfc", oc * 512, 512),
               oc == 0, oc == 7, f"t1_{tc}_{oc}")
            mm(bank(3, 0, 1), gb(3), onescol_bf, gc_ones, sq, gsq,
               oc == 0, oc == 7, f"t2_{tc}_{oc}")
        dma("scalar", x1bf_sc[:, tc * 512:(tc + 1) * 512]
            .rearrange("(oc p) t -> p oc t", p=128),
            r2("x1bfc", 0, 128, 0, 4096).rearrange("p (oc t) -> p oc t", oc=8),
            g2("x1bfc", 0, 4096), [f"x1bf.{tc}"], f"x1o{tc}")
        dve(lambda e: e.tensor_scalar(rowslot(R_M), bank(2, 0, 1), 1.0 / D, None,
                                      AluOpType.mult), gb(2), rowslot_g(R_M), f"mC{tc}")
        dve(lambda e: e.tensor_scalar(rowslot(R_E2), bank(3, 0, 1), 1.0 / D, None,
                                      AluOpType.mult), gb(3), rowslot_g(R_E2), f"eC{tc}")
        dve(lambda e: e.tensor_tensor(rowslot(R_VAR), rowslot(R_M), rowslot(R_M),
                                      AluOpType.mult),
            rowslot_g(R_M), rowslot_g(R_VAR), f"m2C{tc}")
        dve(lambda e: e.tensor_tensor(rowslot(R_VAR), rowslot(R_E2), rowslot(R_VAR),
                                      AluOpType.subtract),
            rowslot_g(R_E2) + rowslot_g(R_VAR), rowslot_g(R_VAR), f"vC{tc}")
        act(lambda e: e.activation(rowslot(R_LN), rowslot(R_VAR), AF.Ln,
                                   bias=beps, scale=1.0),
            rowslot_g(R_VAR) + gsm, rowslot_g(R_LN), f"lnC{tc}")
        act(lambda e: e.activation(rowslot(R_R), rowslot(R_LN), AF.Exp,
                                   bias=bzero, scale=-0.5),
            rowslot_g(R_LN) + gsm, rowslot_g(R_R), f"r2C{tc}")
        dve(lambda e: e.tensor_scalar(rowslot(R_A), rowslot(R_M), -1.0, None,
                                      AluOpType.mult),
            rowslot_g(R_M), rowslot_g(R_A), f"nmC{tc}")
        dve(lambda e: e.tensor_tensor(rowslot(R_B), rowslot(R_A), rowslot(R_R),
                                      AluOpType.mult),
            rowslot_g(R_A) + rowslot_g(R_R), rowslot_g(R_B), f"nmrC{tc}")
        dve(lambda e: e.tensor_scalar(rowslot(R_VAR), rowslot(R_R), 1.0, None,
                                      AluOpType.add),
            rowslot_g(R_R), rowslot_g(R_VAR), f"rpC{tc}")
        dve(lambda e: e.tensor_tensor(rowslot(R_E2), rowslot(R_M), rowslot(R_R),
                                      AluOpType.mult),
            rowslot_g(R_M) + rowslot_g(R_R), rowslot_g(R_E2), f"mrC{tc}")
        for (slot, bnk, dsto) in ((R_VAR, 4, 0), (R_E2, 5, 512)):
            hi = r2("rowt2", 0, 1, 1536, 512)
            lo = r2("rowt2", 0, 1, 2048, 512)
            dve(lambda e, slot=slot, hi=hi: e.tensor_copy(hi, rowslot(slot)),
                rowslot_g(slot), g2("rowt2", 1536, 512), f"hi{tc}_{bnk}")
            dve(lambda e, slot=slot, hi=hi, lo=lo: e.tensor_tensor(
                    lo, rowslot(slot), hi, AluOpType.subtract),
                rowslot_g(slot) + g2("rowt2", 1536, 512), g2("rowt2", 2048, 512),
                f"lo{tc}_{bnk}")
            mm(bank(bnk), gb(bnk), onesrow_bf, gc_ones, hi, g2("rowt2", 1536, 512),
               True, False, f"bb{tc}_{bnk}a")
            mm(bank(bnk), gb(bnk), onesrow_bf, gc_ones, lo, g2("rowt2", 2048, 512),
               False, True, f"bb{tc}_{bnk}b")
            dve(lambda e, bnk=bnk, dsto=dsto: e.tensor_copy(
                    r4("scr4", 0, 128, dsto, 512), bank(bnk)),
                gb(bnk), g4("scr4", dsto, 512), f"bc{tc}_{bnk}")
        for oc in range(8):
            dve(lambda e, oc=oc: e.tensor_tensor(r4("scot", 0, 128, 0, 512),
                                                 r4("x1c", 0, 128, oc * 512, 512),
                                                 r4("scr4", 0, 128, 0, 512),
                                                 AluOpType.mult),
                g4("x1c", oc * 512, 512) + g4("scr4", 0, 512), g4("scot", 0, 512),
                f"u{tc}_{oc}")
            dve(lambda e, oc=oc: e.tensor_tensor(r4("x1c", 0, 128, oc * 512, 512),
                                                 r4("scot", 0, 128, 0, 512),
                                                 r4("scr4", 0, 128, 512, 512),
                                                 AluOpType.subtract),
                g4("scot", 0, 512) + g4("scr4", 512, 512),
                g4("x1c", oc * 512, 512), f"bs{tc}_{oc}")
        dma("scalar", base_sc[:, tc * 512:(tc + 1) * 512]
            .rearrange("(oc p) t -> p oc t", p=128),
            r4("x1c", 0, 128, 0, 4096).rearrange("p (oc t) -> p oc t", oc=8),
            g4("x1c", 0, 4096), [f"base.{tc}"], f"bo{tc}")
        for dt in range(8):
            mm(bank(6, 0, 8), gb(6), r2("ghl", 0, 128, dt * 40, 8), g2("ghl", 0, 640),
               r2("x1bfc", 0, 128, dt * 512, 512), g2("x1bfc", dt * 512, 512),
               dt == 0, dt == 7, f"gA{tc}_{dt}")
        for dt in range(8):
            mm(bank(5, 0, 8), gb(5), r2("ghl", 0, 128, dt * 40 + 32, 8),
               g2("ghl", 0, 640), r2("x1bfc", 0, 128, dt * 512, 512),
               g2("x1bfc", dt * 512, 512), dt == 0, False, f"gB{tc}_{dt}")
        for dt in range(8):
            mm(bank(5, 0, 8), gb(5), r2("ghl", 0, 128, dt * 40, 8),
               g2("ghl", 0, 640), r2("x1loc", 0, 128, dt * 512, 512),
               g2("x1loc", dt * 512, 512), False, dt == 7, f"gC{tc}_{dt}")
        dve(lambda e: e.tensor_copy(r4("scglo", 0, 8, 0, 512), bank(5, 0, 8)),
            gb(5), g4("scglo", 0, 512), f"ge{tc}")
        dve(lambda e: e.tensor_tensor(r4("scot", 0, 8, 0, 512), bank(6, 0, 8),
                                      r4("scglo", 0, 8, 0, 512), AluOpType.add),
            gb(6) + g4("scglo", 0, 512), g4("scot", 0, 512), f"gs{tc}")
        mm(bank(6, 0, 8), gb(6), ones1f[0:1, 0:8], gsm, rowslot(R_M),
           rowslot_g(R_M), True, True, f"mb{tc}")
        dve(lambda e: e.scalar_tensor_tensor(r4("scglo", 0, 8, 0, 512), bank(6, 0, 8),
                                             negG, r4("scot", 0, 8, 0, 512),
                                             AluOpType.mult, AluOpType.add),
            gb(6) + gsm + g4("scot", 0, 512), g4("scglo", 0, 512), f"sc{tc}")
        for sub in range(4):
            sb_ = 7 if sub % 2 == 0 else 5
            o = (sub % 2) * 72

            def s4(lo, n):
                return r4("sc2", 0, 128, o + lo, n)

            def gs4(lo, n):
                return g4("sc2", o + lo, n)

            tr(bank(sb_, 0, 128, 0, 8), gb(sb_),
               r4("scglo", 0, 8, sub * 128, 128), g4("scglo", 0, 512),
               ident128[0:8, 0:8], g4("ident", 0, 128), f"st{tc}_{sub}")
            dve(lambda e, a=s4(0, 8), sb_=sb_: e.tensor_copy(
                    a, bank(sb_, 0, 128, 0, 8)),
                gb(sb_), gs4(0, 8), f"sp{tc}_{sub}")
            pve(lambda e, a=s4(8, 8), b=s4(0, 8): e.tensor_copy(a, b),
                gs4(0, 8), gs4(8, 8), f"hA{tc}_{sub}")
            dve(lambda e, a=s4(16, 1), b=s4(8, 8): e.tensor_reduce(
                    a, b, mybir.AxisListType.X, AluOpType.max),
                gs4(8, 8), gs4(16, 1), f"mx{tc}_{sub}")
            pve(lambda e, a=s4(24, 8), b=s4(8, 8), c=s4(16, 1): e.tensor_scalar(
                    a, b, c, None, AluOpType.is_equal),
                gs4(8, 9), gs4(24, 8), f"eq{tc}_{sub}")
            dve(lambda e, a=s4(32, 8), b=s4(24, 8): e.tensor_tensor(
                    a, b, r4("wcol", 0, 128, 0, 8), AluOpType.mult),
                gs4(24, 8) + g4("wcol", 0, 8), gs4(32, 8), f"wm{tc}_{sub}")
            pve(lambda e, a=s4(40, 8), b=s4(32, 8): e.tensor_copy(a, b),
                gs4(32, 8), gs4(40, 8), f"hB{tc}_{sub}")
            dve(lambda e, a=s4(17, 1), b=s4(40, 8): e.tensor_reduce(
                    a, b, mybir.AxisListType.X, AluOpType.max),
                gs4(40, 8), gs4(17, 1), f"m2{tc}_{sub}")
            pve(lambda e, a=s4(48, 8), b=s4(40, 8), c=s4(17, 1): e.tensor_scalar(
                    a, b, c, None, AluOpType.is_equal),
                gs4(40, 9), gs4(48, 8), f"oh{tc}_{sub}")
            tr(bank(sb_, 0, 8, 8, 128), gb(sb_), s4(48, 8),
               gs4(48, 8), ident128, g4("ident", 0, 128), f"ot{tc}_{sub}")
            dve(lambda e, sub=sub, sb_=sb_: e.tensor_copy(
                    r2("maskst", 0, 8, sub * 128, 128), bank(sb_, 0, 8, 8, 128)),
                gb(sb_), g2("maskst", sub * 128, 128), f"oc{tc}_{sub}")
        pve(lambda e: e.tensor_copy(r2("maskst2", 0, 8, 0, 512),
                                    r2("maskst", 0, 8, 0, 512)),
            g2("maskst", 0, 512), g2("maskst2", 0, 512), f"mh{tc}")
        mm(bank(6, 0, 8), gb(6), ones1f[0:1, 0:8], gsm, rowslot(R_R),
           rowslot_g(R_R), True, True, f"rb8{tc}")
        dve(lambda e: e.tensor_tensor(r2("maskr8", 0, 8, 0, 512),
                                      r2("maskst2", 0, 8, 0, 512), bank(6, 0, 8),
                                      AluOpType.mult),
            g2("maskst2", 0, 512) + gb(6), g2("maskr8", 0, 512), f"mr{tc}")
        mm(bank(6, 0, 8), gb(6), ones1f[0:1, 0:8], gsm, rowslot(R_B),
           rowslot_g(R_B), True, True, f"nb8{tc}")
        dve(lambda e: e.tensor_tensor(r2("maskm8", 0, 8, 0, 512),
                                      r2("maskst2", 0, 8, 0, 512), bank(6, 0, 8),
                                      AluOpType.mult),
            g2("maskst2", 0, 512) + gb(6), g2("maskm8", 0, 512), f"mm{tc}")
        dma("scalar", mask_sc[0, :, tc * 512:(tc + 1) * 512],
            r2("maskr8", 0, 8, 0, 512), g2("maskr8", 0, 512), [f"mskA.{tc}"], f"mo{tc}")
        dma("scalar", mask_sc[1, :, tc * 512:(tc + 1) * 512],
            r2("maskm8", 0, 8, 0, 512), g2("maskm8", 0, 512), [f"mskB.{tc}"], f"m2o{tc}")

    for sweep in range(2):
        th0 = sweep * 2048
        dma("sync", r2("x1half", 0, 128, 0, 16384)
            .rearrange("p (ht t) -> p ht t", ht=8),
            x1bf_sc[:, th0:th0 + 2048].rearrange("(ht p) t -> p ht t", p=128),
            [f"x1bf.{tc}" for tc in range(sweep * 4, sweep * 4 + 4)],
            g2("x1half", 0, 16384), f"xhi{sweep}")
        for ee in range(EXP):
            eb = ee % 2
            dma("gpsimd", r2("wexp", 0, 128, eb * 8192, 8192)
                .rearrange("p (ht f) -> p ht f", ht=8),
                wexpT_h[ee].rearrange("(ht p) f -> p ht f", p=128),
                ["H.wexpT"], g2("wexp", eb * 8192, 8192), f"we{sweep}_{ee}")
            for half in range(2):
                for ht in range(8):
                    mm(bank(6, 0, 1), gb(6), onescol_bf, gc_ones,
                       r2("wexp", 0, 128, eb * 8192 + ht * 1024 + half * 512, 512),
                       g2("wexp", eb * 8192 + ht * 1024 + half * 512, 512),
                       ht == 0, ht == 7, f"wsd{sweep}_{ee}_{half}")
                dve(lambda e, eb=eb, half=half: e.tensor_copy(
                        r2("wse", 0, 1, eb * 1024 + half * 512, 512), bank(6, 0, 1)),
                    gb(6), g2("wse", eb * 1024 + half * 512, 512),
                    f"wsv{sweep}_{ee}_{half}")
            mA = "mrA" if eb == 0 else "mrA2"
            mB = "mrB" if eb == 0 else "mrB2"
            dma("sync", r2(mA, 0, 1, 0, 2048),
                mask_sc[0, ee, th0:th0 + 2048][None, :],
                [f"mskA.{tc}" for tc in range(sweep * 4, sweep * 4 + 4)],
                g2(mA, 0, 2048), f"mai{sweep}_{ee}")
            dma("sync", r2(mB, 0, 1, 0, 2048),
                mask_sc[1, ee, th0:th0 + 2048][None, :],
                [f"mskB.{tc}" for tc in range(sweep * 4, sweep * 4 + 4)],
                g2(mB, 0, 2048), f"mbi{sweep}_{ee}")
            for tci in range(4):
                hb = tci % 2
                mm(bank(7), gb(7), onesrow_bf, gc_ones,
                   r2(mA, 0, 1, tci * 512, 512), g2(mA, tci * 512, 512),
                   True, True, f"Bb{sweep}_{ee}_{tci}")
                dve(lambda e, hb=hb: e.tensor_copy(r2("bsb", 0, 128, hb * 512, 512),
                                                   bank(7)),
                    gb(7), g2("bsb", hb * 512, 512), f"Bc{sweep}_{ee}_{tci}")
                for ht in range(8):
                    dve(lambda e, hb=hb, ht=ht, tci=tci: e.tensor_tensor(
                            r2("h2m", 0, 128, hb * 4096 + ht * 512, 512),
                            r2("x1half", 0, 128, ht * 2048 + tci * 512, 512),
                            r2("bsb", 0, 128, hb * 512, 512), AluOpType.mult),
                        g2("x1half", ht * 2048 + tci * 512, 512)
                        + g2("bsb", hb * 512, 512),
                        g2("h2m", hb * 4096 + ht * 512, 512), f"hm{sweep}_{ee}_{tci}_{ht}")
                for fc in range(8):
                    bnk = fc % 6
                    for ht in range(8):
                        mm(bank(bnk), gb(bnk),
                           r2("wexp", 0, 128, eb * 8192 + ht * 1024 + fc * 128, 128),
                           g2("wexp", eb * 8192 + ht * 1024 + fc * 128, 128),
                           r2("h2m", 0, 128, hb * 4096 + ht * 512, 512),
                           g2("h2m", hb * 4096 + ht * 512, 512),
                           ht == 0, False, f"ym{sweep}_{ee}_{tci}_{fc}_{ht}")
                    mm(bank(bnk), gb(bnk),
                       r2("wse", 0, 1, eb * 1024 + fc * 128, 128),
                       g2("wse", eb * 1024 + fc * 128, 128),
                       r2(mB, 0, 1, tci * 512, 512), g2(mB, tci * 512, 512),
                       False, True, f"yc{sweep}_{ee}_{tci}_{fc}")
                    if ee == 0:
                        dve(lambda e, bnk=bnk, fc=fc, tci=tci: e.tensor_copy(
                                r2("acc", 0, 128, fc * 2048 + tci * 512, 512),
                                bank(bnk)),
                            gb(bnk), g2("acc", fc * 2048 + tci * 512, 512),
                            f"ac{sweep}_{ee}_{tci}_{fc}")
                    else:
                        dve(lambda e, bnk=bnk, fc=fc, tci=tci: e.tensor_tensor(
                                r2("acc", 0, 128, fc * 2048 + tci * 512, 512),
                                bank(bnk),
                                r2("acc", 0, 128, fc * 2048 + tci * 512, 512),
                                AluOpType.add),
                            gb(bnk) + g2("acc", fc * 2048 + tci * 512, 512),
                            g2("acc", fc * 2048 + tci * 512, 512),
                            f"ac{sweep}_{ee}_{tci}_{fc}")
        for tci in range(4):
            tc = sweep * 4 + tci
            dma("sync", r4("based", 0, 128, 0, 4096)
                .rearrange("p (oc t) -> p oc t", oc=8),
                base_sc[:, tc * 512:(tc + 1) * 512]
                .rearrange("(oc p) t -> p oc t", p=128),
                [f"base.{tc}"], g4("based", 0, 4096), f"bi{tc}")
            for fc in range(8):
                dve(lambda e, fc=fc, tci=tci: e.tensor_tensor(
                        r4("outc", 0, 128, fc * 512, 512),
                        r2("acc", 0, 128, fc * 2048 + tci * 512, 512),
                        r4("based", 0, 128, fc * 512, 512), AluOpType.add),
                    g2("acc", fc * 2048 + tci * 512, 512) + g4("based", fc * 512, 512),
                    g4("outc", fc * 512, 512), f"oe{tc}_{fc}")
            dma("scalar", outT_h[:, tc * 512:(tc + 1) * 512]
                .rearrange("(oc p) t -> p oc t", p=128),
                r4("outc", 0, 128, 0, 4096).rearrange("p (oc t) -> p oc t", oc=8),
                g4("outc", 0, 4096), [f"out.{tc}"], f"ot{tc}")

    S.emit("scalar", lambda e: e.nop(),
           reads=[f"out.{tc}" for tc in range(NT)], tag="final")
    S.finalize()
    return nc, S


_CACHE = {}


def _get_nc():
    if "nc" not in _CACHE:
        nc, S = build_program()
        S.check()
        import contextlib
        stack = contextlib.ExitStack()
        sems = {e: stack.enter_context(nc.semaphore(f"sem_{e}")) for e in ENGINES}
        with nc.Block() as block:
            S.replay(block, sems)
        stack.close()
        _CACHE["nc"] = nc
    return _CACHE["nc"]


def kernel(L0_attn_data, L0_attn_W_qkv, L0_attn_b_qkv, L0_attn_W_out, L0_attn_b_out,
           L0_ffn_gate_w, L0_ffn_expert_w):
    x = np.asarray(L0_attn_data, dtype=np.float32)
    wqkv = np.ascontiguousarray(np.asarray(L0_attn_W_qkv, dtype=np.float32))
    bqkv = np.ascontiguousarray(np.asarray(L0_attn_b_qkv, dtype=np.float32)
                                .reshape(1, 3 * D))
    wout = np.ascontiguousarray(np.asarray(L0_attn_W_out, dtype=np.float32))
    boutc = np.ascontiguousarray(np.asarray(L0_attn_b_out, dtype=np.float32)
                                 .reshape(8, 128).T)
    gwT = np.ascontiguousarray(np.asarray(L0_ffn_gate_w, dtype=np.float32).T)
    wexpT = np.ascontiguousarray(np.asarray(L0_ffn_expert_w, dtype=np.float32)
                                 .transpose(0, 2, 1))
    nc = _get_nc()
    in_maps = []
    for b in range(8):
        in_maps.append({
            "xT": np.ascontiguousarray(x[b].T),
            "wqkv": wqkv, "bqkv": bqkv, "wout": wout, "boutc": boutc,
            "gwT": gwT, "wexpT": wexpT,
        })
    res = run_bass_kernel_spmd(nc, in_maps, core_ids=list(range(8)))
    return np.stack([np.ascontiguousarray(res.results[b]["outT"].T) for b in range(8)])


if __name__ == "__main__":
    nc, S = build_program()
    n = S.check()
    print("plan ok:", n, "instrs;", {e: len(S.plan[e]) for e in ENGINES},
          "waits:", S.n_waits, "incs:", sum(len(S.inc_at[e]) for e in ENGINES))


# version 48
# speedup vs baseline: 1.0539x; 1.0109x over previous
import numpy as np
import concourse.bass as bass
import concourse.mybir as mybir
from concourse.alu_op_type import AluOpType
from concourse.bass_utils import run_bass_kernel_spmd

F32 = mybir.dt.float32
BF16 = mybir.dt.bfloat16
U32 = mybir.dt.uint32
AF = mybir.ActivationFunctionType

D = 1024
T = 4096
NT = 8
NPAIR = 8
EXP = 8
C_SHIFT = -9.0

ENGINES = ("sync", "gpsimd", "tensor", "vector", "scalar")


class Sched:
    def __init__(self):
        self.plan = {e: [] for e in ENGINES}
        self.seq = {e: 0 for e in ENGINES}
        self.res = {}
        self.n_waits = 0

    def emit(self, eng, fn, reads=(), writes=(), dma=False, tag=""):
        myseq = self.seq[eng]
        token = (eng, myseq, dma)
        waits = {}

        def need(tok):
            peng, pseq, pdma = tok
            if peng == eng and not pdma:
                return
            waits[peng] = max(waits.get(peng, -1), pseq)

        for r in reads:
            st = self.res.get(r)
            if st is not None and st[0] is not None:
                need(st[0])
        for r in writes:
            st = self.res.get(r)
            if st is not None:
                if st[0] is not None:
                    need(st[0])
                for t in st[1]:
                    need(t)
        for r in reads:
            st = self.res.setdefault(r, [None, []])
            st[1].append(token)
        for r in writes:
            self.res[r] = [token, []]
        wl = sorted(waits.items())
        self.n_waits += len(wl)
        self.plan[eng].append([wl, fn, myseq, dma, tag])
        self.seq[eng] += 1

    def finalize(self):
        targets = {e: set() for e in ENGINES}
        for e in ENGINES:
            for wl, fn, myseq, dma, tag in self.plan[e]:
                if dma:
                    targets[e].add(myseq)
                for peng, pseq in wl:
                    targets[peng].add(pseq)
        self.inc_at = {e: targets[e] for e in ENGINES}
        self.amt = {}
        self.val = {}
        for e in ENGINES:
            tot = 0
            for wl, fn, myseq, dma, tag in self.plan[e]:
                if myseq in self.inc_at[e]:
                    a = 16 if dma else 1
                    self.amt[(e, myseq)] = a
                    tot += a
                self.val[(e, myseq)] = tot
        for e in ENGINES:
            seen = {}
            n_kept = 0
            for entry in self.plan[e]:
                wl = []
                for p, q in entry[0]:
                    v = self.val[(p, q)]
                    if seen.get(p, -1) < v:
                        seen[p] = v
                        wl.append((p, v))
                n_kept += len(wl)
                entry[0] = wl
        self.n_waits = sum(len(en[0]) for ee_ in ENGINES for en in self.plan[ee_])

    def check(self):
        idx = {e: 0 for e in ENGINES}
        counts = {e: 0 for e in ENGINES}
        total = sum(len(v) for v in self.plan.values())
        done = 0
        progress = True
        while progress:
            progress = False
            for e in ENGINES:
                while idx[e] < len(self.plan[e]):
                    wl, fn, myseq, dma, tag = self.plan[e][idx[e]]
                    if all(counts[p] >= v for p, v in wl):
                        if myseq in self.inc_at[e]:
                            counts[e] += self.amt[(e, myseq)]
                        idx[e] += 1
                        done += 1
                        progress = True
                    else:
                        break
        if done != total:
            msgs = []
            for e in ENGINES:
                if idx[e] < len(self.plan[e]):
                    wl, fn, myseq, dma, tag = self.plan[e][idx[e]]
                    blocked = [(p, v, counts[p]) for p, v in wl if counts[p] < v]
                    msgs.append(f"{e}@{idx[e]} tag={tag} blocked={blocked}")
            raise RuntimeError("DEADLOCK: " + " | ".join(msgs))
        return total

    def replay(self, block, sems):
        for e in ENGINES:
            entries = self.plan[e]
            inc_at = self.inc_at[e]
            amts = self.amt
            mysem = sems[e]

            def body(eng, entries=entries, e=e, inc_at=inc_at, mysem=mysem):
                for wl, fn, myseq, dma, tag in entries:
                    for peng, v in wl:
                        eng.wait_ge(sems[peng], v)
                    instr = fn(eng)
                    if myseq in inc_at:
                        if instr is None:
                            instr = eng.nop()
                        instr.then_inc(mysem, amts[(e, myseq)])
            getattr(block, e)(body)


def build_program():
    nc = bass.Bass("TRN2", target_bir_lowering=False)

    xT = nc.dram_tensor("xT", [D, T], F32, kind="ExternalInput").ap()
    wqkv_h = nc.dram_tensor("wqkv", [D, 3 * D], F32, kind="ExternalInput").ap()
    bqkv_h = nc.dram_tensor("bqkv", [1, 3 * D], F32, kind="ExternalInput").ap()
    wout_h = nc.dram_tensor("wout", [D, D], F32, kind="ExternalInput").ap()
    boutc_h = nc.dram_tensor("boutc", [128, 8], F32, kind="ExternalInput").ap()
    gwT_h = nc.dram_tensor("gwT", [D, EXP], F32, kind="ExternalInput").ap()
    wexpT_h = nc.dram_tensor("wexpT", [EXP, D, D], F32, kind="ExternalInput").ap()
    outT_h = nc.dram_tensor("outT", [D, T], F32, kind="ExternalOutput").ap()

    attn_sc = nc.dram_tensor("attn_sc", [D, T], BF16, kind="Internal").ap()
    x1bf_sc = nc.dram_tensor("x1bf_sc", [D, T], BF16, kind="Internal").ap()
    base_sc = nc.dram_tensor("base_sc", [D, T], F32, kind="Internal").ap()
    mask_sc = nc.dram_tensor("mask_sc", [2, EXP, T], BF16, kind="Internal").ap()

    L2 = {}

    def a2(name, off, n):
        L2[name] = (off, n)
        assert off + n <= 80960, (name, off + n)

    a2("xbf", 0, 32768)
    a2("qT", 32768, 4096)
    a2("kT", 36864, 4096)
    a2("vext", 40960, 4160)
    a2("astage", 45120, 2048)
    a2("wst", 47168, 3072)
    a2("rows2", 50240, 12288)
    a2("bqkvr", 62528, 3072)
    a2("rowt2", 65600, 2560)
    a2("consts2", 68160, 448)
    a2("pp", 68608, 4096)
    a2("xsq", 68608, 1024)
    a2("woutw", 32768, 8192)
    a2("attnc", 40960, 8192)
    a2("x1bfc", 50240, 4096)
    a2("x1loc", 54336, 4096)
    a2("ghl", 58432, 640)
    a2("maskst", 59072, 512)
    a2("maskr8", 59584, 512)
    a2("maskm8", 60096, 512)
    a2("maskst2", 60608, 512)
    a2("wexp", 0, 16384)
    a2("x1half", 16384, 16384)
    a2("h2m", 32768, 8192)
    a2("acc", 40960, 16384)
    a2("bsb", 57344, 1024)
    a2("wse", 58368, 2048)
    a2("mrA", 60416, 2048)
    a2("mrB", 65600, 2048)
    a2("mrA2", 62464, 2048)
    a2("mrB2", 68608, 2048)
    a2("qT2", 72768, 4096)
    a2("kT2", 76864, 4096)
    E2 = 80960

    L4 = {}

    def a4(name, off, n):
        L4[name] = (off, n)
        assert off + n <= 12624, (name, off + n)

    a4("rb1", 0, 4096)
    a4("xf32", 4096, 2048)
    a4("scr4", 6144, 1024)
    a4("r1cols", 7168, 64)
    a4("rowp", 7232, 3584)
    a4("smalls", 10816, 320)
    a4("ident", 11136, 128)
    a4("iorow", 11264, 128)
    a4("sctm", 11392, 32)
    a4("wcol", 11424, 8)
    a4("sctm2", 11432, 24)
    a4("scot", 11456, 512)
    a4("scglo", 11968, 512)
    a4("sc2", 12480, 144)
    a4("x1c", 0, 4096)
    a4("outc", 0, 4096)
    a4("based", 4096, 4096)
    E4 = 12624

    assert 2 * E2 + 4 * E4 <= 212500, (E2, E4, 2 * E2 + 4 * E4)

    AR2 = nc.sbuf_tensor("AR2", [128, E2], BF16).__enter__()
    AR4 = nc.sbuf_tensor("AR4", [128, E4], F32).__enter__()
    U32B = nc.sbuf_tensor("U32B", [128, 8], U32).__enter__()
    PS = nc.psum_tensor("PS", [128, 4096], F32).__enter__()

    def r2(name, p0, p1, lo, n):
        o, sz = L2[name]
        assert lo + n <= sz, (name, lo, n, sz)
        return AR2[p0:p1, o + lo: o + lo + n]

    def r4(name, p0, p1, lo, n):
        o, sz = L4[name]
        assert lo + n <= sz, (name, lo, n, sz)
        return AR4[p0:p1, o + lo: o + lo + n]

    def g2(name, lo, n):
        o = L2[name][0] + lo
        return [f"A2.{g}" for g in range(o // 512, (o + n - 1) // 512 + 1)]

    def g4(name, lo, n):
        o = L4[name][0] + lo
        return [f"A4.{g}" for g in range(o // 512, (o + n - 1) // 512 + 1)]

    def bank(i, p0=0, p1=128, lo=0, n=512):
        return PS[p0:p1, i * 512 + lo: i * 512 + lo + n]

    def gb(i):
        return [f"PS.{i}"]

    S = Sched()

    def mm(ps, gps, lhsT, glh, rhs, grh, start, stop, tag=""):
        S.emit("tensor", lambda e, p=ps, l=lhsT, r=rhs, st=start, sp=stop:
               e.matmul(p, l, r, start=st, stop=sp),
               reads=list(glh) + list(grh), writes=list(gps), tag=tag)

    def tr(ps, gps, in_, gin, ident, gid, tag=""):
        S.emit("tensor", lambda e, p=ps, i=in_, d=ident: e.transpose(p, i, d),
               reads=list(gin) + list(gid), writes=list(gps), tag=tag)

    def dve(fn, reads, writes, tag=""):
        S.emit("vector", fn, reads=reads, writes=writes, tag=tag)

    def pve(fn, reads, writes, tag=""):
        S.emit("gpsimd", fn, reads=reads, writes=writes, tag=tag)

    def act(fn, reads, writes, tag=""):
        S.emit("scalar", fn, reads=reads, writes=writes, tag=tag)

    def dma(q, out, in_, reads, writes, tag=""):
        S.emit(q, lambda e, o=out, i=in_: e.dma_start(out=o, in_=i),
               reads=reads, writes=writes, dma=True, tag=tag)

    onescol_bf = r2("consts2", 0, 128, 0, 1)
    onesrow_bf = r2("consts2", 0, 1, 1, 128)
    onesrow64_bf = r2("consts2", 64, 65, 129, 128)
    gc_ones = g2("consts2", 0, 512)
    dve(lambda e: e.memset(r2("consts2", 0, 128, 0, 1), 1.0), [], gc_ones, "c0")
    dve(lambda e: e.memset(r2("consts2", 0, 1, 1, 128), 1.0), [], gc_ones, "c1")
    dve(lambda e: e.memset(r2("consts2", 64, 65, 129, 128), 1.0), [], gc_ones, "c1b")
    dve(lambda e: e.memset(r4("r1cols", 0, 128, 32, 1), C_SHIFT),
        [], g4("r1cols", 32, 1), "c1c")
    ones1f = r4("smalls", 0, 1, 128, 128)
    onescol_f = r4("smalls", 0, 128, 256, 1)
    bexp = r4("smalls", 0, 128, 257, 1)
    beps = r4("smalls", 0, 1, 258, 1)
    bzero = r4("smalls", 0, 1, 259, 1)
    ident11 = r4("smalls", 0, 1, 260, 1)
    iocol = r4("smalls", 0, 128, 261, 1)
    negG = r4("smalls", 0, 8, 262, 1)
    ecol8 = r4("smalls", 0, 8, 261, 1)
    gsm = g4("smalls", 0, 512)
    dve(lambda e: e.memset(r4("smalls", 0, 1, 128, 128), 1.0), [], gsm, "c2")
    dve(lambda e: e.memset(r4("smalls", 0, 128, 256, 1), 1.0), [], gsm, "c3")
    dve(lambda e: e.memset(r4("smalls", 0, 128, 257, 1), C_SHIFT), [], gsm, "c4")
    dve(lambda e: e.memset(r4("smalls", 0, 1, 258, 1), 1e-5), [], gsm, "c5")
    dve(lambda e: e.memset(r4("smalls", 0, 1, 259, 1), 0.0), [], gsm, "c6")
    dve(lambda e: e.memset(r4("smalls", 0, 128, 264, 1), 0.0), [], gsm, "c6b")
    dve(lambda e: e.memset(r4("smalls", 0, 1, 260, 1), 1.0), [], gsm, "c7")
    iorow = r4("iorow", 0, 1, 0, 128)
    for i in range(128):
        dve(lambda e, i=i: e.memset(r4("iorow", 0, 1, i, 1), float(i)),
            [], g4("iorow", i, 1), f"io{i}")
    for ei in range(8):
        dve(lambda e, ei=ei: e.memset(r4("wcol", 0, 128, ei, 1), float(8 - ei)),
            [], g4("wcol", ei, 1), f"wc{ei}")
    for ei in range(32):
        dve(lambda e, ei=ei: e.memset(r4("sc2", 0, 128, 100 + ei, 1),
                                      float(8 - ei % 8)),
            [], g4("sc2", 100 + ei, 1), f"wc3_{ei}")
    tr(bank(7, 0, 128, 0, 1), gb(7), iorow, g4("iorow", 0, 128), ident11, gsm, "ioct")
    dve(lambda e: e.tensor_copy(r4("smalls", 0, 128, 261, 1), bank(7, 0, 128, 0, 1)),
        gb(7), gsm, "iocp")
    mm(bank(6, 0, 128, 0, 128), gb(6), ones1f, gsm, iorow, g4("iorow", 0, 128), True, True, "iobc")
    pve(lambda e: e.tensor_copy(r4("smalls", 0, 128, 263, 1),
                                r4("smalls", 0, 128, 261, 1)), gsm, gsm, "ioc2")
    ident128 = r4("ident", 0, 128, 0, 128)
    dve(lambda e: e.tensor_scalar(r4("ident", 0, 128, 0, 128), bank(6, 0, 128, 0, 128),
                                  r4("smalls", 0, 128, 263, 1), None, AluOpType.is_equal),
        gb(6) + gsm, g4("ident", 0, 128), "ident")

    dma("sync", r4("smalls", 0, 128, 0, 64).rearrange("p (dt e) -> p dt e", dt=8),
        gwT_h.rearrange("(dt p) e -> p dt e", p=128), ["H.gw"], gsm, "gw_in")
    dma("sync", r4("smalls", 0, 128, 64, 8), boutc_h, ["H.boutc"], gsm, "bout_in")
    dma("gpsimd", r2("bqkvr", 0, 1, 0, 3072), bqkv_h, ["H.bqkv"],
        g2("bqkvr", 0, 3072), "bqkv_in")
    for dt in range(8):
        mm(bank(5, 0, 1, 0, 8), gb(5), onescol_f, gsm,
           r4("smalls", 0, 128, dt * 8, 8), gsm, dt == 0, dt == 7, "Gsum")
    dve(lambda e: e.tensor_scalar(r4("rowp", 0, 1, 0, 8), bank(5, 0, 1, 0, 8), -1.0,
                                  None, AluOpType.mult), gb(5), g4("rowp", 0, 8), "negGr")
    tr(bank(5, 0, 8, 512, 1), gb(5), r4("rowp", 0, 1, 0, 8), g4("rowp", 0, 8),
       ident11, gsm, "negGt")
    dve(lambda e: e.tensor_copy(r4("smalls", 0, 8, 262, 1), bank(5, 0, 8, 512, 1)),
        gb(5), gsm, "negGc")

    def rowslot(i, n=512, p0=0, p1=1):
        return r4("rowp", p0, p1, i * 512, n)

    def rowslot_g(i, n=512):
        return g4("rowp", i * 512, n)

    R_M, R_E2, R_VAR, R_LN, R_R, R_A, R_B = 0, 1, 2, 3, 4, 5, 6

    def emit_phaseA_tc(tc):
        for dt in range(8):
            if dt % 4 == 0:
                half = dt // 4
                dma("sync", r4("xf32", 0, 128, 0, 2048)
                    .rearrange("p (dt t) -> p dt t", dt=4),
                    xT[half * 512:(half + 1) * 512, tc * 512:(tc + 1) * 512]
                    .rearrange("(dt p) t -> p dt t", p=128),
                    ["H.xT"], g4("xf32", 0, 2048), f"xin{tc}_{half}")
            xf = r4("xf32", 0, 128, (dt % 4) * 512, 512)
            gxf = g4("xf32", (dt % 4) * 512, 512)
            xb = r2("xbf", 0, 128, dt * 4096 + tc * 512, 512)
            gxb = g2("xbf", dt * 4096 + tc * 512, 512)
            dve(lambda e, o=xb, i=xf: e.tensor_copy(o, i), gxf, gxb, f"cast{tc}_{dt}")
            sq = r2("xsq", 0, 128, (dt % 2) * 512, 512)
            gsq = g2("xsq", (dt % 2) * 512, 512)
            dve(lambda e, o=sq, i=xb: e.tensor_tensor(o, i, i, AluOpType.mult),
                gxb, gsq, f"sqA{tc}_{dt}")
            mm(bank(0, 0, 1), gb(0), onescol_bf, gc_ones, xb, gxb,
               dt == 0, dt == 7, f"s1_{tc}_{dt}")
            mm(bank(1, 0, 1), gb(1), onescol_bf, gc_ones, sq, gsq,
               dt == 0, dt == 7, f"s2_{tc}_{dt}")
        dve(lambda e: e.tensor_scalar(rowslot(R_M), bank(0, 0, 1), 1.0 / D, None,
                                      AluOpType.mult), gb(0), rowslot_g(R_M), f"mA{tc}")
        dve(lambda e: e.tensor_scalar(rowslot(R_E2), bank(1, 0, 1), 1.0 / D, None,
                                      AluOpType.mult), gb(1), rowslot_g(R_E2), f"eA{tc}")
        dve(lambda e: e.tensor_tensor(rowslot(R_VAR), rowslot(R_M), rowslot(R_M),
                                      AluOpType.mult),
            rowslot_g(R_M), rowslot_g(R_VAR), f"m2A{tc}")
        dve(lambda e: e.tensor_tensor(rowslot(R_VAR), rowslot(R_E2), rowslot(R_VAR),
                                      AluOpType.subtract),
            rowslot_g(R_E2) + rowslot_g(R_VAR), rowslot_g(R_VAR), f"vA{tc}")
        act(lambda e: e.activation(rowslot(R_LN), rowslot(R_VAR), AF.Ln,
                                   bias=beps, scale=1.0),
            rowslot_g(R_VAR) + gsm, rowslot_g(R_LN), f"lnA{tc}")
        act(lambda e: e.activation(rowslot(R_R), rowslot(R_LN), AF.Exp,
                                   bias=bzero, scale=-0.5),
            rowslot_g(R_LN) + gsm, rowslot_g(R_R), f"rA{tc}")
        act(lambda e, tc=tc: e.activation(r2("rows2", 0, 1, 4096 + tc * 512, 512),
                                          rowslot(R_LN), AF.Exp, bias=bzero, scale=0.5),
            rowslot_g(R_LN) + gsm, g2("rows2", 4096 + tc * 512, 512), f"ivA{tc}")
        dve(lambda e, tc=tc: e.tensor_scalar(r2("rows2", 0, 1, 8192 + tc * 512, 512),
                                             rowslot(R_M), -1.0, None, AluOpType.mult),
            rowslot_g(R_M), g2("rows2", 8192 + tc * 512, 512), f"nmA{tc}")
        dve(lambda e, tc=tc: e.scalar_tensor_tensor(r2("rows2", 0, 1, tc * 512, 512),
                                                    rowslot(R_R), -1.0, rowslot(R_M),
                                                    AluOpType.mult, AluOpType.mult),
            rowslot_g(R_R) + rowslot_g(R_M), g2("rows2", tc * 512, 512), f"nrmA{tc}")
        rhi = r2("rowt2", 0, 1, 0, 512)
        rlo = r2("rowt2", 0, 1, 512, 512)
        dve(lambda e: e.tensor_copy(rhi, rowslot(R_R)), rowslot_g(R_R),
            g2("rowt2", 0, 512), f"rhiA{tc}")
        dve(lambda e: e.tensor_tensor(rlo, rowslot(R_R), rhi, AluOpType.subtract),
            rowslot_g(R_R) + g2("rowt2", 0, 512), g2("rowt2", 512, 512), f"rloA{tc}")
        mm(bank(2), gb(2), onesrow_bf, gc_ones, rhi, g2("rowt2", 0, 512),
           True, False, f"rbA{tc}")
        mm(bank(2), gb(2), onesrow_bf, gc_ones, rlo, g2("rowt2", 512, 512),
           False, True, f"rbB{tc}")
        dve(lambda e, tc=tc: e.tensor_copy(r4("rb1", 0, 128, tc * 512, 512), bank(2)),
            gb(2), g4("rb1", tc * 512, 512), f"rbC{tc}")
        for sub in range(4):
            tr(bank(3, 0, 128, sub, 1), gb(3),
               r4("rowp", 0, 1, R_R * 512 + sub * 128, 128), rowslot_g(R_R),
               ident11, gsm, f"rcT{tc}_{sub}")
        dve(lambda e, tc=tc: e.tensor_copy(r4("r1cols", 0, 128, tc * 4, 4),
                                           bank(3, 0, 128, 0, 4)),
            gb(3), g4("r1cols", tc * 4, 4), f"rcC{tc}")

    wq = r2("wst", 0, 128, 0, 1024)
    wk = r2("wst", 0, 128, 1024, 1024)
    wv = r2("wst", 0, 128, 2048, 1024)
    gwq = g2("wst", 0, 1024)
    gwk = g2("wst", 1024, 1024)
    gwv = g2("wst", 2048, 1024)
    wsq_r = r2("rowt2", 0, 1, 1024, 128)
    wsk_r = r2("rowt2", 0, 1, 1152, 128)
    wsv_r = r2("rowt2", 0, 1, 1280, 128)
    gws0 = g2("rowt2", 1024, 128)
    gws1 = g2("rowt2", 1152, 128)
    gws2 = g2("rowt2", 1280, 128)

    def proj_thunks(pn):
        th = []
        qdst = "qT" if pn % 2 == 0 else "qT2"
        kdst = "kT" if pn % 2 == 0 else "kT2"
        for (w, gw_, col0, nm) in ((wq, gwq, 128 * pn, "q"),
                                   (wk, gwk, D + 128 * pn, "k"),
                                   (wv, gwv, 2 * D + 128 * pn, "v")):
            th.append(lambda w=w, gw_=gw_, col0=col0, nm=nm: dma(
                "gpsimd", w.rearrange("p (dt o) -> p dt o", dt=8),
                wqkv_h[:, col0:col0 + 128].rearrange("(dt p2) o -> p2 dt o", p2=128),
                ["H.wqkv"], gw_, f"w{nm}i{pn}"))
        for si, (w, gw_) in enumerate(((wq, gwq), (wk, gwk), (wv, gwv))):
            for dt in range(8):
                th.append(lambda w=w, gw_=gw_, si=si, dt=dt: mm(
                    bank(7, 0, 1, si * 128, 128), gb(7), onescol_bf, gc_ones,
                    w[:, dt * 128:(dt + 1) * 128], gw_, dt == 0, dt == 7,
                    f"ws{pn}{si}"))
            th.append(lambda si=si: dve(
                lambda e, si=si: e.tensor_copy(
                    r2("rowt2", 0, 1, 1024 + si * 128, 128),
                    bank(7, 0, 1, si * 128, 128)),
                gb(7), g2("rowt2", 1024 + si * 128, 128), f"wsc{pn}{si}"))
        bq_r = r2("bqkvr", 0, 1, 128 * pn, 128)
        bk_r = r2("bqkvr", 0, 1, D + 128 * pn, 128)
        gbq = g2("bqkvr", 128 * pn, 128)
        gbk = g2("bqkvr", D + 128 * pn, 128)
        for tc in range(NT):
            for (dst, w, gw_, wsr, gwsr, br, gbr) in (
                    (qdst, wq, gwq, wsq_r, gws0, bq_r, gbq),
                    (kdst, wk, gwk, wsk_r, gws1, bk_r, gbk)):
                for dt in range(8):
                    th.append(lambda dst=dst, w=w, gw_=gw_, tc=tc, dt=dt: mm(
                        bank(7), gb(7), w[:, dt * 128:(dt + 1) * 128], gw_,
                        r2("xbf", 0, 128, dt * 4096 + tc * 512, 512),
                        g2("xbf", dt * 4096 + tc * 512, 512),
                        dt == 0, False, f"{dst}m{pn}_{tc}_{dt}"))
                th.append(lambda dst=dst, wsr=wsr, gwsr=gwsr, tc=tc: mm(
                    bank(7), gb(7), wsr, gwsr,
                    r2("rows2", 0, 1, tc * 512, 512), g2("rows2", tc * 512, 512),
                    False, False, f"{dst}c1{pn}_{tc}"))
                th.append(lambda dst=dst, br=br, gbr=gbr, tc=tc: mm(
                    bank(7), gb(7), br, gbr,
                    r2("rows2", 0, 1, 4096 + tc * 512, 512),
                    g2("rows2", 4096 + tc * 512, 512), False, True,
                    f"{dst}c2{pn}_{tc}"))
                th.append(lambda dst=dst, tc=tc: dve(
                    lambda e, dst=dst, tc=tc: e.tensor_tensor(
                        r2(dst, 0, 128, tc * 512, 512), bank(7),
                        r4("rb1", 0, 128, tc * 512, 512), AluOpType.mult),
                    gb(7) + g4("rb1", tc * 512, 512), g2(dst, tc * 512, 512),
                    f"{dst}e{pn}_{tc}"))
        return th

    th0 = proj_thunks(0)
    n0 = len(th0)
    for tc in range(NT):
        emit_phaseA_tc(tc)
        want0 = min(n0, 30 + 22 * (tc + 1))
        while n0 - len(th0) < want0:
            th0.pop(0)()
    while th0:
        th0.pop(0)()
    norm_bg = []
    for p in range(NPAIR):
        qsrc = "qT" if p % 2 == 0 else "qT2"
        ksrc = "kT" if p % 2 == 0 else "kT2"
        if p == 0:
            dve(lambda e: e.memset(
                r2("vext", 0, 128, 0, 4160)
                .rearrange("p (kt c) -> p kt c", kt=32)[:, :, 64:130:65], 1.0),
                [], g2("vext", 0, 4160), "vones")
        bv_r = r2("bqkvr", 0, 1, 2 * D + 128 * p, 128)
        gbv = g2("bqkvr", 2 * D + 128 * p, 128)

        def v_group(kt, vb, p=p, bv_r=bv_r, gbv=gbv):
            for dt in range(8):
                mm(bank(vb, 0, 128, 0, 128), gb(vb),
                   r2("xbf", 0, 128, dt * 4096 + kt * 128, 128),
                   g2("xbf", dt * 4096 + kt * 128, 128),
                   wv[:, dt * 128:(dt + 1) * 128], gwv, dt == 0, False, f"vm{p}_{kt}")
            mm(bank(vb, 0, 128, 0, 128), gb(vb),
               r2("rows2", 0, 1, 8192 + kt * 128, 128),
               g2("rows2", 8192 + kt * 128, 128), wsv_r, gws2, False, False,
               f"vc1{p}_{kt}")
            mm(bank(vb, 0, 128, 0, 128), gb(vb),
               r2("rows2", 0, 1, 4096 + kt * 128, 128),
               g2("rows2", 4096 + kt * 128, 128), bv_r, gbv, False, True, f"vc2{p}_{kt}")
            for hh in range(2):
                dve(lambda e, kt=kt, hh=hh, vb=vb: e.tensor_scalar(
                        r2("vext", 0, 128, kt * 130 + 65 * hh, 64),
                        bank(vb, 0, 128, hh * 64, 64),
                        r4("r1cols", 0, 128, kt, 1), None, AluOpType.mult),
                    gb(vb) + g4("r1cols", kt, 1), g2("vext", kt * 130 + 65 * hh, 64),
                    f"ve{p}_{kt}_{hh}")

        if p == 0:
            for kt in range(32):
                v_group(kt, 5 + kt % 2)

        bg = proj_thunks(p + 1) if p < NPAIR - 1 else []
        nbg = len(bg)
        bgi = 0
        for qc in range(NT):
            par = qc % 2

            def emit_S(kt, qc=qc, qsrc=qsrc, ksrc=ksrc):
                for hh in range(2):
                    s_b = 2 + 2 * (kt % 2) + hh
                    mm(bank(s_b), gb(s_b),
                       r2(ksrc, 64 * hh, 64 * hh + 64, kt * 128, 128),
                       g2(ksrc, kt * 128, 128),
                       r2(qsrc, 64 * hh, 64 * hh + 64, qc * 512, 512),
                       g2(qsrc, qc * 512, 512), True, True, f"S{p}{qc}_{kt}_{hh}")

            def emit_E(kt, qc=qc):
                pb = 2 + 2 * (kt % 2)
                pp = r2("pp", 0, 128, (kt % 4) * 1024, 1024)
                gpp = g2("pp", (kt % 4) * 1024, 1024)
                act(lambda e, o=pp, b=pb: e.activation(
                        o, bank(b, 0, 128, 0, 1024), AF.Exp,
                        bias=r4("r1cols", 0, 128, 32, 1), scale=0.125),
                    gb(pb) + gb(pb + 1) + g4("r1cols", 32, 1), gpp, f"E{p}{qc}_{kt}")

            def emit_A(kt, qc=qc):
                for hh in range(2):
                    pp = r2("pp", 0, 128, (kt % 4) * 1024 + 512 * hh, 512)
                    gpp = g2("pp", (kt % 4) * 1024 + 512 * hh, 512)
                    mm(bank(hh, 0, 65), gb(hh),
                       r2("vext", 0, 128, kt * 130 + 65 * hh, 65),
                       g2("vext", kt * 130 + 65 * hh, 65),
                       pp, gpp, kt == 0, kt == 31, f"A{p}{qc}_{kt}_{hh}")

            for kt in range(32):
                if qc == 0 and p > 0:
                    v_group(kt, 6 + kt % 2)
                emit_S(kt)
                if kt >= 2:
                    emit_A(kt - 2)
                emit_E(kt)
                if norm_bg and kt >= 10:
                    norm_bg.pop(0)()
                kv = min(max(kt - 4, 0), 22)
                if p > 0:
                    virt = -1 if qc == 0 else (qc - 1) * 23 + kv
                    want = 0 if virt < 0 else (nbg * (virt + 1)) // (7 * 23)
                else:
                    virt = qc * 23 + kv
                    want = (nbg * (virt + 1)) // (8 * 23)
                if 4 <= kt <= 26:
                    while bgi < want:
                        bg[bgi]()
                        bgi += 1
            emit_A(30)
            emit_A(31)
            rowsets = (
                (("rowt2", 1536), ("rowt2", 2048)),
                (("astage", 0), ("astage", 512)),
                (("astage", 1024), ("astage", 1536)),
                (("rowt2", 1536), ("rowt2", 2048)),
            )
            for hh in range(2):
                cidx = hh + 2 * par
                dve(lambda e, hh=hh: e.tensor_copy(
                        r4("scr4", 0, 65, hh * 512, 512), bank(hh, 0, 65)),
                    gb(hh), g4("scr4", hh * 512, 512), f"avc{p}{qc}_{hh}")
                cT = "scot" if hh == 0 else "scglo"
                rslot = R_A if hh == 0 else R_B
                rec = rowslot(rslot, p0=64, p1=65)
                grec = rowslot_g(rslot)
                dve(lambda e, rec=rec, hh=hh: e.reciprocal(
                        rec, r4("scr4", 64, 65, hh * 512, 512)),
                    g4("scr4", hh * 512, 512), grec, f"rc{p}{qc}_{hh}")
                (nhi, ohi), (nlo, olo) = rowsets[cidx]
                rchi = r2(nhi, 64, 65, ohi, 512)
                ghi = g2(nhi, ohi, 512)
                rclo = r2(nlo, 64, 65, olo, 512)
                glo = g2(nlo, olo, 512)
                dve(lambda e, rec=rec, rchi=rchi: e.tensor_copy(rchi, rec), grec,
                    ghi, f"rh{p}{qc}_{hh}")

                def mk1(rchi=rchi, ghi=ghi, cT=cT, p=p, qc=qc, hh=hh, par=par):
                    def t():
                        mm(bank(6, 0, 64), gb(6), onesrow64_bf[0:1, 0:64], gc_ones,
                           rchi, ghi, True, True, f"bA{p}{qc}_{hh}")
                        dve(lambda e: e.tensor_copy(r4(cT, 0, 64, 0, 512),
                                                    bank(6, 0, 64)),
                            gb(6), g4(cT, 0, 512), f"bC{p}{qc}_{hh}")
                        stg = r2("astage", 0, 64, hh * 1024 + par * 512, 512)
                        gstg = g2("astage", hh * 1024 + par * 512, 512)
                        dve(lambda e, stg=stg: e.tensor_tensor(
                                stg, r4("scr4", 0, 64, hh * 512, 512),
                                r4(cT, 0, 64, 0, 512), AluOpType.mult),
                            g4("scr4", hh * 512, 512) + g4(cT, 0, 512), gstg,
                            f"ae{p}{qc}_{hh}")
                        dma("sync",
                            attn_sc[128 * p + 64 * hh:128 * p + 64 * hh + 64,
                                    qc * 512:(qc + 1) * 512],
                            stg, gstg, [f"attn.{p}_{hh}"], f"ao{p}{qc}_{hh}")
                    return t

                norm_bg.append(mk1())
        while bgi < nbg:
            bg[bgi]()
            bgi += 1
    while norm_bg:
        norm_bg.pop(0)()

    dma("gpsimd", r2("woutw", 0, 128, 0, 8192).rearrange("p (dt o) -> p dt o", dt=8),
        wout_h.rearrange("(dt p) o -> p dt o", p=128), ["H.wout"],
        g2("woutw", 0, 8192), "wouti")
    dve(lambda e: e.memset(r2("ghl", 0, 128, 0, 640), 0.0), [], g2("ghl", 0, 640), "gh0")
    for dt in range(8):
        dve(lambda e, dt=dt: e.tensor_copy(r2("ghl", 0, 128, dt * 40, 8),
                                           r4("smalls", 0, 128, dt * 8, 8)),
            gsm + g2("ghl", 0, 640), g2("ghl", 0, 640), f"gh{dt}")
        pve(lambda e, dt=dt: e.tensor_tensor(r2("ghl", 0, 128, dt * 40 + 32, 8),
                                             r4("smalls", 0, 128, dt * 8, 8),
                                             r2("ghl", 0, 128, dt * 40, 8),
                                             AluOpType.subtract),
            gsm + g2("ghl", 0, 640), g2("ghl", 0, 640), f"gl{dt}")

    for tc in range(NT):
        cb = tc % 2
        dma("gpsimd",
            r2("attnc", 0, 128, cb * 4096, 4096).rearrange("p (dt t) -> p dt t", dt=8),
            attn_sc[:, tc * 512:(tc + 1) * 512].rearrange("(dt p) t -> p dt t", p=128),
            [f"attn.{pp_}_{hh_}" for pp_ in range(8) for hh_ in range(2)],
            g2("attnc", cb * 4096, 4096), f"ai{tc}")
        for oc in range(8):
            wb = oc % 2
            if oc % 4 == 0:
                half = oc // 4
                dma("sync", r4("xf32", 0, 128, 0, 2048)
                    .rearrange("p (dt t) -> p dt t", dt=4),
                    xT[half * 512:(half + 1) * 512, tc * 512:(tc + 1) * 512]
                    .rearrange("(dt p) t -> p dt t", p=128),
                    ["H.xT"], g4("xf32", 0, 2048), f"x2i{tc}_{half}")
            for dt in range(8):
                mm(bank(wb), gb(wb),
                   r2("woutw", 0, 128, dt * 1024 + oc * 128, 128),
                   g2("woutw", dt * 1024 + oc * 128, 128),
                   r2("attnc", 0, 128, cb * 4096 + dt * 512, 512),
                   g2("attnc", cb * 4096 + dt * 512, 512),
                   dt == 0, dt == 7, f"wo{tc}_{oc}_{dt}")
            dve(lambda e, oc=oc, wb=wb: e.scalar_tensor_tensor(
                    r4("x1c", 0, 128, oc * 512, 512), bank(wb),
                    r4("smalls", 0, 128, 64 + oc, 1),
                    r4("xf32", 0, 128, (oc % 4) * 512, 512),
                    AluOpType.add, AluOpType.add),
                gb(wb) + gsm + g4("xf32", (oc % 4) * 512, 512),
                g4("x1c", oc * 512, 512), f"x1_{tc}_{oc}")
        for oc in range(8):
            dve(lambda e, oc=oc: e.tensor_copy(r2("x1bfc", 0, 128, oc * 512, 512),
                                               r4("x1c", 0, 128, oc * 512, 512)),
                g4("x1c", oc * 512, 512), g2("x1bfc", oc * 512, 512), f"xb_{tc}_{oc}")
            dve(lambda e, oc=oc: e.tensor_tensor(r2("x1loc", 0, 128, oc * 512, 512),
                                                 r4("x1c", 0, 128, oc * 512, 512),
                                                 r2("x1bfc", 0, 128, oc * 512, 512),
                                                 AluOpType.subtract),
                g4("x1c", oc * 512, 512) + g2("x1bfc", oc * 512, 512),
                g2("x1loc", oc * 512, 512), f"xl_{tc}_{oc}")
            sq = r2("xsq", 0, 128, (oc % 2) * 512, 512)
            gsq = g2("xsq", (oc % 2) * 512, 512)
            dve(lambda e, oc=oc, o=sq: e.tensor_tensor(
                    o, r2("x1bfc", 0, 128, oc * 512, 512),
                    r2("x1bfc", 0, 128, oc * 512, 512), AluOpType.mult),
                g2("x1bfc", oc * 512, 512), gsq, f"sqC{tc}_{oc}")
            mm(bank(2, 0, 1), gb(2), onescol_bf, gc_ones,
               r2("x1bfc", 0, 128, oc * 512, 512), g2("x1bfc", oc * 512, 512),
               oc == 0, oc == 7, f"t1_{tc}_{oc}")
            mm(bank(3, 0, 1), gb(3), onescol_bf, gc_ones, sq, gsq,
               oc == 0, oc == 7, f"t2_{tc}_{oc}")
        dma("scalar", x1bf_sc[:, tc * 512:(tc + 1) * 512]
            .rearrange("(oc p) t -> p oc t", p=128),
            r2("x1bfc", 0, 128, 0, 4096).rearrange("p (oc t) -> p oc t", oc=8),
            g2("x1bfc", 0, 4096), [f"x1bf.{tc}"], f"x1o{tc}")
        dve(lambda e: e.tensor_scalar(rowslot(R_M), bank(2, 0, 1), 1.0 / D, None,
                                      AluOpType.mult), gb(2), rowslot_g(R_M), f"mC{tc}")
        dve(lambda e: e.tensor_scalar(rowslot(R_E2), bank(3, 0, 1), 1.0 / D, None,
                                      AluOpType.mult), gb(3), rowslot_g(R_E2), f"eC{tc}")
        dve(lambda e: e.tensor_tensor(rowslot(R_VAR), rowslot(R_M), rowslot(R_M),
                                      AluOpType.mult),
            rowslot_g(R_M), rowslot_g(R_VAR), f"m2C{tc}")
        dve(lambda e: e.tensor_tensor(rowslot(R_VAR), rowslot(R_E2), rowslot(R_VAR),
                                      AluOpType.subtract),
            rowslot_g(R_E2) + rowslot_g(R_VAR), rowslot_g(R_VAR), f"vC{tc}")
        act(lambda e: e.activation(rowslot(R_LN), rowslot(R_VAR), AF.Ln,
                                   bias=beps, scale=1.0),
            rowslot_g(R_VAR) + gsm, rowslot_g(R_LN), f"lnC{tc}")
        act(lambda e: e.activation(rowslot(R_R), rowslot(R_LN), AF.Exp,
                                   bias=bzero, scale=-0.5),
            rowslot_g(R_LN) + gsm, rowslot_g(R_R), f"r2C{tc}")
        dve(lambda e: e.tensor_scalar(rowslot(R_A), rowslot(R_M), -1.0, None,
                                      AluOpType.mult),
            rowslot_g(R_M), rowslot_g(R_A), f"nmC{tc}")
        dve(lambda e: e.tensor_tensor(rowslot(R_B), rowslot(R_A), rowslot(R_R),
                                      AluOpType.mult),
            rowslot_g(R_A) + rowslot_g(R_R), rowslot_g(R_B), f"nmrC{tc}")
        dve(lambda e: e.tensor_scalar(rowslot(R_VAR), rowslot(R_R), 1.0, None,
                                      AluOpType.add),
            rowslot_g(R_R), rowslot_g(R_VAR), f"rpC{tc}")
        dve(lambda e: e.tensor_tensor(rowslot(R_E2), rowslot(R_M), rowslot(R_R),
                                      AluOpType.mult),
            rowslot_g(R_M) + rowslot_g(R_R), rowslot_g(R_E2), f"mrC{tc}")
        for (slot, bnk, dsto) in ((R_VAR, 4, 0), (R_E2, 5, 512)):
            hi = r2("rowt2", 0, 1, 1536, 512)
            lo = r2("rowt2", 0, 1, 2048, 512)
            dve(lambda e, slot=slot, hi=hi: e.tensor_copy(hi, rowslot(slot)),
                rowslot_g(slot), g2("rowt2", 1536, 512), f"hi{tc}_{bnk}")
            dve(lambda e, slot=slot, hi=hi, lo=lo: e.tensor_tensor(
                    lo, rowslot(slot), hi, AluOpType.subtract),
                rowslot_g(slot) + g2("rowt2", 1536, 512), g2("rowt2", 2048, 512),
                f"lo{tc}_{bnk}")
            mm(bank(bnk), gb(bnk), onesrow_bf, gc_ones, hi, g2("rowt2", 1536, 512),
               True, False, f"bb{tc}_{bnk}a")
            mm(bank(bnk), gb(bnk), onesrow_bf, gc_ones, lo, g2("rowt2", 2048, 512),
               False, True, f"bb{tc}_{bnk}b")
            dve(lambda e, bnk=bnk, dsto=dsto: e.tensor_copy(
                    r4("scr4", 0, 128, dsto, 512), bank(bnk)),
                gb(bnk), g4("scr4", dsto, 512), f"bc{tc}_{bnk}")
        for oc in range(8):
            dve(lambda e, oc=oc: e.tensor_tensor(r4("scot", 0, 128, 0, 512),
                                                 r4("x1c", 0, 128, oc * 512, 512),
                                                 r4("scr4", 0, 128, 0, 512),
                                                 AluOpType.mult),
                g4("x1c", oc * 512, 512) + g4("scr4", 0, 512), g4("scot", 0, 512),
                f"u{tc}_{oc}")
            dve(lambda e, oc=oc: e.tensor_tensor(r4("x1c", 0, 128, oc * 512, 512),
                                                 r4("scot", 0, 128, 0, 512),
                                                 r4("scr4", 0, 128, 512, 512),
                                                 AluOpType.subtract),
                g4("scot", 0, 512) + g4("scr4", 512, 512),
                g4("x1c", oc * 512, 512), f"bs{tc}_{oc}")
        dma("scalar", base_sc[:, tc * 512:(tc + 1) * 512]
            .rearrange("(oc p) t -> p oc t", p=128),
            r4("x1c", 0, 128, 0, 4096).rearrange("p (oc t) -> p oc t", oc=8),
            g4("x1c", 0, 4096), [f"base.{tc}"], f"bo{tc}")
        for dt in range(8):
            mm(bank(6, 0, 8), gb(6), r2("ghl", 0, 128, dt * 40, 8), g2("ghl", 0, 640),
               r2("x1bfc", 0, 128, dt * 512, 512), g2("x1bfc", dt * 512, 512),
               dt == 0, dt == 7, f"gA{tc}_{dt}")
        for dt in range(8):
            mm(bank(5, 0, 8), gb(5), r2("ghl", 0, 128, dt * 40 + 32, 8),
               g2("ghl", 0, 640), r2("x1bfc", 0, 128, dt * 512, 512),
               g2("x1bfc", dt * 512, 512), dt == 0, False, f"gB{tc}_{dt}")
        for dt in range(8):
            mm(bank(5, 0, 8), gb(5), r2("ghl", 0, 128, dt * 40, 8),
               g2("ghl", 0, 640), r2("x1loc", 0, 128, dt * 512, 512),
               g2("x1loc", dt * 512, 512), False, dt == 7, f"gC{tc}_{dt}")
        dve(lambda e: e.tensor_copy(r4("scglo", 0, 8, 0, 512), bank(5, 0, 8)),
            gb(5), g4("scglo", 0, 512), f"ge{tc}")
        dve(lambda e: e.tensor_tensor(r4("scot", 0, 8, 0, 512), bank(6, 0, 8),
                                      r4("scglo", 0, 8, 0, 512), AluOpType.add),
            gb(6) + g4("scglo", 0, 512), g4("scot", 0, 512), f"gs{tc}")
        mm(bank(6, 0, 8), gb(6), ones1f[0:1, 0:8], gsm, rowslot(R_M),
           rowslot_g(R_M), True, True, f"mb{tc}")
        dve(lambda e: e.scalar_tensor_tensor(r4("scglo", 0, 8, 0, 512), bank(6, 0, 8),
                                             negG, r4("scot", 0, 8, 0, 512),
                                             AluOpType.mult, AluOpType.add),
            gb(6) + gsm + g4("scot", 0, 512), g4("scglo", 0, 512), f"sc{tc}")
        for sub in range(4):
            sb_ = 7 if sub % 2 == 0 else 5
            tr(bank(sb_, 0, 128, 0, 8), gb(sb_),
               r4("scglo", 0, 8, sub * 128, 128), g4("scglo", 0, 512),
               ident128[0:8, 0:8], g4("ident", 0, 128), f"st{tc}_{sub}")
            dve(lambda e, sub=sub, sb_=sb_: e.tensor_copy(
                    r4("sc2", 0, 128, sub * 8, 8), bank(sb_, 0, 128, 0, 8)),
                gb(sb_), g4("sc2", sub * 8, 8), f"sp{tc}_{sub}")
        pve(lambda e: e.tensor_copy(r4("sc2", 0, 128, 32, 32),
                                    r4("sc2", 0, 128, 0, 32)),
            g4("sc2", 0, 32), g4("sc2", 32, 32), f"hA{tc}")
        dve(lambda e: e.tensor_reduce(
                r4("sc2", 0, 128, 64, 4),
                r4("sc2", 0, 128, 32, 32).rearrange("p (s e) -> p s e", s=4),
                mybir.AxisListType.X, AluOpType.max),
            g4("sc2", 32, 32), g4("sc2", 64, 4), f"mx{tc}")
        for sub in range(4):
            pve(lambda e, sub=sub: e.tensor_scalar(
                    r4("sc2", 0, 128, 68 + sub * 8, 8),
                    r4("sc2", 0, 128, 32 + sub * 8, 8),
                    r4("sc2", 0, 128, 64 + sub, 1), None, AluOpType.is_equal),
                g4("sc2", 32, 36), g4("sc2", 68 + sub * 8, 8), f"eq{tc}_{sub}")
        dve(lambda e: e.tensor_tensor(r4("sc2", 0, 128, 32, 32),
                                      r4("sc2", 0, 128, 68, 32),
                                      r4("sc2", 0, 128, 100, 32), AluOpType.mult),
            g4("sc2", 68, 64), g4("sc2", 32, 32), f"wm{tc}")
        pve(lambda e: e.tensor_copy(r4("sc2", 0, 128, 0, 32),
                                    r4("sc2", 0, 128, 32, 32)),
            g4("sc2", 32, 32), g4("sc2", 0, 32), f"hB{tc}")
        dve(lambda e: e.tensor_reduce(
                r4("sc2", 0, 128, 64, 4),
                r4("sc2", 0, 128, 0, 32).rearrange("p (s e) -> p s e", s=4),
                mybir.AxisListType.X, AluOpType.max),
            g4("sc2", 0, 32), g4("sc2", 64, 4), f"m2{tc}")
        for sub in range(4):
            sb_ = 7 if sub % 2 == 0 else 5
            pve(lambda e, sub=sub: e.tensor_scalar(
                    r4("sc2", 0, 128, 68 + sub * 8, 8),
                    r4("sc2", 0, 128, sub * 8, 8),
                    r4("sc2", 0, 128, 64 + sub, 1), None, AluOpType.is_equal),
                g4("sc2", 0, 68), g4("sc2", 68 + sub * 8, 8), f"oh{tc}_{sub}")
            tr(bank(sb_, 0, 8, 8, 128), gb(sb_), r4("sc2", 0, 128, 68 + sub * 8, 8),
               g4("sc2", 68 + sub * 8, 8), ident128, g4("ident", 0, 128),
               f"ot{tc}_{sub}")
            dve(lambda e, sub=sub, sb_=sb_: e.tensor_copy(
                    r2("maskst", 0, 8, sub * 128, 128), bank(sb_, 0, 8, 8, 128)),
                gb(sb_), g2("maskst", sub * 128, 128), f"oc{tc}_{sub}")
        pve(lambda e: e.tensor_copy(r2("maskst2", 0, 8, 0, 512),
                                    r2("maskst", 0, 8, 0, 512)),
            g2("maskst", 0, 512), g2("maskst2", 0, 512), f"mh{tc}")
        mm(bank(6, 0, 8), gb(6), ones1f[0:1, 0:8], gsm, rowslot(R_R),
           rowslot_g(R_R), True, True, f"rb8{tc}")
        dve(lambda e: e.tensor_tensor(r2("maskr8", 0, 8, 0, 512),
                                      r2("maskst2", 0, 8, 0, 512), bank(6, 0, 8),
                                      AluOpType.mult),
            g2("maskst2", 0, 512) + gb(6), g2("maskr8", 0, 512), f"mr{tc}")
        mm(bank(6, 0, 8), gb(6), ones1f[0:1, 0:8], gsm, rowslot(R_B),
           rowslot_g(R_B), True, True, f"nb8{tc}")
        dve(lambda e: e.tensor_tensor(r2("maskm8", 0, 8, 0, 512),
                                      r2("maskst2", 0, 8, 0, 512), bank(6, 0, 8),
                                      AluOpType.mult),
            g2("maskst2", 0, 512) + gb(6), g2("maskm8", 0, 512), f"mm{tc}")
        dma("scalar", mask_sc[0, :, tc * 512:(tc + 1) * 512],
            r2("maskr8", 0, 8, 0, 512), g2("maskr8", 0, 512), [f"mskA.{tc}"], f"mo{tc}")
        dma("scalar", mask_sc[1, :, tc * 512:(tc + 1) * 512],
            r2("maskm8", 0, 8, 0, 512), g2("maskm8", 0, 512), [f"mskB.{tc}"], f"m2o{tc}")

    for sweep in range(2):
        th0 = sweep * 2048
        dma("sync", r2("x1half", 0, 128, 0, 16384)
            .rearrange("p (ht t) -> p ht t", ht=8),
            x1bf_sc[:, th0:th0 + 2048].rearrange("(ht p) t -> p ht t", p=128),
            [f"x1bf.{tc}" for tc in range(sweep * 4, sweep * 4 + 4)],
            g2("x1half", 0, 16384), f"xhi{sweep}")
        for ee in range(EXP):
            eb = ee % 2
            dma("gpsimd", r2("wexp", 0, 128, eb * 8192, 8192)
                .rearrange("p (ht f) -> p ht f", ht=8),
                wexpT_h[ee].rearrange("(ht p) f -> p ht f", p=128),
                ["H.wexpT"], g2("wexp", eb * 8192, 8192), f"we{sweep}_{ee}")
            for half in range(2):
                for ht in range(8):
                    mm(bank(6, 0, 1), gb(6), onescol_bf, gc_ones,
                       r2("wexp", 0, 128, eb * 8192 + ht * 1024 + half * 512, 512),
                       g2("wexp", eb * 8192 + ht * 1024 + half * 512, 512),
                       ht == 0, ht == 7, f"wsd{sweep}_{ee}_{half}")
                dve(lambda e, eb=eb, half=half: e.tensor_copy(
                        r2("wse", 0, 1, eb * 1024 + half * 512, 512), bank(6, 0, 1)),
                    gb(6), g2("wse", eb * 1024 + half * 512, 512),
                    f"wsv{sweep}_{ee}_{half}")
            mA = "mrA" if eb == 0 else "mrA2"
            mB = "mrB" if eb == 0 else "mrB2"
            dma("sync", r2(mA, 0, 1, 0, 2048),
                mask_sc[0, ee, th0:th0 + 2048][None, :],
                [f"mskA.{tc}" for tc in range(sweep * 4, sweep * 4 + 4)],
                g2(mA, 0, 2048), f"mai{sweep}_{ee}")
            dma("sync", r2(mB, 0, 1, 0, 2048),
                mask_sc[1, ee, th0:th0 + 2048][None, :],
                [f"mskB.{tc}" for tc in range(sweep * 4, sweep * 4 + 4)],
                g2(mB, 0, 2048), f"mbi{sweep}_{ee}")
            for tci in range(4):
                hb = tci % 2
                mm(bank(7), gb(7), onesrow_bf, gc_ones,
                   r2(mA, 0, 1, tci * 512, 512), g2(mA, tci * 512, 512),
                   True, True, f"Bb{sweep}_{ee}_{tci}")
                dve(lambda e, hb=hb: e.tensor_copy(r2("bsb", 0, 128, hb * 512, 512),
                                                   bank(7)),
                    gb(7), g2("bsb", hb * 512, 512), f"Bc{sweep}_{ee}_{tci}")
                for ht in range(8):
                    dve(lambda e, hb=hb, ht=ht, tci=tci: e.tensor_tensor(
                            r2("h2m", 0, 128, hb * 4096 + ht * 512, 512),
                            r2("x1half", 0, 128, ht * 2048 + tci * 512, 512),
                            r2("bsb", 0, 128, hb * 512, 512), AluOpType.mult),
                        g2("x1half", ht * 2048 + tci * 512, 512)
                        + g2("bsb", hb * 512, 512),
                        g2("h2m", hb * 4096 + ht * 512, 512), f"hm{sweep}_{ee}_{tci}_{ht}")
                for fc in range(8):
                    bnk = fc % 6
                    for ht in range(8):
                        mm(bank(bnk), gb(bnk),
                           r2("wexp", 0, 128, eb * 8192 + ht * 1024 + fc * 128, 128),
                           g2("wexp", eb * 8192 + ht * 1024 + fc * 128, 128),
                           r2("h2m", 0, 128, hb * 4096 + ht * 512, 512),
                           g2("h2m", hb * 4096 + ht * 512, 512),
                           ht == 0, False, f"ym{sweep}_{ee}_{tci}_{fc}_{ht}")
                    mm(bank(bnk), gb(bnk),
                       r2("wse", 0, 1, eb * 1024 + fc * 128, 128),
                       g2("wse", eb * 1024 + fc * 128, 128),
                       r2(mB, 0, 1, tci * 512, 512), g2(mB, tci * 512, 512),
                       False, True, f"yc{sweep}_{ee}_{tci}_{fc}")
                    if ee == 0:
                        dve(lambda e, bnk=bnk, fc=fc, tci=tci: e.tensor_copy(
                                r2("acc", 0, 128, fc * 2048 + tci * 512, 512),
                                bank(bnk)),
                            gb(bnk), g2("acc", fc * 2048 + tci * 512, 512),
                            f"ac{sweep}_{ee}_{tci}_{fc}")
                    else:
                        dve(lambda e, bnk=bnk, fc=fc, tci=tci: e.tensor_tensor(
                                r2("acc", 0, 128, fc * 2048 + tci * 512, 512),
                                bank(bnk),
                                r2("acc", 0, 128, fc * 2048 + tci * 512, 512),
                                AluOpType.add),
                            gb(bnk) + g2("acc", fc * 2048 + tci * 512, 512),
                            g2("acc", fc * 2048 + tci * 512, 512),
                            f"ac{sweep}_{ee}_{tci}_{fc}")
        for tci in range(4):
            tc = sweep * 4 + tci
            dma("sync", r4("based", 0, 128, 0, 4096)
                .rearrange("p (oc t) -> p oc t", oc=8),
                base_sc[:, tc * 512:(tc + 1) * 512]
                .rearrange("(oc p) t -> p oc t", p=128),
                [f"base.{tc}"], g4("based", 0, 4096), f"bi{tc}")
            for fc in range(8):
                dve(lambda e, fc=fc, tci=tci: e.tensor_tensor(
                        r4("outc", 0, 128, fc * 512, 512),
                        r2("acc", 0, 128, fc * 2048 + tci * 512, 512),
                        r4("based", 0, 128, fc * 512, 512), AluOpType.add),
                    g2("acc", fc * 2048 + tci * 512, 512) + g4("based", fc * 512, 512),
                    g4("outc", fc * 512, 512), f"oe{tc}_{fc}")
            dma("scalar", outT_h[:, tc * 512:(tc + 1) * 512]
                .rearrange("(oc p) t -> p oc t", p=128),
                r4("outc", 0, 128, 0, 4096).rearrange("p (oc t) -> p oc t", oc=8),
                g4("outc", 0, 4096), [f"out.{tc}"], f"ot{tc}")

    S.emit("scalar", lambda e: e.nop(),
           reads=[f"out.{tc}" for tc in range(NT)], tag="final")
    S.finalize()
    return nc, S


_CACHE = {}


def _get_nc():
    if "nc" not in _CACHE:
        nc, S = build_program()
        S.check()
        import contextlib
        stack = contextlib.ExitStack()
        sems = {e: stack.enter_context(nc.semaphore(f"sem_{e}")) for e in ENGINES}
        with nc.Block() as block:
            S.replay(block, sems)
        stack.close()
        _CACHE["nc"] = nc
    return _CACHE["nc"]


def kernel(L0_attn_data, L0_attn_W_qkv, L0_attn_b_qkv, L0_attn_W_out, L0_attn_b_out,
           L0_ffn_gate_w, L0_ffn_expert_w):
    x = np.asarray(L0_attn_data, dtype=np.float32)
    wqkv = np.ascontiguousarray(np.asarray(L0_attn_W_qkv, dtype=np.float32))
    bqkv = np.ascontiguousarray(np.asarray(L0_attn_b_qkv, dtype=np.float32)
                                .reshape(1, 3 * D))
    wout = np.ascontiguousarray(np.asarray(L0_attn_W_out, dtype=np.float32))
    boutc = np.ascontiguousarray(np.asarray(L0_attn_b_out, dtype=np.float32)
                                 .reshape(8, 128).T)
    gwT = np.ascontiguousarray(np.asarray(L0_ffn_gate_w, dtype=np.float32).T)
    wexpT = np.ascontiguousarray(np.asarray(L0_ffn_expert_w, dtype=np.float32)
                                 .transpose(0, 2, 1))
    nc = _get_nc()
    in_maps = []
    for b in range(8):
        in_maps.append({
            "xT": np.ascontiguousarray(x[b].T),
            "wqkv": wqkv, "bqkv": bqkv, "wout": wout, "boutc": boutc,
            "gwT": gwT, "wexpT": wexpT,
        })
    res = run_bass_kernel_spmd(nc, in_maps, core_ids=list(range(8)))
    return np.stack([np.ascontiguousarray(res.results[b]["outT"].T) for b in range(8)])


if __name__ == "__main__":
    nc, S = build_program()
    n = S.check()
    print("plan ok:", n, "instrs;", {e: len(S.plan[e]) for e in ENGINES},
          "waits:", S.n_waits, "incs:", sum(len(S.inc_at[e]) for e in ENGINES))
